# version 07
# baseline (speedup 1.0000x reference)
import numpy as np
import ml_dtypes
from contextlib import ExitStack

import concourse.bass as bass
import concourse.mybir as mybir
from concourse.bass_utils import run_bass_kernel_spmd

F32 = mybir.dt.float32
BF16 = mybir.dt.bfloat16
OP = mybir.AluOpType
AF = mybir.ActivationFunctionType
AX = mybir.AxisListType

NCORES = 8
BB, HH, WW = 4, 8192, 2048
NCHUNK = 128
NTILE = 8
CHT = 16
NM = 16
TCOL = 512
DFF = 5504
NKG = 43
KC_GU = 17
KC_D = 44
GUON = 2176
DON = 5632
SCALE_ATT = float(8.0 ** -0.5)
DBG_HOOK = []
DBG_SKIP = set()
AANW = NM * 224
AACW = 224
ADNW = NKG * 72
ADNWP = 3104
GU_STRIDE = 2 * KC_GU * 128
D_STRIDE = KC_D * 128
WBUF_W = 2 * D_STRIDE


def _ap(t, offset, dims):
    return bass.AP(t.tensor if hasattr(t, "tensor") else t, offset, dims)



def _stage_weights(I):
    f = np.float32
    W_gate = np.asarray(I["L0_ffn_W_gate"], f)
    W_up = np.asarray(I["L0_ffn_W_up"], f)
    W_down = np.asarray(I["L0_ffn_W_down"], f)
    sB_gate = np.asarray(I["L0_ffn_sB_gate"], f)
    sB_up = np.asarray(I["L0_ffn_sB_up"], f)
    sB_down = np.asarray(I["L0_ffn_sB_down"], f)
    B_gate = np.asarray(I["L0_ffn_B_gate"], f).reshape(64, DFF)
    B_up = np.asarray(I["L0_ffn_B_up"], f).reshape(64, DFF)
    B_down = np.asarray(I["L0_ffn_B_down"], f).reshape(64, WW)
    A_gate = np.asarray(I["L0_ffn_A_gate"], f).transpose(1, 0, 2).reshape(WW, 64)
    A_up = np.asarray(I["L0_ffn_A_up"], f).transpose(1, 0, 2).reshape(WW, 64)
    A_down = np.asarray(I["L0_ffn_A_down"], f).transpose(1, 0, 2).reshape(DFF, 64)
    sA_gate = np.asarray(I["L0_ffn_sA_gate"], f)
    sA_up = np.asarray(I["L0_ffn_sA_up"], f)
    sA_down = np.asarray(I["L0_ffn_sA_down"], f)
    W_router = np.asarray(I["L0_ffn_W_router"], f)

    zg = np.zeros((GUON - 2120, DFF), f)
    Wg_aug = np.concatenate([W_gate, B_gate, sB_gate, zg], 0)
    Wu_aug = np.concatenate([W_up, B_up, sB_up, zg], 0)
    zd = np.zeros((DON - 5576, WW), f)
    Wd_aug = np.concatenate([W_down, B_down, sB_down, zd], 0)

    def slabify(Waug):
        return Waug.reshape(KC_GU, 128, NKG, 128).transpose(2, 1, 0, 3)
    wGU = np.stack([slabify(Wg_aug), slabify(Wu_aug)], 2)
    wGU = np.ascontiguousarray(wGU, f)
    wD = np.ascontiguousarray(
        Wd_aug.reshape(KC_D, 128, NM, 128).transpose(2, 1, 0, 3), f)

    A_full = np.zeros((WW, AACW), f)
    A_full[:, 0:64] = A_gate
    A_full[:, 64:128] = A_up
    A_full[:, 128:136] = sA_gate
    A_full[:, 160:168] = sA_up
    A_full[:, 192:200] = W_router
    aAll = np.ascontiguousarray(
        A_full.reshape(NM, 128, AACW).transpose(1, 0, 2).reshape(128, AANW), f)
    A_dn_full = np.concatenate([A_down, sA_down], 1)
    aDn = A_dn_full.reshape(NKG, 128, 72).transpose(1, 0, 2).reshape(128, ADNW)
    aDn = np.concatenate([aDn, np.zeros((128, ADNWP - ADNW), f)], 1)
    aDn = np.ascontiguousarray(aDn, f)

    qkv_w = np.asarray(I["L0_attn_qkv_w"], f)
    proj_w = np.asarray(I["L0_attn_proj_w"], f)
    col3_lo = np.r_[0:8, 16:24, 32:40]
    col3_hi = np.r_[8:16, 24:32, 40:48]
    wbd1 = np.zeros((128, 192), f)
    wbd2 = np.zeros((128, 192), f)
    for s in range(8):
        wbd1[16 * s:16 * s + 16, 24 * s:24 * s + 24] = qkv_w[:, col3_lo]
        wbd2[16 * s:16 * s + 16, 24 * s:24 * s + 24] = qkv_w[:, col3_hi]
    wpL = np.zeros((64, 128), f)
    wpH = np.zeros((64, 128), f)
    for s in range(8):
        wpL[8 * s:8 * s + 8, 16 * s:16 * s + 16] = proj_w[0:8, :]
        wpH[8 * s:8 * s + 8, 16 * s:16 * s + 16] = proj_w[8:16, :]

    cw0 = np.asarray(I["L0_attn_conv_w0"], f)
    cb0 = np.asarray(I["L0_attn_conv_b0"], f)
    cw1 = np.asarray(I["L0_attn_conv_w1"], f)
    cb1 = np.asarray(I["L0_attn_conv_b1"], f)
    taps = np.zeros((8, 64), f)
    for s in range(8):
        for c in range(8):
            for ti, tau in enumerate((-1, 0, 1)):
                taps[ti, 8 * s + c] = cw0[c, 0, 1, tau + 1]
                taps[3 + ti, 8 * s + c] = cw1[c, 0, tau + 1, 1]
            taps[6, 8 * s + c] = cb0[c]
            taps[7, 8 * s + c] = cb1[c]

    ln_w = np.asarray(I["L0_attn_ln_w"], f)
    ln_b = np.asarray(I["L0_attn_ln_b"], f)
    proj_b = np.asarray(I["L0_attn_proj_b"], f)
    rows = np.zeros((3, TCOL), f)
    for chi in range(16):
        for j in range(2):
            for c in range(16):
                col = 32 * chi + 16 * j + c
                rows[0, col] = ln_w[c]
                rows[1, col] = ln_b[c]
                rows[2, col] = proj_b[c]
    ebias = np.asarray(I["L0_ffn_expert_bias"], f).reshape(1, 8)
    return dict(wGU=wGU, wD=wD, aAll=aAll, aDn=aDn, wbd1=wbd1, wbd2=wbd2,
                wpL=wpL, wpH=wpH, taps=taps, rows=rows, ebias=ebias)


def _stage_x(x, core):
    f = np.float32
    u0 = core * NCHUNK
    b = u0 // 256
    k0 = u0 % 256
    xc = np.asarray(x[b, 32 * k0: 32 * k0 + 32 * NCHUNK, :], f)
    flat = xc.reshape(NCHUNK, 65536)
    xD = np.ascontiguousarray(
        flat.reshape(NCHUNK, 512, 8, 16).transpose(0, 2, 3, 1).reshape(NCHUNK, 128, 512))
    xP2 = np.ascontiguousarray(
        flat.reshape(NCHUNK, 8, 512, 16).transpose(0, 1, 3, 2).reshape(NCHUNK, 128, 512))
    v = flat.reshape(NTILE, CHT, 2, NM, 128, 16)
    xTok = np.ascontiguousarray(
        v.transpose(0, 3, 4, 1, 2, 5).reshape(NTILE, NM, 128, TCOL))
    t_, chi_, j_, c_ = np.meshgrid(np.arange(NTILE), np.arange(CHT),
                                   np.arange(2), np.arange(16), indexing="ij")
    hidx = 512 * c_ + 2 * (k0 + 16 * t_ + chi_) + j_
    xr = np.asarray(x[b], f)[hidx.reshape(-1), :]
    xr = xr.reshape(NTILE, CHT, 2, 16, NM, 128)
    xRes = np.ascontiguousarray(xr.transpose(0, 4, 5, 1, 2, 3).reshape(NTILE, NM, 128, TCOL))
    return dict(xD=xD, xP2=xP2, xTok=xTok, xRes=xRes)


def _gather_out(outs):
    final = np.empty((BB, HH, WW), np.float32)
    for core in range(NCORES):
        u0 = core * NCHUNK
        b = u0 // 256
        k0 = u0 % 256
        o = np.asarray(outs[core], np.float32).reshape(NTILE, NM, 128, CHT, 2, 16)
        o = o.transpose(0, 3, 4, 5, 1, 2).reshape(4096, 2048)
        t_, chi_, j_, c_ = np.meshgrid(np.arange(NTILE), np.arange(CHT),
                                       np.arange(2), np.arange(16), indexing="ij")
        hidx = (512 * c_ + 2 * (k0 + 16 * t_ + chi_) + j_).reshape(-1)
        final[b, hidx, :] = o
    return final



ENGS = ("sync", "tensor", "vector", "scalar", "gpsimd")


class Prog:
    def __init__(self):
        self.ops = {e: [] for e in ENGS}

    def on(self, eng, fn):
        self.ops[eng].append(fn)


class Sem:
    def __init__(self, ctx, nc, name):
        self.h = ctx.enter_context(nc.semaphore(name))
        self.n = 0


class K:
    def __init__(self, nc, ctx, prog):
        self.nc, self.ctx, self.p = nc, ctx, prog
        self.sem = {}
        for name in ("ld", "ld2", "lat", "ldw", "ldu", "lda", "ten", "vec", "sca",
                     "gp", "dsync", "dsca", "dsc2"):
            self.sem[name] = Sem(ctx, nc, "s_" + name)

    def dma(self, qeng, out, in_, sem_name):
        s = self.sem[sem_name]
        s.n += 16
        self.p.on(qeng, lambda e, o=out, i=in_, h=s.h: e.dma_start(out=o, in_=i).then_inc(h, 16))

    def wait(self, eng, sem_name, val=None):
        s = self.sem[sem_name]
        v = s.n if val is None else val
        if v > 0:
            self.p.on(eng, lambda e, h=s.h, vv=v: e.wait_ge(h, vv))

    def spacer(self):
        pass

    def _mk(self, eng, sem_name):
        dr = eng in ("vector", "scalar")
        def em(fn, inc=False):
            if inc:
                s = self.sem[sem_name]
                s.n += 1
                self.p.on(eng, lambda e, h=s.h, f=fn: f(e).then_inc(h, 1))
            else:
                self.p.on(eng, fn)
            if dr:
                self.p.on(eng, lambda e: e.drain())
        return em

    def barrier(self):
        for e in ENGS:
            for nm in ("ten", "vec", "sca", "gp", "ld", "ld2", "lat", "ldw",
                       "ldu", "lda", "dsync", "dsca", "dsc2"):
                self.wait(e, nm)


class IL:

    def __init__(self, thunks, every):
        self.thunks = list(thunks)
        self.every = every
        self.t = 0

    def tick(self):
        self.t += 1
        if self.thunks and self.t % self.every == 0:
            self.thunks.pop(0)()

    def flush(self):
        while self.thunks:
            self.thunks.pop(0)()



def build_nc(dev=None):
    dev = dev or {}
    dv_tiles = dev.get("tiles", NTILE)
    dv_phases = dev.get("phases", ("attn", "z", "router", "b", "c"))
    dv_expose = dev.get("expose", ())
    dv_cht = dev.get("cht", CHT)
    nc = bass.Bass("TRN2", target_bir_lowering=False)
    ctx = ExitStack()
    prog = Prog()

    d_xD = nc.dram_tensor("xD", [NCHUNK, 128, 512], F32, kind="ExternalInput").ap()
    d_xP2 = nc.dram_tensor("xP2", [NCHUNK, 128, 512], F32, kind="ExternalInput").ap()
    d_xTok = nc.dram_tensor("xTok", [NTILE, NM, 128, TCOL], F32, kind="ExternalInput").ap()
    d_xRes = nc.dram_tensor("xRes", [NTILE, NM, 128, TCOL], F32, kind="ExternalInput").ap()
    d_wGU = nc.dram_tensor("wGU", [NKG, 128, 2, KC_GU, 128], F32, kind="ExternalInput").ap()
    d_wD = nc.dram_tensor("wD", [NM, 128, KC_D, 128], F32, kind="ExternalInput").ap()
    d_aAll = nc.dram_tensor("aAll", [128, AANW], F32, kind="ExternalInput").ap()
    d_aDn = nc.dram_tensor("aDn", [128, ADNWP], F32, kind="ExternalInput").ap()
    d_wbd1 = nc.dram_tensor("wbd1", [128, 192], F32, kind="ExternalInput").ap()
    d_wbd2 = nc.dram_tensor("wbd2", [128, 192], F32, kind="ExternalInput").ap()
    d_wpL = nc.dram_tensor("wpL", [64, 128], F32, kind="ExternalInput").ap()
    d_wpH = nc.dram_tensor("wpH", [64, 128], F32, kind="ExternalInput").ap()
    d_taps = nc.dram_tensor("taps", [8, 64], F32, kind="ExternalInput").ap()
    d_rows = nc.dram_tensor("rows", [3, TCOL], F32, kind="ExternalInput").ap()
    d_ebias = nc.dram_tensor("ebias", [1, 8], F32, kind="ExternalInput").ap()
    d_out = nc.dram_tensor("out", [NTILE, NM, 128, TCOL], F32, kind="ExternalOutput").ap()

    d_wGUh = nc.dram_tensor("wGUs_hi", [NKG, 128, 2, KC_GU, 128], BF16).ap()
    d_wGUl = nc.dram_tensor("wGUs_lo", [NKG, 128, 2, KC_GU, 128], BF16).ap()
    d_wDh = nc.dram_tensor("wDs_hi", [NM, 128, KC_D, 128], BF16).ap()
    d_wDl = nc.dram_tensor("wDs_lo", [NM, 128, KC_D, 128], BF16).ap()
    d_aAllh = nc.dram_tensor("aAll_hi", [128, AANW], BF16).ap()
    d_aAlll = nc.dram_tensor("aAll_lo", [128, AANW], BF16).ap()
    d_aDnh = nc.dram_tensor("aDn_hi", [128, ADNWP], BF16).ap()
    d_aDnl = nc.dram_tensor("aDn_lo", [128, ADNWP], BF16).ap()
    delkind = dict(kind="ExternalOutput") if "del" in dv_expose else {}
    d_del1 = nc.dram_tensor("del1", [2, NM, 128, TCOL], F32, **delkind).ap()
    d_del2 = nc.dram_tensor("del2", [2, NM, 128, TCOL], F32, **delkind).ap()
    d_dbgz = None
    if "z" in dv_expose:
        d_dbgz = nc.dram_tensor("dbgz", [2, NM, 128, TCOL], F32, kind="ExternalOutput").ap()
    d_dbgh = None
    if "h" in dv_expose:
        d_dbgh = nc.dram_tensor("dbgh", [4, 128, TCOL], F32, kind="ExternalOutput").ap()
    d_dbgw = None
    if "attnwork" in dv_expose:
        d_dbgw = nc.dram_tensor("dbgw", [6, 128, 768], F32, kind="ExternalOutput").ap()

    e = ctx.enter_context
    sb = {}

    def S(name, shape, dtype=F32):
        sb[name] = e(nc.sbuf_tensor(name, list(shape), dtype))
        return sb[name]

    ident = S("ident", (128, 128))
    onesrow = S("onesrow", (1, 128))
    onescol = S("onescol", (128, 1), BF16)
    tapT = S("tapT", (128, 8 * 64))
    gamT = S("gamT", (128, TCOL))
    betT = S("betT", (128, TCOL))
    biasT = S("biasT", (128, TCOL))
    ebT = S("ebT", (128, 8))
    wbd1S = S("wbd1S", (128, 192))
    wbd2S = S("wbd2S", (128, 192))
    wpLS = S("wpLS", (64, 128))
    wpHS = S("wpHS", (64, 128))

    zh = S("zh", (128, NM * TCOL), BF16)
    zl = S("zl", (128, NM * TCOL), BF16)
    hh = S("hh", (128, NKG * TCOL), BF16)
    hl = S("hl", (128, NKG * TCOL), BF16)
    ug_h = S("ug_h", (128, TCOL), BF16)
    ug_l = S("ug_l", (128, TCOL), BF16)
    uu_h = S("uu_h", (128, TCOL), BF16)
    uu_l = S("uu_l", (128, TCOL), BF16)
    ud_h = S("ud_h", (128, TCOL), BF16)
    ud_l = S("ud_l", (128, TCOL), BF16)
    wexp = S("wexp", (64, TCOL))

    wbuf = S("wbuf", (128, WBUF_W), BF16)
    aABuf = S("aABuf", (128, 2 * AACW), BF16)
    aDBuf = S("aDBuf", (128, 4 * 72), BF16)

    sbD = S("sbD", (128, 512))
    sbP2 = S("sbP2", (128, 512))
    sbQ1 = S("sbQ1", (128, 768))
    sbQ2 = S("sbQ2", (128, 768))
    sbS = S("sbS", (128, 256))
    sbPr = S("sbPr", (128, 256))
    sbM = S("sbM", (128, 32))
    sbZr = S("sbZr", (128, 32))
    sbAV = S("sbAV", (128, 256))
    sbL = S("sbL", (128, 256))
    sbT1 = S("sbT1", (128, 256))
    sbOT = S("sbOT", (64, 512))
    sbAtt = S("sbAtt", (128, 512))

    sbDel1 = S("sbDel1", (128, TCOL))
    sbDel2 = S("sbDel2", (128, TCOL))
    sbXT = S("sbXT", (128, TCOL))
    sbXR = S("sbXR", (128, TCOL))
    sbA = S("sbA", (128, TCOL))
    sbSq = S("sbSq", (128, TCOL))
    sbMu = S("sbMu", (128, 32))
    sbVar = S("sbVar", (128, 32))
    sbRs = S("sbRs", (128, 32))
    sbT2 = S("sbT2", (128, 32))
    sbZt = S("sbZt", (128, TCOL))
    sbBh = S("sbBh", (128, TCOL), BF16)
    sbBl = S("sbBl", (128, TCOL), BF16)
    sbSil = S("sbSil", (128, TCOL))
    sbY = S("sbY", (128, TCOL))
    sbRow = S("sbRow", (4, TCOL))
    sbSc = S("sbSc", (128, 32))
    sbSc2 = S("sbSc2", (128, 32))
    sbSc3 = S("sbSc3", (128, 32))
    sbSc4 = S("sbSc4", (128, 8))
    sbWt = S("sbWt", (128, 64))
    sbFm = S("sbFm", (8, TCOL))

    sbBh2 = S("sbBh2", (128, TCOL), BF16)
    sbBl2 = S("sbBl2", (128, TCOL), BF16)

    ps = {}

    def P(name):
        ps[name] = e(nc.psum_tensor(name, [128, 512], F32))
        return ps[name]
    pbG0, pbG1 = P("pbG0"), P("pbG1")
    pbU0, pbU1 = P("pbU0"), P("pbU1")
    pbA0, pbA1 = P("pbA0"), P("pbA1")
    pbB0, pbB1 = P("pbB0"), P("pbB1")

    fence_v = S("fence_v", (1, 2))
    fence_s = S("fence_s", (1, 2))
    blk = e(nc.Block())
    k = K(nc, ctx, prog)
    k.fence_v = fence_v
    k.fence_s = fence_s
    T = k._mk("tensor", "ten")
    V = k._mk("vector", "vec")
    C = k._mk("scalar", "sca")
    G = k._mk("gpsimd", "gp")

    def col(t, m, w=TCOL):
        return t[:, m * w:(m + 1) * w]

    def split(src_f32, dst_h, dst_l):
        V(lambda e, s=src_f32, d=dst_h: e.tensor_copy(d, s))
        k.spacer()
        V(lambda e, s=src_f32, dh=dst_h, dl=dst_l: e.scalar_tensor_tensor(
            dl, s, 1.0, dh, OP.mult, OP.subtract), inc=True)

    k.dma("sync", wbd1S[:, :], d_wbd1[:, :], "ld")
    k.dma("sync", wbd2S[:, :], d_wbd2[:, :], "ld")
    k.dma("sync", wpLS[:, :], d_wpL[:, :], "ld")
    k.dma("sync", wpHS[:, :], d_wpH[:, :], "ld")

    V(lambda e: e.memset(ident[:], 1.0))
    V(lambda e: e.memset(onesrow[:], 1.0))
    V(lambda e: e.memset(onescol[:], 1.0))
    for nm_ in ("ug_h", "ug_l", "uu_h", "uu_l", "ud_h", "ud_l"):
        V(lambda e, t=sb[nm_]: e.memset(t[:], 0.0))
    V(lambda e: e.memset(sb["fence_v"][:], 0.0))
    V(lambda e: e.memset(sb["fence_s"][:], 0.0))
    V(lambda e: e.memset(sbT1[:], 0.0), inc=True)
    k.wait("gpsimd", "vec")
    G(lambda e: e.affine_select(ident[:], ident[:], [[-1, 128]], OP.is_equal,
                                0.0, base=0, channel_multiplier=1), inc=True)
    srow = sbFm
    def bcast_row(d_src_row, width, dst_tile, dst_w):
        k.wait("sync", "ten")
        k.dma("sync", srow[0:1, 0:width], d_src_row, "ld")
        k.wait("tensor", "ld")
        k.wait("tensor", "vec")
        k.wait("tensor", "gp")
        k.wait("tensor", "sca")
        T(lambda e, w=width: e.matmul(pbB0[:, 0:w], onesrow[:], srow.ap()[0:1, 0:w],
                                      start=True, stop=True), inc=True)
        k.wait("scalar", "ten")
        C(lambda e, d=dst_tile, w=dst_w: e.copy(d, pbB0[:, 0:w]), inc=True)

    bcast_row(d_rows[0:1, :], TCOL, gamT[:, :], TCOL)
    bcast_row(d_rows[1:2, :], TCOL, betT[:, :], TCOL)
    bcast_row(d_rows[2:3, :], TCOL, biasT[:, :], TCOL)
    bcast_row(d_ebias[0:1, :], 8, ebT[:, :], 8)
    for ti in range(8):
        bcast_row(d_taps[ti:ti + 1, :], 64, tapT[:, 64 * ti:64 * ti + 64], 64)
    k.barrier()

    pp_state = {"vc": [0, 0], "dc": [0, 0], "lc": [0, 0]}

    def prepass(total, d_src, d_hi, d_lo, il=None):
        W = 512
        piece = 128 * W
        n = total // piece
        rem = total - n * piece
        assert rem % W == 0
        bufs = [(sbA, sbBh, sbBl), (sbSq, sbBh2, sbBl2)]
        ldsem = ("ld", "ld2")
        stsem = ("dsca", "dsc2")
        st = pp_state
        for i in range(n + (1 if rem else 0)):
            if il is not None:
                il.tick()
            lo = i * piece
            rows = 128 if i < n else rem // W
            p = i % 2
            bA, bh, bl = bufs[p]
            k.wait("sync", "vec", st["vc"][p])
            k.wait("sync", stsem[p], st["dc"][p])
            k.dma("sync", bA[0:rows, :], _ap(d_src, lo, [[W, rows], [1, W]]), ldsem[p])
            st["lc"][p] = k.sem[ldsem[p]].n
            k.wait("vector", ldsem[p], st["lc"][p])
            V(lambda e, bA=bA, bh=bh, r=rows: e.tensor_copy(bh[0:r, :], bA[0:r, :]))
            k.spacer()
            V(lambda e, bA=bA, bh=bh, bl=bl, r=rows: e.scalar_tensor_tensor(
                bl[0:r, :], bA[0:r, :], 1.0, bh[0:r, :], OP.mult, OP.subtract), inc=True)
            st["vc"][p] = k.sem["vec"].n
            k.wait("scalar", "vec", st["vc"][p])
            k.dma("scalar", _ap(d_hi, lo, [[W, rows], [1, W]]), bh[0:rows, :], stsem[p])
            k.dma("scalar", _ap(d_lo, lo, [[W, rows], [1, W]]), bl[0:rows, :], stsem[p])
            st["dc"][p] = k.sem[stsem[p]].n

    def attn_branch(Q, wp, tapbase, cbrow):
        pbufs = [sbAV[:, :], sbL[:, :], sbT1[:, :]]
        def pview(i):
            t = pbufs[i % 3]
            return bass.AP(t.tensor, t.offset, [t.ap[0], [64, 4], [8, 8], [1, 8]])
        for i in range(8):
            in0 = _ap(Q, 24 * i, [[768, 128], [192, 4], [0, 8], [1, 8]])
            in1 = _ap(Q, 8, [[768, 128], [192, 4], [24, 8], [1, 8]])
            V(lambda e, in0=in0, in1=in1, dv=pview(i): e.tensor_tensor(dv, in0, in1, OP.mult))
            if i == 0:
                k.spacer()
            dst = _ap(sbS, 8 * i, [[256, 128], [64, 4], [1, 8]])
            rv = pbufs[i % 3]
            rv = bass.AP(rv.tensor, rv.offset, [rv.ap[0], [64, 4], [8, 8], [1, 8]])
            V(lambda e, dst=dst, rv=rv: e.tensor_reduce(dst, rv, AX.X, OP.add))
        k.spacer()
        V(lambda e: e.tensor_reduce(sbM[:], sbS[:].rearrange("p (bi j) -> p bi j", j=8), AX.X, OP.max))
        k.spacer()
        mbc = _ap(sbM, 0, [[32, 128], [1, 32], [0, 8]])
        V(lambda e, mbc=mbc: e.tensor_tensor(
            sbPr[:].rearrange("p (bi j) -> p bi j", j=8),
            sbS[:].rearrange("p (bi j) -> p bi j", j=8), mbc, OP.subtract), inc=True)
        k.wait("scalar", "vec")
        C(lambda e: e.activation(sbPr[:], sbPr[:], AF.Exp, scale=SCALE_ATT), inc=True)
        k.wait("vector", "sca")
        V(lambda e: e.tensor_reduce(sbZr[:], sbPr[:].rearrange("p (bi j) -> p bi j", j=8), AX.X, OP.add))
        k.spacer()
        V(lambda e: e.reciprocal(sbZr[:], sbZr[:]))
        abufs = [sbS[:, :], sbL[:, :], sbT1[:, :]]
        for i in range(8):
            in0 = _ap(sbPr, 8 * i, [[256, 128], [64, 4], [1, 8], [0, 8]])
            in1 = _ap(Q, 16, [[768, 128], [192, 4], [24, 8], [1, 8]])
            t = abufs[i % 3]
            dstp = bass.AP(t.tensor, t.offset, [t.ap[0], [64, 4], [1, 8], [8, 8]])
            V(lambda e, in0=in0, in1=in1, dstp=dstp: e.tensor_tensor(dstp, in0, in1, OP.mult))
            if i == 0:
                k.spacer()
            dst = _ap(sbAV, 8 * i, [[256, 128], [64, 4], [1, 8]])
            rv = bass.AP(t.tensor, t.offset, [t.ap[0], [64, 4], [8, 8], [1, 8]])
            V(lambda e, dst=dst, rv=rv: e.tensor_reduce(dst, rv, AX.X, OP.add))
        Vall = _ap(Q, 16, [[768, 128], [192, 4], [24, 8], [1, 8]])
        Lv = _ap(sbL, 0, [[256, 128], [64, 4], [1, 64]])
        cb = _ap(tapT, 64 * cbrow, [[512, 128], [0, 4], [1, 64]])
        V(lambda e, Vall=Vall: e.tensor_tensor(
            _ap(sbL, 0, [[256, 128], [64, 4], [8, 8], [1, 8]]), Vall,
            _ap(tapT, 64 * (tapbase + 1), [[512, 128], [0, 4], [8, 8], [1, 8]]), OP.mult))
        vs1 = _ap(Q, 16 + 24, [[768, 128], [192, 4], [24, 7], [1, 8]])
        tp1 = _ap(tapT, 64 * (tapbase + 2), [[512, 128], [0, 4], [8, 7], [1, 8]])
        tslA = _ap(sbS, 0, [[256, 128], [64, 4], [8, 7], [1, 8]])
        V(lambda e, vs1=vs1, tp1=tp1, tslA=tslA: e.tensor_tensor(tslA, vs1, tp1, OP.mult))
        vsm = _ap(Q, 16, [[768, 128], [192, 4], [24, 7], [1, 8]])
        tm1 = _ap(tapT, 64 * (tapbase + 0), [[512, 128], [0, 4], [8, 7], [1, 8]])
        tslB = _ap(sbT1, 0, [[256, 128], [64, 4], [8, 7], [1, 8]])
        V(lambda e, vsm=vsm, tm1=tm1, tslB=tslB: e.tensor_tensor(tslB, vsm, tm1, OP.mult))
        dstL1 = _ap(sbL, 0, [[256, 128], [64, 4], [8, 7], [1, 8]])
        V(lambda e, dstL1=dstL1, tslA=tslA: e.tensor_tensor(dstL1, dstL1, tslA, OP.add))
        k.spacer()
        dstL2 = _ap(sbL, 8, [[256, 128], [64, 4], [8, 7], [1, 8]])
        V(lambda e, dstL2=dstL2, tslB=tslB: e.tensor_tensor(dstL2, dstL2, tslB, OP.add))
        k.spacer()
        V(lambda e, Lv=Lv, cb=cb: e.tensor_tensor(Lv, Lv, cb, OP.add))
        zbc = _ap(sbZr, 0, [[32, 128], [8, 4], [1, 8], [0, 8]])
        avv = sbAV[:].rearrange("p (b i c) -> p b i c", b=4, c=8)
        k.spacer()
        t1v = sbT1[:].rearrange("p (b i c) -> p b i c", b=4, c=8)
        V(lambda e, zbc=zbc, avv=avv, t1v=t1v: e.tensor_tensor(t1v, avv, zbc, OP.mult))
        k.spacer()
        V(lambda e: e.tensor_tensor(sbAV[:], sbT1[:], sbL[:], OP.add), inc=True)
        for b4 in range(4):
            pband = [pbA0, pbA1][b4 % 2]
            k.wait("tensor", "vec")
            k.wait("tensor", "sca")
            T(lambda e, b4=b4, pband=pband: e.transpose(
                _ap(pband, 0, [[512, 64], [1, 128]]), sbAV[:, 64 * b4:64 * b4 + 64], ident[:]), inc=True)
            k.wait("scalar", "ten")
            C(lambda e, b4=b4, pband=pband: e.copy(
                sbOT[:, 128 * b4:128 * b4 + 128], _ap(pband, 0, [[512, 64], [1, 128]])), inc=True)
        for b4 in range(4):
            pband = [pbA0, pbA1][b4 % 2]
            k.wait("tensor", "sca")
            k.wait("tensor", "vec")
            T(lambda e, b4=b4, pband=pband, wp=wp: e.matmul(
                pband[:, 0:128], sbOT[:, 128 * b4:128 * b4 + 128], wp[:], start=True, stop=True), inc=True)
            k.wait("vector", "ten")
            V(lambda e, b4=b4, pband=pband: e.tensor_copy(
                sbAtt[:, 128 * b4:128 * b4 + 128], pband[:, 0:128]), inc=True)

    def dump_att1(tp, chi):
        if "contig" in DBG_SKIP:
            dst = _ap(d_del1, tp * (NM * 128 * TCOL) + chi * 128 * TCOL, [[512, 128], [1, 512]])
            k.dma("sync", dst, sbAtt[:, :], "dsync")
            return
        for gB in range(4):
            j = gB // 2
            mbase = (8 * gB) % 16
            off = tp * (NM * 128 * TCOL) + mbase * (128 * TCOL) + 32 * chi + 16 * j
            dst = _ap(d_del1, off, [[4096, 128], [512, 8], [1, 16]])
            src = _ap(sbAtt, 128 * gB, [[512, 128], [16, 8], [1, 16]])
            k.dma("gpsimd", dst, src, "dsync")

    def dump_att2(tp, chi):
        if "contig" in DBG_SKIP:
            dst = _ap(d_del2, tp * (NM * 128 * TCOL) + chi * 128 * TCOL, [[512, 128], [1, 512]])
            k.dma("scalar", dst, sbAtt[:, :], "dsync")
            return
        for wgB in range(4):
            for j in range(2):
                off = tp * (NM * 128 * TCOL) + wgB * (128 * TCOL) + 32 * chi + 16 * j
                dst = _ap(d_del2, off, [[512, 128], [4 * 128 * TCOL, 4], [1, 16]])
                src = _ap(sbAtt, 128 * wgB + 64 * j, [[512, 128], [16, 4], [1, 16]])
                k.dma("gpsimd", dst, src, "dsync")

    ast = {"qkv": 0, "evac": 0}

    def attn_chunk(tile, chi, tp):
        cidx = CHT * tile + chi
        k.wait("gpsimd", "ten", ast["qkv"])
        k.dma("gpsimd", sbD[:, :], d_xD[cidx], "lat")
        k.dma("gpsimd", sbP2[:, :], d_xP2[cidx], "lat")
        k.wait("tensor", "lat")
        k.wait("tensor", "vec", ast["evac"])
        for b4 in range(4):
            pband = [pbA0, pbA1][b4 % 2]
            k.wait("tensor", "sca")
            T(lambda e, b4=b4, pband=pband: e.matmul(
                pband[:, 0:192], sbD[:, 128 * b4:128 * b4 + 128], wbd1S[:], start=True, stop=True), inc=True)
            k.wait("scalar", "ten")
            C(lambda e, b4=b4, pband=pband: e.copy(sbQ1[:, 192 * b4:192 * b4 + 192], pband[:, 0:192]), inc=True)
        for b4 in range(4):
            pband = [pbA0, pbA1][b4 % 2]
            k.wait("tensor", "sca")
            T(lambda e, b4=b4, pband=pband: e.matmul(
                pband[:, 0:192], sbP2[:, 128 * b4:128 * b4 + 128], wbd2S[:], start=True, stop=True), inc=True)
            k.wait("scalar", "ten")
            C(lambda e, b4=b4, pband=pband: e.copy(sbQ2[:, 192 * b4:192 * b4 + 192], pband[:, 0:192]), inc=True)
        ast["qkv"] = k.sem["ten"].n
        k.wait("vector", "sca")
        k.wait("vector", "dsync")
        attn_branch(sbQ1, wpLS, 0, 6)
        k.wait("gpsimd", "vec")
        dump_att1(tp, chi)
        k.wait("vector", "dsync")
        attn_branch(sbQ2, wpHS, 3, 7)
        k.wait("gpsimd", "vec")
        dump_att2(tp, chi)
        ast["evac"] = k.sem["vec"].n

    def z_phase(tile, tp):
        for m in range(NM):
            k.wait("sync", "vec")
            k.dma("sync", sbDel1[:, :], _ap(d_del1, (tp * NM + m) * 128 * TCOL, [[512, 128], [1, 512]]), "ld")
            k.dma("sync", sbDel2[:, :], _ap(d_del2, (tp * NM + m) * 128 * TCOL, [[512, 128], [1, 512]]), "ld")
            k.dma("sync", sbXT[:, :], d_xTok[tile, m], "ld")
            k.dma("sync", sbXR[:, :], d_xRes[tile, m], "ld")
            k.wait("vector", "ld")
            V(lambda e: e.tensor_tensor(sbA[:], sbDel1[:], sbDel2[:], OP.add))
            k.spacer()
            V(lambda e: e.tensor_tensor(sbZt[:], sbA[:], biasT[:], OP.add), inc=True)
            k.spacer()
            V(lambda e: e.tensor_reduce(sbMu[:], sbZt[:].rearrange("p (a c) -> p a c", c=16), AX.X, OP.add))
            k.spacer()
            V(lambda e: e.tensor_scalar(sbRs[:], sbMu[:], 1.0 / 16.0, None, OP.mult))
            k.wait("scalar", "vec")
            C(lambda e: e.activation(sbSq[:], sbZt[:], AF.Square), inc=True)
            k.wait("vector", "sca")
            V(lambda e: e.tensor_reduce(sbVar[:], sbSq[:].rearrange("p (a c) -> p a c", c=16), AX.X, OP.add))
            k.spacer()
            V(lambda e: e.tensor_scalar(sbT2[:], sbVar[:], 1.0 / 16.0, 1e-5, OP.mult, OP.add))
            V(lambda e: e.tensor_tensor(sbMu[:], sbRs[:], sbRs[:], OP.mult))
            k.spacer()
            V(lambda e: e.tensor_tensor(sbVar[:], sbT2[:], sbMu[:], OP.subtract), inc=True)
            k.wait("scalar", "vec")
            C(lambda e: e.activation(sbT2[:], sbVar[:], AF.Sqrt), inc=True)
            k.wait("vector", "sca")
            V(lambda e: e.reciprocal(sbVar[:], sbT2[:]))
            mubc = _ap(sbRs, 0, [[32, 128], [1, 32], [0, 16]])
            rsbc = _ap(sbVar, 0, [[32, 128], [1, 32], [0, 16]])
            zv = sbZt[:].rearrange("p (a c) -> p a c", c=16)
            avz = sbA[:].rearrange("p (a c) -> p a c", c=16)
            V(lambda e, mubc=mubc, zv=zv, avz=avz: e.tensor_tensor(avz, zv, mubc, OP.subtract))
            k.spacer()
            V(lambda e, rsbc=rsbc, zv=zv, avz=avz: e.tensor_tensor(zv, avz, rsbc, OP.mult))
            k.spacer()
            V(lambda e: e.tensor_tensor(sbA[:], sbZt[:], gamT[:], OP.mult))
            k.spacer()
            V(lambda e: e.tensor_tensor(sbZt[:], sbA[:], betT[:], OP.add))
            k.spacer()
            V(lambda e: e.tensor_tensor(sbA[:], sbZt[:], sbXT[:], OP.add))
            k.spacer()
            V(lambda e: e.tensor_tensor(sbZt[:], sbA[:], sbXR[:], OP.add))
            k.spacer()
            split(sbZt[:, :], col(zh, m), col(zl, m))
            k.wait("scalar", "vec")
            C(lambda e: e.activation(sbSq[:], sbZt[:], AF.Square), inc=True)
            k.wait("vector", "sca")
            k.wait("vector", "ten")
            split(sbSq[:, :], sbBh[:, :], sbBl[:, :])
            k.wait("tensor", "vec")
            T(lambda e, m=m: e.matmul(pbG0[0:1, :], onescol[:], col(zh, m), start=(m == 0), stop=False))
            T(lambda e, m=m: e.matmul(pbG0[0:1, :], onescol[:], col(zl, m), start=False, stop=(m == NM - 1)))
            T(lambda e, m=m: e.matmul(pbG1[0:1, :], onescol[:], sbBh[:], start=(m == 0), stop=False))
            T(lambda e, m=m: e.matmul(pbG1[0:1, :], onescol[:], sbBl[:], start=False, stop=(m == NM - 1)), inc=True)
        finalize_ln(pbG0, pbG1)
        k.wait("vector", "ten")
        for m in range(NM):
            V(lambda e, m=m: e.tensor_tensor(sbA[:], col(zh, m), col(zl, m), OP.add))
            k.spacer()
            V(lambda e: e.tensor_tensor(sbZt[:], sbA[:], pbB0[:, :], OP.subtract))
            k.spacer()
            V(lambda e: e.tensor_tensor(sbA[:], sbZt[:], pbB1[:, :], OP.mult))
            k.spacer()
            split(sbA[:, :], col(zh, m), col(zl, m))

    def finalize_ln(psA, psB):
        k.wait("vector", "ten")
        V(lambda e, psA=psA: e.tensor_scalar(sbRow[0:1, :], psA[0:1, :], 1.0 / 2048.0, None, OP.mult))
        V(lambda e, psB=psB: e.tensor_scalar(sbFm[0:1, :], psB[0:1, :], 1.0 / 2048.0, 1e-5, OP.mult, OP.add))
        V(lambda e: e.tensor_tensor(sbY[0:1, :], sbRow[0:1, :], sbRow[0:1, :], OP.mult))
        k.spacer()
        V(lambda e: e.tensor_tensor(sbSq[0:1, :], sbFm[0:1, :], sbY[0:1, :], OP.subtract), inc=True)
        k.wait("scalar", "vec")
        C(lambda e: e.activation(sbFm[0:1, :], sbSq[0:1, :], AF.Sqrt), inc=True)
        k.wait("vector", "sca")
        V(lambda e: e.reciprocal(sbY[0:1, :], sbFm[0:1, :]), inc=True)
        k.wait("tensor", "vec")
        T(lambda e: e.matmul(pbB0[:, :], onesrow[:], sbRow[0:1, :], start=True, stop=True))
        T(lambda e: e.matmul(pbB1[:, :], onesrow[:], sbY[0:1, :], start=True, stop=True), inc=True)

    def router(tile):
        k.wait("tensor", "vec")
        for kc in range(NM):
            aoff = 0
            k.wait("sync", "ten")
            k.dma("sync", _ap(aABuf, aoff, [[2 * AACW, 128], [1, AACW]]),
                  _ap(d_aAllh, kc * AACW, [[AANW, 128], [1, AACW]]), "ldw")
            k.dma("sync", _ap(aABuf, aoff + AACW, [[2 * AACW, 128], [1, AACW]]),
                  _ap(d_aAlll, kc * AACW, [[AANW, 128], [1, AACW]]), "ldw")
            k.wait("tensor", "ldw")
            for mb, pband, mw in ((0, pbA0, 128), (1, pbA1, 96)):
                Ah = _ap(aABuf, aoff + 128 * mb, [[2 * AACW, 128], [1, mw]])
                Al = _ap(aABuf, aoff + AACW + 128 * mb, [[2 * AACW, 128], [1, mw]])
                for pi, (lhs, rhs) in enumerate(
                        ((Ah, col(zh, kc)), (Al, col(zh, kc)), (Ah, col(zl, kc)))):
                    T(lambda e, pband=pband, lhs=lhs, rhs=rhs, mw=mw,
                        st=(kc == 0 and pi == 0), sp=(kc == NM - 1 and pi == 2):
                        e.matmul(pband[0:mw, :], lhs, rhs, start=st, stop=sp),
                        inc=(mb == 1 and pi == 2))
        k.wait("scalar", "ten")
        C(lambda e: e.activation(sbFm[:, :], pbA1[64:72, :], AF.Sigmoid), inc=True)
        k.wait("tensor", "sca")
        for b4 in range(4):
            k.wait("tensor", "vec")
            T(lambda e, b4=b4: e.transpose(
                _ap(pbB0, 0, [[512, 128], [1, 8]]), sbFm[:, 128 * b4:128 * b4 + 128], ident[0:8, 0:8]), inc=True)
            k.wait("vector", "ten")
            V(lambda e, b4=b4: e.tensor_copy(
                _ap(sbSc, 8 * b4, [[32, 128], [1, 8]]), _ap(pbB0, 0, [[512, 128], [1, 8]])), inc=True)
        def B4(i):
            return dict(
                sc=_ap(sbSc, 8 * i, [[32, 128], [1, 8]]),
                bi=_ap(sbSc2, 8 * i, [[32, 128], [1, 8]]),
                eq=_ap(sbSc3, 8 * i, [[32, 128], [1, 8]]),
                msk=_ap(sbT1, 8 * i, [[256, 128], [1, 8]]),
                m1=_ap(sbSc4, i, [[8, 128], [1, 1]]),
                m1b=_ap(sbSc4, i, [[8, 128], [0, 8]]),
                m2=_ap(sbT2, i, [[32, 128], [1, 1]]),
                m2b=_ap(sbT2, i, [[32, 128], [0, 8]]),
                den=_ap(sbMu, i, [[32, 128], [1, 1]]),
                den2=_ap(sbMu, 8 + i, [[32, 128], [1, 1]]),
                rec=_ap(sbMu, 16 + i, [[32, 128], [1, 1]]),
            )
        BB4 = [B4(i) for i in range(4)]
        for d in BB4:
            V(lambda e, d=d: e.tensor_tensor(d["bi"], d["sc"], ebT[:], OP.add))
        for d in BB4:
            V(lambda e, d=d: e.tensor_reduce(d["m1"], d["bi"], AX.X, OP.max))
        for d in BB4:
            V(lambda e, d=d: e.scalar_tensor_tensor(d["eq"], d["bi"], 1.0, d["m1b"], OP.mult, OP.is_equal))
        for d in BB4:
            V(lambda e, d=d: e.scalar_tensor_tensor(d["msk"], d["eq"], -2e30, d["bi"], OP.mult, OP.add))
        for d in BB4:
            V(lambda e, d=d: e.tensor_reduce(d["m2"], d["msk"], AX.X, OP.max))
        for d in BB4:
            V(lambda e, d=d: e.scalar_tensor_tensor(d["eq"], d["bi"], 1.0, d["m2b"], OP.mult, OP.is_ge))
        for d in BB4:
            V(lambda e, d=d: e.tensor_tensor(d["msk"], d["sc"], d["eq"], OP.mult))
        for d in BB4:
            V(lambda e, d=d: e.tensor_reduce(d["den"], d["msk"], AX.X, OP.add))
        for d in BB4:
            V(lambda e, d=d: e.tensor_scalar(d["den2"], d["den"], 1e-8, 8.0, OP.add, OP.mult))
        for d in BB4:
            V(lambda e, d=d: e.reciprocal(d["rec"], d["den2"]))
        for d in BB4:
            V(lambda e, d=d: e.tensor_scalar(d["eq"], d["msk"], d["rec"], None, OP.mult), inc=True)
        k.spacer()
        for b4 in range(4):
            src = _ap(sbSc3, 8 * b4, [[32, 128], [1, 8], [0, 8]])
            V(lambda e, src=src: e.tensor_copy(
                sbWt[:].rearrange("p (e r) -> p e r", r=8), src), inc=True)
            k.wait("tensor", "vec")
            T(lambda e, b4=b4: e.transpose(_ap(pbB0, 0, [[512, 64], [1, 128]]), sbWt[:, :], ident[:]), inc=True)
            k.wait("vector", "ten")
            V(lambda e, b4=b4: e.tensor_copy(wexp[:, 128 * b4:128 * b4 + 128],
                                             _ap(pbB0, 0, [[512, 64], [1, 128]])), inc=True)
        V(lambda e: e.tensor_tensor(sbZt[0:64, :], pbA0[0:64, :], wexp[:, :], OP.mult))
        V(lambda e: e.tensor_scalar(sbZt[64:72, :], pbA1[0:8, :], 0.125, None, OP.mult), inc=True)
        k.spacer()
        split(sbZt[0:72, :], ug_h[0:72, :], ug_l[0:72, :])
        V(lambda e: e.tensor_tensor(sbZt[0:64, :], pbA0[64:128, :], wexp[:, :], OP.mult))
        V(lambda e: e.tensor_scalar(sbZt[64:72, :], pbA1[32:40, :], 0.125, None, OP.mult), inc=True)
        k.spacer()
        split(sbZt[0:72, :], uu_h[0:72, :], uu_l[0:72, :])

    def phase_b(tile, il=None):
        GW = KC_GU * 128
        tb = {"g": [0, 0], "u": [0, 0]}
        pinb = {"vec": {}, "sca": {}}
        for kg in range(NKG):
            if il is not None:
                il.tick()
            pg = [pbG0, pbG1][kg % 2]
            pu = [pbU0, pbU1][kg % 2]
            for gi, (gu, pband, semn) in enumerate((("g", pg, "ldw"), ("u", pu, "ldu"))):
                half = 0 if gu == "g" else 4352
                k.wait("sync", "ten", tb[gu][0])
                k.dma("sync", _ap(wbuf, half, [[WBUF_W, 128], [1, GW]]),
                      _ap(d_wGUh, (kg * 128 * 2 + (0 if gu == "g" else 1)) * GW, [[2 * GW, 128], [1, GW]]), semn)
                k.dma("sync", _ap(wbuf, half + GW, [[WBUF_W, 128], [1, GW]]),
                      _ap(d_wGUl, (kg * 128 * 2 + (0 if gu == "g" else 1)) * GW, [[2 * GW, 128], [1, GW]]), semn)
                k.wait("tensor", semn)
                k.wait("tensor", "vec", pinb["vec"].get(kg - 2, 0))
                k.wait("tensor", "sca", pinb["sca"].get(kg - 2, 0))
                nmm = 0
                for kc in range(KC_GU):
                    Wh = _ap(wbuf, half + kc * 128, [[WBUF_W, 128], [1, 128]])
                    Wl = _ap(wbuf, half + GW + kc * 128, [[WBUF_W, 128], [1, 128]])
                    if kc < NM:
                        rh, rl = col(zh, kc), col(zl, kc)
                    else:
                        rh, rl = (ug_h[:, :], ug_l[:, :]) if gu == "g" else (uu_h[:, :], uu_l[:, :])
                    for lhs, rhs in ((Wh, rh), (Wl, rh), (Wh, rl)):
                        sp = (nmm == 3 * KC_GU - 1)
                        T(lambda e, pband=pband, lhs=lhs, rhs=rhs,
                            st=(nmm == 0), sp=sp:
                            e.matmul(pband[:, :], lhs, rhs, start=st, stop=sp),
                            inc=sp)
                        nmm += 1
                tb[gu][0] = k.sem["ten"].n
            k.wait("scalar", "ten")
            C(lambda e, pg=pg: e.activation(sbSil[:], pg[:, :], AF.Silu), inc=True)
            pinb["sca"][kg] = k.sem["sca"].n
            k.wait("vector", "sca")
            k.wait("vector", "ten")
            V(lambda e, pu=pu: e.tensor_tensor(sbZt[:], sbSil[:], pu[:, :], OP.mult))
            k.spacer()
            split(sbZt[:, :], col(hh, kg), col(hl, kg))
            pinb["vec"][kg] = k.sem["vec"].n
            k.wait("tensor", "vec")
            aoff = (kg % 2) * 144
            k.wait("sync", "lda")
            k.dma("sync", _ap(aDBuf, aoff, [[4 * 72, 128], [1, 72]]),
                  _ap(d_aDnh, kg * 72, [[ADNWP, 128], [1, 72]]), "lda")
            k.dma("sync", _ap(aDBuf, aoff + 72, [[4 * 72, 128], [1, 72]]),
                  _ap(d_aDnl, kg * 72, [[ADNWP, 128], [1, 72]]), "lda")
            k.wait("tensor", "lda")
            Ah = _ap(aDBuf, aoff, [[4 * 72, 128], [1, 72]])
            Al = _ap(aDBuf, aoff + 72, [[4 * 72, 128], [1, 72]])
            T(lambda e, Ah=Ah, kg=kg, st=(kg == 0): e.matmul(
                pbB0[0:72, :], Ah, col(hh, kg), start=st, stop=False))
            T(lambda e, Al=Al, kg=kg: e.matmul(pbB0[0:72, :], Al, col(hh, kg), start=False, stop=False))
            T(lambda e, Ah=Ah, kg=kg, sp=(kg == NKG - 1): e.matmul(
                pbB0[0:72, :], Ah, col(hl, kg), start=False, stop=sp), inc=True)
        k.wait("vector", "ten")
        V(lambda e: e.tensor_tensor(sbZt[0:64, :], pbB0[0:64, :], wexp[:, :], OP.mult))
        V(lambda e: e.tensor_scalar(sbZt[64:72, :], pbB0[64:72, :], 0.125, None, OP.mult), inc=True)
        k.spacer()
        split(sbZt[0:72, :], ud_h[0:72, :], ud_l[0:72, :])

    def phase_c(tile, tp, il=None):
        KH = 22
        DH = KH * 128
        tc_ = {"A": 0, "B": 0}
        pinc = {}
        for m in range(NM):
            if il is not None:
                il.tick()
            pband = [pbG0, pbG1][m % 2]
            k.wait("sync", "ten", tc_["A"])
            k.dma("sync", _ap(wbuf, 0, [[WBUF_W, 128], [1, DH]]),
                  _ap(d_wDh, m * 128 * D_STRIDE, [[D_STRIDE, 128], [1, DH]]), "ldw")
            k.dma("sync", _ap(wbuf, DH, [[WBUF_W, 128], [1, DH]]),
                  _ap(d_wDl, m * 128 * D_STRIDE, [[D_STRIDE, 128], [1, DH]]), "ldw")
            k.wait("sync", "ten", tc_["B"])
            k.dma("sync", _ap(wbuf, 2 * DH, [[WBUF_W, 128], [1, D_STRIDE - DH]]),
                  _ap(d_wDh, m * 128 * D_STRIDE + DH, [[D_STRIDE, 128], [1, D_STRIDE - DH]]), "ldu")
            k.dma("sync", _ap(wbuf, 2 * DH + (D_STRIDE - DH), [[WBUF_W, 128], [1, D_STRIDE - DH]]),
                  _ap(d_wDl, m * 128 * D_STRIDE + DH, [[D_STRIDE, 128], [1, D_STRIDE - DH]]), "ldu")
            k.wait("tensor", "ldw")
            k.wait("tensor", "vec", pinc.get(m - 2, 0))
            nmm = 0
            for kk in range(KC_D):
                if kk == KH:
                    k.wait("tensor", "ldu")
                if kk < KH:
                    Wh = _ap(wbuf, kk * 128, [[WBUF_W, 128], [1, 128]])
                    Wl = _ap(wbuf, DH + kk * 128, [[WBUF_W, 128], [1, 128]])
                else:
                    Wh = _ap(wbuf, 2 * DH + (kk - KH) * 128, [[WBUF_W, 128], [1, 128]])
                    Wl = _ap(wbuf, 2 * DH + (D_STRIDE - DH) + (kk - KH) * 128, [[WBUF_W, 128], [1, 128]])
                if kk < NKG:
                    rh, rl = col(hh, kk), col(hl, kk)
                else:
                    rh, rl = ud_h[:, :], ud_l[:, :]
                for lhs, rhs in ((Wh, rh), (Wl, rh), (Wh, rl)):
                    sp = (nmm == 3 * KC_D - 1)
                    spA = (nmm == 3 * KH - 1)
                    T(lambda e, pband=pband, lhs=lhs, rhs=rhs,
                        st=(nmm == 0), sp=sp:
                        e.matmul(pband[:, :], lhs, rhs, start=st, stop=sp),
                        inc=(sp or spA))
                    if spA:
                        tc_["A"] = k.sem["ten"].n
                    nmm += 1
            tc_["B"] = k.sem["ten"].n
            k.wait("vector", "ten")
            V(lambda e, pband=pband, m=m: e.tensor_tensor(sbZt[:], pband[:, :], col(zh, m), OP.add))
            k.spacer()
            V(lambda e, m=m: e.tensor_tensor(sbA[:], sbZt[:], col(zl, m), OP.add), inc=True)
            pinc[m] = k.sem["vec"].n
            k.spacer()
            split(sbA[:, :], col(zh, m), col(zl, m))
            k.wait("scalar", "vec")
            C(lambda e: e.activation(sbSq[:], sbA[:], AF.Square), inc=True)
            k.wait("vector", "sca")
            k.wait("vector", "ten")
            split(sbSq[:, :], sbBh[:, :], sbBl[:, :])
            k.wait("tensor", "vec")
            T(lambda e, m=m: e.matmul(pbU0[0:1, :], onescol[:], col(zh, m), start=(m == 0), stop=False))
            T(lambda e, m=m: e.matmul(pbU0[0:1, :], onescol[:], col(zl, m), start=False, stop=(m == NM - 1)))
            T(lambda e, m=m: e.matmul(pbU1[0:1, :], onescol[:], sbBh[:], start=(m == 0), stop=False))
            T(lambda e, m=m: e.matmul(pbU1[0:1, :], onescol[:], sbBl[:], start=False, stop=(m == NM - 1)), inc=True)
        finalize_ln(pbU0, pbU1)
        k.wait("vector", "ten")
        for m in range(NM):
            k.wait("vector", "dsca")
            V(lambda e, m=m: e.tensor_tensor(sbA[:], col(zh, m), col(zl, m), OP.add))
            k.spacer()
            V(lambda e: e.tensor_tensor(sbZt[:], sbA[:], pbB0[:, :], OP.subtract))
            k.spacer()
            V(lambda e: e.tensor_tensor(sbY[:], sbZt[:], pbB1[:, :], OP.mult), inc=True)
            k.wait("scalar", "vec")
            k.dma("scalar", d_out[tile, m], sbY[:, :], "dsca")

    dv_iltiles = dev.get("iltiles", NTILE)
    dv_il = dev.get("il", (dv_tiles == NTILE or "iltiles" in dev)
                    and dv_phases == ("attn", "z", "router", "b", "c")
                    and dv_cht == CHT and not dv_expose)

    def chunk_thunks(tile):
        return [(lambda chi=chi, tile=tile: attn_chunk(tile, chi, tile % 2))
                for chi in range(CHT)]

    il0 = IL(chunk_thunks(0), 30) if dv_il else None
    prepass(NKG * 128 * GU_STRIDE, d_wGU, d_wGUh, d_wGUl, il=il0)
    prepass(NM * 128 * D_STRIDE, d_wD, d_wDh, d_wDl, il=il0)
    prepass(128 * AANW, d_aAll, d_aAllh, d_aAlll, il=il0)
    prepass(128 * ADNWP, d_aDn, d_aDnh, d_aDnl, il=il0)
    if il0 is not None:
        il0.flush()
    k.barrier()

    if dv_il:
        for tile in range(dv_iltiles):
            tp = tile % 2
            z_phase(tile, tp)
            k.barrier()
            router(tile)
            k.barrier()
            if tile + 1 < dv_iltiles:
                nthunks = chunk_thunks(tile + 1)
                ilb, ilc = IL(nthunks[:12], 3), IL(nthunks[12:], 3)
            else:
                ilb = ilc = None
            phase_b(tile, il=ilb)
            if ilb is not None:
                ilb.flush()
            k.barrier()
            phase_c(tile, tp, il=ilc)
            if ilc is not None:
                ilc.flush()
            k.barrier()

    for tile in (() if dv_il else range(dv_tiles)):
        tp = tile % 2
        if "attn" in dv_phases:
            for chi in range(dv_cht):
                attn_chunk(tile, chi, tp)
            k.barrier()
            if d_dbgw is not None:
                for di, src in ((0, sbQ1[:, :]), (1, sbQ2[:, :])):
                    k.dma("scalar", _ap(d_dbgw, di * 128 * 768, [[768, 128], [1, 768]]), src, "dsca")
                for di, src, wdt in ((2, sbZr[:, :], 32), (3, sbPr[:, :], 256),
                                     (4, sbAV[:, :], 256), (5, sbL[:, :], 256)):
                    k.dma("scalar", _ap(d_dbgw, di * 128 * 768, [[768, 128], [1, wdt]]), src, "dsca")
                k.barrier()
        if "z" in dv_phases:
            z_phase(tile, tp)
            k.barrier()
            if d_dbgz is not None:
                for m in range(NM):
                    V(lambda e, m=m: e.tensor_tensor(sbY[:], col(zh, m), col(zl, m), OP.add), inc=True)
                    k.wait("scalar", "vec")
                    k.dma("scalar", _ap(d_dbgz, (tp * NM + m) * 128 * TCOL, [[512, 128], [1, 512]]), sbY[:, :], "dsca")
                    k.wait("vector", "dsca")
                k.barrier()
        if "router" in dv_phases:
            router(tile)
            k.barrier()
        if "b" in dv_phases:
            phase_b(tile)
            k.barrier()
            if d_dbgh is not None:
                for kg in range(4):
                    V(lambda e, kg=kg: e.tensor_tensor(sbY[:], col(hh, kg), col(hl, kg), OP.add), inc=True)
                    k.wait("scalar", "vec")
                    k.dma("scalar", _ap(d_dbgh, kg * 128 * TCOL, [[512, 128], [1, 512]]), sbY[:, :], "dsca")
                    k.wait("vector", "dsca")
                k.barrier()
        if "c" in dv_phases:
            phase_c(tile, tp)
            k.barrier()

    k.wait("sync", "dsca")
    k.wait("sync", "dsync")

    @blk.sync
    def _(eng):
        for fn in prog.ops["sync"]:
            fn(eng)

    @blk.tensor
    def _(eng):
        for fn in prog.ops["tensor"]:
            fn(eng)

    @blk.vector
    def _(eng):
        for fn in prog.ops["vector"]:
            fn(eng)

    @blk.scalar
    def _(eng):
        for fn in prog.ops["scalar"]:
            fn(eng)

    @blk.gpsimd
    def _(eng):
        for fn in prog.ops["gpsimd"]:
            fn(eng)

    ctx.close()
    return nc


def kernel(**inputs):
    x = np.asarray(inputs["L0_attn_x"], np.float32)
    wmaps = _stage_weights(inputs)
    in_maps = []
    for core in range(NCORES):
        m = dict(wmaps)
        m.update(_stage_x(x, core))
        in_maps.append(m)
    nc = build_nc()
    res = run_bass_kernel_spmd(nc, in_maps, core_ids=list(range(NCORES)))
    outs = [r["out"] for r in res.results]
    return _gather_out(outs)


# version 08
# speedup vs baseline: 1.4383x; 1.4383x over previous
import numpy as np
import ml_dtypes
from contextlib import ExitStack

import concourse.bass as bass
import concourse.mybir as mybir
from concourse.bass_utils import run_bass_kernel_spmd

F32 = mybir.dt.float32
BF16 = mybir.dt.bfloat16
OP = mybir.AluOpType
AF = mybir.ActivationFunctionType
AX = mybir.AxisListType

NCORES = 8
BB, HH, WW = 4, 8192, 2048
NCHUNK = 128
NTILE = 8
CHT = 16
NM = 16
TCOL = 512
DFF = 5504
NKG = 43
KC_GU = 17
KC_D = 44
GUON = 2176
DON = 5632
SCALE_ATT = float(8.0 ** -0.5)
DBG_HOOK = []
DBG_SKIP = set()
AANW = NM * 224
AACW = 224
ADNW = NKG * 72
ADNWP = 3104
GU_STRIDE = 2 * KC_GU * 128
D_STRIDE = KC_D * 128
WBUF_W = 2 * D_STRIDE


def _ap(t, offset, dims):
    return bass.AP(t.tensor if hasattr(t, "tensor") else t, offset, dims)



def _stage_weights(I):
    f = np.float32
    W_gate = np.asarray(I["L0_ffn_W_gate"], f)
    W_up = np.asarray(I["L0_ffn_W_up"], f)
    W_down = np.asarray(I["L0_ffn_W_down"], f)
    sB_gate = np.asarray(I["L0_ffn_sB_gate"], f)
    sB_up = np.asarray(I["L0_ffn_sB_up"], f)
    sB_down = np.asarray(I["L0_ffn_sB_down"], f)
    B_gate = np.asarray(I["L0_ffn_B_gate"], f).reshape(64, DFF)
    B_up = np.asarray(I["L0_ffn_B_up"], f).reshape(64, DFF)
    B_down = np.asarray(I["L0_ffn_B_down"], f).reshape(64, WW)
    A_gate = np.asarray(I["L0_ffn_A_gate"], f).transpose(1, 0, 2).reshape(WW, 64)
    A_up = np.asarray(I["L0_ffn_A_up"], f).transpose(1, 0, 2).reshape(WW, 64)
    A_down = np.asarray(I["L0_ffn_A_down"], f).transpose(1, 0, 2).reshape(DFF, 64)
    sA_gate = np.asarray(I["L0_ffn_sA_gate"], f)
    sA_up = np.asarray(I["L0_ffn_sA_up"], f)
    sA_down = np.asarray(I["L0_ffn_sA_down"], f)
    W_router = np.asarray(I["L0_ffn_W_router"], f)

    zg = np.zeros((GUON - 2120, DFF), f)
    Wg_aug = np.concatenate([W_gate, B_gate, sB_gate, zg], 0)
    Wu_aug = np.concatenate([W_up, B_up, sB_up, zg], 0)
    zd = np.zeros((DON - 5576, WW), f)
    Wd_aug = np.concatenate([W_down, B_down, sB_down, zd], 0)

    def slabify(Waug):
        return Waug.reshape(KC_GU, 128, NKG, 128).transpose(2, 1, 0, 3)
    wGU = np.stack([slabify(Wg_aug), slabify(Wu_aug)], 2)
    wGU = np.ascontiguousarray(wGU, f)
    wD = np.ascontiguousarray(
        Wd_aug.reshape(KC_D, 128, NM, 128).transpose(2, 1, 0, 3), f)

    A_full = np.zeros((WW, AACW), f)
    A_full[:, 0:64] = A_gate
    A_full[:, 64:128] = A_up
    A_full[:, 128:136] = sA_gate
    A_full[:, 160:168] = sA_up
    A_full[:, 192:200] = W_router
    aAll = np.ascontiguousarray(
        A_full.reshape(NM, 128, AACW).transpose(1, 0, 2).reshape(128, AANW), f)
    A_dn_full = np.concatenate([A_down, sA_down], 1)
    aDn = A_dn_full.reshape(NKG, 128, 72).transpose(1, 0, 2).reshape(128, ADNW)
    aDn = np.concatenate([aDn, np.zeros((128, ADNWP - ADNW), f)], 1)
    aDn = np.ascontiguousarray(aDn, f)

    qkv_w = np.asarray(I["L0_attn_qkv_w"], f)
    proj_w = np.asarray(I["L0_attn_proj_w"], f)
    col3_lo = np.r_[0:8, 16:24, 32:40]
    col3_hi = np.r_[8:16, 24:32, 40:48]
    wbd1 = np.zeros((128, 192), f)
    wbd2 = np.zeros((128, 192), f)
    for s in range(8):
        wbd1[16 * s:16 * s + 16, 24 * s:24 * s + 24] = qkv_w[:, col3_lo]
        wbd2[16 * s:16 * s + 16, 24 * s:24 * s + 24] = qkv_w[:, col3_hi]
    wpL = np.zeros((64, 128), f)
    wpH = np.zeros((64, 128), f)
    for s in range(8):
        wpL[8 * s:8 * s + 8, 16 * s:16 * s + 16] = proj_w[0:8, :]
        wpH[8 * s:8 * s + 8, 16 * s:16 * s + 16] = proj_w[8:16, :]

    cw0 = np.asarray(I["L0_attn_conv_w0"], f)
    cb0 = np.asarray(I["L0_attn_conv_b0"], f)
    cw1 = np.asarray(I["L0_attn_conv_w1"], f)
    cb1 = np.asarray(I["L0_attn_conv_b1"], f)
    taps = np.zeros((8, 64), f)
    for s in range(8):
        for c in range(8):
            for ti, tau in enumerate((-1, 0, 1)):
                taps[ti, 8 * s + c] = cw0[c, 0, 1, tau + 1]
                taps[3 + ti, 8 * s + c] = cw1[c, 0, tau + 1, 1]
            taps[6, 8 * s + c] = cb0[c]
            taps[7, 8 * s + c] = cb1[c]

    ln_w = np.asarray(I["L0_attn_ln_w"], f)
    ln_b = np.asarray(I["L0_attn_ln_b"], f)
    proj_b = np.asarray(I["L0_attn_proj_b"], f)
    rows = np.zeros((3, TCOL), f)
    for chi in range(16):
        for j in range(2):
            for c in range(16):
                col = 32 * chi + 16 * j + c
                rows[0, col] = ln_w[c]
                rows[1, col] = ln_b[c]
                rows[2, col] = proj_b[c]
    ebias = np.asarray(I["L0_ffn_expert_bias"], f).reshape(1, 8)
    return dict(wGU=wGU, wD=wD, aAll=aAll, aDn=aDn, wbd1=wbd1, wbd2=wbd2,
                wpL=wpL, wpH=wpH, taps=taps, rows=rows, ebias=ebias)


def _stage_x(x, core):
    f = np.float32
    u0 = core * NCHUNK
    b = u0 // 256
    k0 = u0 % 256
    xc = np.asarray(x[b, 32 * k0: 32 * k0 + 32 * NCHUNK, :], f)
    flat = xc.reshape(NCHUNK, 65536)
    xD = np.ascontiguousarray(
        flat.reshape(NCHUNK, 512, 8, 16).transpose(0, 2, 3, 1).reshape(NCHUNK, 128, 512))
    xP2 = np.ascontiguousarray(
        flat.reshape(NCHUNK, 8, 512, 16).transpose(0, 1, 3, 2).reshape(NCHUNK, 128, 512))
    v = flat.reshape(NTILE, CHT, 2, NM, 128, 16)
    xTok = np.ascontiguousarray(
        v.transpose(0, 3, 4, 1, 2, 5).reshape(NTILE, NM, 128, TCOL))
    t_, chi_, j_, c_ = np.meshgrid(np.arange(NTILE), np.arange(CHT),
                                   np.arange(2), np.arange(16), indexing="ij")
    hidx = 512 * c_ + 2 * (k0 + 16 * t_ + chi_) + j_
    xr = np.asarray(x[b], f)[hidx.reshape(-1), :]
    xr = xr.reshape(NTILE, CHT, 2, 16, NM, 128)
    xRes = np.ascontiguousarray(xr.transpose(0, 4, 5, 1, 2, 3).reshape(NTILE, NM, 128, TCOL))
    return dict(xD=xD, xP2=xP2, xTok=xTok, xRes=xRes)


def _gather_out(outs):
    final = np.empty((BB, HH, WW), np.float32)
    for core in range(NCORES):
        u0 = core * NCHUNK
        b = u0 // 256
        k0 = u0 % 256
        o = np.asarray(outs[core], np.float32).reshape(NTILE, NM, 128, CHT, 2, 16)
        o = o.transpose(0, 3, 4, 5, 1, 2).reshape(4096, 2048)
        t_, chi_, j_, c_ = np.meshgrid(np.arange(NTILE), np.arange(CHT),
                                       np.arange(2), np.arange(16), indexing="ij")
        hidx = (512 * c_ + 2 * (k0 + 16 * t_ + chi_) + j_).reshape(-1)
        final[b, hidx, :] = o
    return final



ENGS = ("sync", "tensor", "vector", "scalar", "gpsimd")


class Prog:
    def __init__(self):
        self.ops = {e: [] for e in ENGS}

    def on(self, eng, fn):
        self.ops[eng].append(fn)


class Sem:
    def __init__(self, ctx, nc, name):
        self.h = ctx.enter_context(nc.semaphore(name))
        self.n = 0


class K:
    def __init__(self, nc, ctx, prog):
        self.nc, self.ctx, self.p = nc, ctx, prog
        self.sem = {}
        for name in ("ld", "ld2", "lat", "ldw", "ldu", "lda", "ten", "vec", "sca",
                     "gp", "dsync", "dsca", "dsc2"):
            self.sem[name] = Sem(ctx, nc, "s_" + name)

    def dma(self, qeng, out, in_, sem_name):
        s = self.sem[sem_name]
        s.n += 16
        self.p.on(qeng, lambda e, o=out, i=in_, h=s.h: e.dma_start(out=o, in_=i).then_inc(h, 16))

    def wait(self, eng, sem_name, val=None):
        s = self.sem[sem_name]
        v = s.n if val is None else val
        if v > 0:
            self.p.on(eng, lambda e, h=s.h, vv=v: e.wait_ge(h, vv))

    def spacer(self):
        pass

    def _mk(self, eng, sem_name):
        dr = eng in ("vector", "scalar")
        def em(fn, inc=False):
            if inc:
                s = self.sem[sem_name]
                s.n += 1
                self.p.on(eng, lambda e, h=s.h, f=fn: f(e).then_inc(h, 1))
            else:
                self.p.on(eng, fn)
            if dr:
                self.p.on(eng, lambda e: e.drain())
        return em

    def barrier(self):
        for e in ENGS:
            for nm in ("ten", "vec", "sca", "gp", "ld", "ld2", "lat", "ldw",
                       "ldu", "lda", "dsync", "dsca", "dsc2"):
                self.wait(e, nm)


class IL:

    def __init__(self, thunks, positions):
        assert len(positions) == len(thunks)
        self.items = list(zip(positions, thunks))
        self.i = 0
        self.t = -1

    def tick(self):
        self.t += 1
        while self.i < len(self.items) and self.items[self.i][0] <= self.t:
            self.items[self.i][1]()
            self.i += 1

    def flush(self):
        while self.i < len(self.items):
            self.items[self.i][1]()
            self.i += 1



def build_nc(dev=None):
    dev = dev or {}
    dv_tiles = dev.get("tiles", NTILE)
    dv_phases = dev.get("phases", ("attn", "z", "router", "b", "c"))
    dv_expose = dev.get("expose", ())
    dv_cht = dev.get("cht", CHT)
    nc = bass.Bass("TRN2", target_bir_lowering=False)
    ctx = ExitStack()
    prog = Prog()

    d_xD = nc.dram_tensor("xD", [NCHUNK, 128, 512], F32, kind="ExternalInput").ap()
    d_xP2 = nc.dram_tensor("xP2", [NCHUNK, 128, 512], F32, kind="ExternalInput").ap()
    d_xTok = nc.dram_tensor("xTok", [NTILE, NM, 128, TCOL], F32, kind="ExternalInput").ap()
    d_xRes = nc.dram_tensor("xRes", [NTILE, NM, 128, TCOL], F32, kind="ExternalInput").ap()
    d_wGU = nc.dram_tensor("wGU", [NKG, 128, 2, KC_GU, 128], F32, kind="ExternalInput").ap()
    d_wD = nc.dram_tensor("wD", [NM, 128, KC_D, 128], F32, kind="ExternalInput").ap()
    d_aAll = nc.dram_tensor("aAll", [128, AANW], F32, kind="ExternalInput").ap()
    d_aDn = nc.dram_tensor("aDn", [128, ADNWP], F32, kind="ExternalInput").ap()
    d_wbd1 = nc.dram_tensor("wbd1", [128, 192], F32, kind="ExternalInput").ap()
    d_wbd2 = nc.dram_tensor("wbd2", [128, 192], F32, kind="ExternalInput").ap()
    d_wpL = nc.dram_tensor("wpL", [64, 128], F32, kind="ExternalInput").ap()
    d_wpH = nc.dram_tensor("wpH", [64, 128], F32, kind="ExternalInput").ap()
    d_taps = nc.dram_tensor("taps", [8, 64], F32, kind="ExternalInput").ap()
    d_rows = nc.dram_tensor("rows", [3, TCOL], F32, kind="ExternalInput").ap()
    d_ebias = nc.dram_tensor("ebias", [1, 8], F32, kind="ExternalInput").ap()
    d_out = nc.dram_tensor("out", [NTILE, NM, 128, TCOL], F32, kind="ExternalOutput").ap()

    d_wGUh = nc.dram_tensor("wGUs_hi", [NKG, 128, 2, KC_GU, 128], BF16).ap()
    d_wGUl = nc.dram_tensor("wGUs_lo", [NKG, 128, 2, KC_GU, 128], BF16).ap()
    d_wDh = nc.dram_tensor("wDs_hi", [NM, 128, KC_D, 128], BF16).ap()
    d_wDl = nc.dram_tensor("wDs_lo", [NM, 128, KC_D, 128], BF16).ap()
    d_aAllh = nc.dram_tensor("aAll_hi", [128, AANW], BF16).ap()
    d_aAlll = nc.dram_tensor("aAll_lo", [128, AANW], BF16).ap()
    d_aDnh = nc.dram_tensor("aDn_hi", [128, ADNWP], BF16).ap()
    d_aDnl = nc.dram_tensor("aDn_lo", [128, ADNWP], BF16).ap()
    delkind = dict(kind="ExternalOutput") if "del" in dv_expose else {}
    d_del1 = nc.dram_tensor("del1", [2, NM, 128, TCOL], F32, **delkind).ap()
    d_del2 = nc.dram_tensor("del2", [2, NM, 128, TCOL], F32, **delkind).ap()
    d_dbgz = None
    if "z" in dv_expose:
        d_dbgz = nc.dram_tensor("dbgz", [2, NM, 128, TCOL], F32, kind="ExternalOutput").ap()
    d_dbgh = None
    if "h" in dv_expose:
        d_dbgh = nc.dram_tensor("dbgh", [4, 128, TCOL], F32, kind="ExternalOutput").ap()
    d_dbgw = None
    if "attnwork" in dv_expose:
        d_dbgw = nc.dram_tensor("dbgw", [6, 128, 768], F32, kind="ExternalOutput").ap()

    e = ctx.enter_context
    sb = {}

    def S(name, shape, dtype=F32):
        sb[name] = e(nc.sbuf_tensor(name, list(shape), dtype))
        return sb[name]

    ident = S("ident", (128, 128))
    onesrow = S("onesrow", (1, 128))
    onescol = S("onescol", (128, 1), BF16)
    tapT = S("tapT", (128, 8 * 64))
    gamT = S("gamT", (128, TCOL))
    betT = S("betT", (128, TCOL))
    biasT = S("biasT", (128, TCOL))
    ebT = S("ebT", (128, 8))
    wbd1S = S("wbd1S", (128, 192))
    wbd2S = S("wbd2S", (128, 192))
    wpLS = S("wpLS", (64, 128))
    wpHS = S("wpHS", (64, 128))

    zh = S("zh", (128, NM * TCOL), BF16)
    zl = S("zl", (128, NM * TCOL), BF16)
    hh = S("hh", (128, NKG * TCOL), BF16)
    hl = S("hl", (128, NKG * TCOL), BF16)
    ug_h = S("ug_h", (128, TCOL), BF16)
    ug_l = S("ug_l", (128, TCOL), BF16)
    uu_h = S("uu_h", (128, TCOL), BF16)
    uu_l = S("uu_l", (128, TCOL), BF16)
    ud_h = S("ud_h", (128, TCOL), BF16)
    ud_l = S("ud_l", (128, TCOL), BF16)
    wexp = S("wexp", (64, TCOL))

    wbuf = S("wbuf", (128, WBUF_W), BF16)
    aABuf = S("aABuf", (128, 2 * AACW), BF16)
    aDBuf = S("aDBuf", (128, 4 * 72), BF16)

    sbD = S("sbD", (128, 512))
    sbP2 = S("sbP2", (128, 512))
    sbQ1 = S("sbQ1", (128, 768))
    sbQ2 = S("sbQ2", (128, 768))
    sbS = S("sbS", (128, 256))
    sbPr = S("sbPr", (128, 256))
    sbM = S("sbM", (128, 32))
    sbZr = S("sbZr", (128, 32))
    sbAV = S("sbAV", (128, 256))
    sbL = S("sbL", (128, 256))
    sbT1 = S("sbT1", (128, 256))
    sbOT = S("sbOT", (64, 512))
    sbAtt = S("sbAtt", (128, 512))

    sbDel1 = S("sbDel1", (128, TCOL))
    sbDel2 = S("sbDel2", (128, TCOL))
    sbXT = S("sbXT", (128, TCOL))
    sbXR = S("sbXR", (128, TCOL))
    sbA = S("sbA", (128, TCOL))
    sbSq = S("sbSq", (128, TCOL))
    sbMu = S("sbMu", (128, 32))
    sbVar = S("sbVar", (128, 32))
    sbRs = S("sbRs", (128, 32))
    sbT2 = S("sbT2", (128, 32))
    sbZt = S("sbZt", (128, TCOL))
    sbBh = S("sbBh", (128, TCOL), BF16)
    sbBl = S("sbBl", (128, TCOL), BF16)
    sbSil = S("sbSil", (128, TCOL))
    sbY = S("sbY", (128, TCOL))
    sbRow = S("sbRow", (4, TCOL))
    sbSc = S("sbSc", (128, 32))
    sbSc2 = S("sbSc2", (128, 32))
    sbSc3 = S("sbSc3", (128, 32))
    sbSc4 = S("sbSc4", (128, 8))
    sbWt = S("sbWt", (128, 64))
    sbFm = S("sbFm", (8, TCOL))

    sbBh2 = S("sbBh2", (128, TCOL), BF16)
    sbBl2 = S("sbBl2", (128, TCOL), BF16)

    ps = {}

    def P(name):
        ps[name] = e(nc.psum_tensor(name, [128, 512], F32))
        return ps[name]
    pbG0, pbG1 = P("pbG0"), P("pbG1")
    pbU0, pbU1 = P("pbU0"), P("pbU1")
    pbA0, pbA1 = P("pbA0"), P("pbA1")
    pbB0, pbB1 = P("pbB0"), P("pbB1")

    fence_v = S("fence_v", (1, 2))
    fence_s = S("fence_s", (1, 2))
    blk = e(nc.Block())
    k = K(nc, ctx, prog)
    k.fence_v = fence_v
    k.fence_s = fence_s
    T = k._mk("tensor", "ten")
    V = k._mk("vector", "vec")
    C = k._mk("scalar", "sca")
    G = k._mk("gpsimd", "gp")

    def col(t, m, w=TCOL):
        return t[:, m * w:(m + 1) * w]

    def split(src_f32, dst_h, dst_l):
        V(lambda e, s=src_f32, d=dst_h: e.tensor_copy(d, s))
        k.spacer()
        V(lambda e, s=src_f32, dh=dst_h, dl=dst_l: e.scalar_tensor_tensor(
            dl, s, 1.0, dh, OP.mult, OP.subtract), inc=True)

    k.dma("sync", wbd1S[:, :], d_wbd1[:, :], "ld")
    k.dma("sync", wbd2S[:, :], d_wbd2[:, :], "ld")
    k.dma("sync", wpLS[:, :], d_wpL[:, :], "ld")
    k.dma("sync", wpHS[:, :], d_wpH[:, :], "ld")

    V(lambda e: e.memset(ident[:], 1.0))
    V(lambda e: e.memset(onesrow[:], 1.0))
    V(lambda e: e.memset(onescol[:], 1.0))
    for nm_ in ("ug_h", "ug_l", "uu_h", "uu_l", "ud_h", "ud_l"):
        V(lambda e, t=sb[nm_]: e.memset(t[:], 0.0))
    V(lambda e: e.memset(sb["fence_v"][:], 0.0))
    V(lambda e: e.memset(sb["fence_s"][:], 0.0))
    V(lambda e: e.memset(sbT1[:], 0.0), inc=True)
    k.wait("gpsimd", "vec")
    G(lambda e: e.affine_select(ident[:], ident[:], [[-1, 128]], OP.is_equal,
                                0.0, base=0, channel_multiplier=1), inc=True)
    srow = sbFm
    def bcast_row(d_src_row, width, dst_tile, dst_w):
        k.wait("sync", "ten")
        k.dma("sync", srow[0:1, 0:width], d_src_row, "ld")
        k.wait("tensor", "ld")
        k.wait("tensor", "vec")
        k.wait("tensor", "gp")
        k.wait("tensor", "sca")
        T(lambda e, w=width: e.matmul(pbB0[:, 0:w], onesrow[:], srow.ap()[0:1, 0:w],
                                      start=True, stop=True), inc=True)
        k.wait("scalar", "ten")
        C(lambda e, d=dst_tile, w=dst_w: e.copy(d, pbB0[:, 0:w]), inc=True)

    bcast_row(d_rows[0:1, :], TCOL, gamT[:, :], TCOL)
    bcast_row(d_rows[1:2, :], TCOL, betT[:, :], TCOL)
    bcast_row(d_rows[2:3, :], TCOL, biasT[:, :], TCOL)
    bcast_row(d_ebias[0:1, :], 8, ebT[:, :], 8)
    for ti in range(8):
        bcast_row(d_taps[ti:ti + 1, :], 64, tapT[:, 64 * ti:64 * ti + 64], 64)
    k.barrier()

    pp_state = {"vc": [0, 0], "dc": [0, 0], "lc": [0, 0]}

    def prepass(total, d_src, d_hi, d_lo, il=None):
        W = 512
        piece = 128 * W
        n = total // piece
        rem = total - n * piece
        assert rem % W == 0
        bufs = [(sbA, sbBh, sbBl), (sbSq, sbBh2, sbBl2)]
        ldsem = ("ld", "ld2")
        stsem = ("dsca", "dsc2")
        st = pp_state
        for i in range(n + (1 if rem else 0)):
            if il is not None:
                il.tick()
            lo = i * piece
            rows = 128 if i < n else rem // W
            p = i % 2
            bA, bh, bl = bufs[p]
            k.wait("sync", "vec", st["vc"][p])
            k.wait("sync", stsem[p], st["dc"][p])
            k.dma("sync", bA[0:rows, :], _ap(d_src, lo, [[W, rows], [1, W]]), ldsem[p])
            st["lc"][p] = k.sem[ldsem[p]].n
            k.wait("vector", ldsem[p], st["lc"][p])
            V(lambda e, bA=bA, bh=bh, r=rows: e.tensor_copy(bh[0:r, :], bA[0:r, :]))
            k.spacer()
            V(lambda e, bA=bA, bh=bh, bl=bl, r=rows: e.scalar_tensor_tensor(
                bl[0:r, :], bA[0:r, :], 1.0, bh[0:r, :], OP.mult, OP.subtract), inc=True)
            st["vc"][p] = k.sem["vec"].n
            k.wait("scalar", "vec", st["vc"][p])
            k.dma("scalar", _ap(d_hi, lo, [[W, rows], [1, W]]), bh[0:rows, :], stsem[p])
            k.dma("scalar", _ap(d_lo, lo, [[W, rows], [1, W]]), bl[0:rows, :], stsem[p])
            st["dc"][p] = k.sem[stsem[p]].n

    def attn_branch_vec(Q, tapbase, cbrow):
        pbufs = [sbAV[:, :], sbL[:, :], sbT1[:, :]]
        def pview(i):
            t = pbufs[i % 3]
            return bass.AP(t.tensor, t.offset, [t.ap[0], [64, 4], [8, 8], [1, 8]])
        for i in range(8):
            in0 = _ap(Q, 24 * i, [[768, 128], [192, 4], [0, 8], [1, 8]])
            in1 = _ap(Q, 8, [[768, 128], [192, 4], [24, 8], [1, 8]])
            V(lambda e, in0=in0, in1=in1, dv=pview(i): e.tensor_tensor(dv, in0, in1, OP.mult))
            if i == 0:
                k.spacer()
            dst = _ap(sbS, 8 * i, [[256, 128], [64, 4], [1, 8]])
            rv = pbufs[i % 3]
            rv = bass.AP(rv.tensor, rv.offset, [rv.ap[0], [64, 4], [8, 8], [1, 8]])
            V(lambda e, dst=dst, rv=rv: e.tensor_reduce(dst, rv, AX.X, OP.add))
        k.spacer()
        V(lambda e: e.tensor_reduce(sbM[:], sbS[:].rearrange("p (bi j) -> p bi j", j=8), AX.X, OP.max))
        k.spacer()
        mbc = _ap(sbM, 0, [[32, 128], [1, 32], [0, 8]])
        V(lambda e, mbc=mbc: e.tensor_tensor(
            sbPr[:].rearrange("p (bi j) -> p bi j", j=8),
            sbS[:].rearrange("p (bi j) -> p bi j", j=8), mbc, OP.subtract), inc=True)
        k.wait("scalar", "vec")
        C(lambda e: e.activation(sbPr[:], sbPr[:], AF.Exp, scale=SCALE_ATT), inc=True)
        k.wait("vector", "sca")
        V(lambda e: e.tensor_reduce(sbZr[:], sbPr[:].rearrange("p (bi j) -> p bi j", j=8), AX.X, OP.add))
        k.spacer()
        V(lambda e: e.reciprocal(sbZr[:], sbZr[:]))
        abufs = [sbS[:, :], sbL[:, :], sbT1[:, :]]
        for i in range(8):
            in0 = _ap(sbPr, 8 * i, [[256, 128], [64, 4], [1, 8], [0, 8]])
            in1 = _ap(Q, 16, [[768, 128], [192, 4], [24, 8], [1, 8]])
            t = abufs[i % 3]
            dstp = bass.AP(t.tensor, t.offset, [t.ap[0], [64, 4], [1, 8], [8, 8]])
            V(lambda e, in0=in0, in1=in1, dstp=dstp: e.tensor_tensor(dstp, in0, in1, OP.mult))
            if i == 0:
                k.spacer()
            dst = _ap(sbAV, 8 * i, [[256, 128], [64, 4], [1, 8]])
            rv = bass.AP(t.tensor, t.offset, [t.ap[0], [64, 4], [8, 8], [1, 8]])
            V(lambda e, dst=dst, rv=rv: e.tensor_reduce(dst, rv, AX.X, OP.add))
        Vall = _ap(Q, 16, [[768, 128], [192, 4], [24, 8], [1, 8]])
        Lv = _ap(sbL, 0, [[256, 128], [64, 4], [1, 64]])
        cb = _ap(tapT, 64 * cbrow, [[512, 128], [0, 4], [1, 64]])
        V(lambda e, Vall=Vall: e.tensor_tensor(
            _ap(sbL, 0, [[256, 128], [64, 4], [8, 8], [1, 8]]), Vall,
            _ap(tapT, 64 * (tapbase + 1), [[512, 128], [0, 4], [8, 8], [1, 8]]), OP.mult))
        vs1 = _ap(Q, 16 + 24, [[768, 128], [192, 4], [24, 7], [1, 8]])
        tp1 = _ap(tapT, 64 * (tapbase + 2), [[512, 128], [0, 4], [8, 7], [1, 8]])
        tslA = _ap(sbS, 0, [[256, 128], [64, 4], [8, 7], [1, 8]])
        V(lambda e, vs1=vs1, tp1=tp1, tslA=tslA: e.tensor_tensor(tslA, vs1, tp1, OP.mult))
        vsm = _ap(Q, 16, [[768, 128], [192, 4], [24, 7], [1, 8]])
        tm1 = _ap(tapT, 64 * (tapbase + 0), [[512, 128], [0, 4], [8, 7], [1, 8]])
        tslB = _ap(sbT1, 0, [[256, 128], [64, 4], [8, 7], [1, 8]])
        V(lambda e, vsm=vsm, tm1=tm1, tslB=tslB: e.tensor_tensor(tslB, vsm, tm1, OP.mult))
        dstL1 = _ap(sbL, 0, [[256, 128], [64, 4], [8, 7], [1, 8]])
        V(lambda e, dstL1=dstL1, tslA=tslA: e.tensor_tensor(dstL1, dstL1, tslA, OP.add))
        k.spacer()
        dstL2 = _ap(sbL, 8, [[256, 128], [64, 4], [8, 7], [1, 8]])
        V(lambda e, dstL2=dstL2, tslB=tslB: e.tensor_tensor(dstL2, dstL2, tslB, OP.add))
        k.spacer()
        V(lambda e, Lv=Lv, cb=cb: e.tensor_tensor(Lv, Lv, cb, OP.add))
        zbc = _ap(sbZr, 0, [[32, 128], [8, 4], [1, 8], [0, 8]])
        avv = sbAV[:].rearrange("p (b i c) -> p b i c", b=4, c=8)
        k.spacer()
        t1v = sbT1[:].rearrange("p (b i c) -> p b i c", b=4, c=8)
        V(lambda e, zbc=zbc, avv=avv, t1v=t1v: e.tensor_tensor(t1v, avv, zbc, OP.mult))
        k.spacer()
        V(lambda e: e.tensor_tensor(sbAV[:], sbT1[:], sbL[:], OP.add), inc=True)
        return k.sem["vec"].n

    def attn_branch_mat(wp, vpin, spin):
        otp = [0, 0, 0, 0]
        for b4 in range(4):
            pband = [pbA0, pbA1][b4 % 2]
            k.wait("tensor", "vec", vpin)
            k.wait("tensor", "sca", spin if b4 < 2 else otp[b4 - 2])
            T(lambda e, b4=b4, pband=pband: e.transpose(
                _ap(pband, 0, [[512, 64], [1, 128]]), sbAV[:, 64 * b4:64 * b4 + 64], ident[:]), inc=True)
            k.wait("scalar", "ten")
            C(lambda e, b4=b4, pband=pband: e.copy(
                sbOT[:, 128 * b4:128 * b4 + 128], _ap(pband, 0, [[512, 64], [1, 128]])), inc=True)
            otp[b4] = k.sem["sca"].n
        app = [0, 0, 0, 0]
        for b4 in range(4):
            pband = [pbA0, pbA1][b4 % 2]
            k.wait("tensor", "sca", otp[b4 + 2] if b4 < 2 else otp[b4])
            k.wait("tensor", "vec", vpin if b4 < 2 else app[b4 - 2])
            T(lambda e, b4=b4, pband=pband, wp=wp: e.matmul(
                pband[:, 0:128], sbOT[:, 128 * b4:128 * b4 + 128], wp[:], start=True, stop=True), inc=True)
            k.wait("vector", "ten")
            V(lambda e, b4=b4, pband=pband: e.tensor_copy(
                sbAtt[:, 128 * b4:128 * b4 + 128], pband[:, 0:128]), inc=True)
            app[b4] = k.sem["vec"].n

    def dump_att1(tp, chi):
        if "contig" in DBG_SKIP:
            dst = _ap(d_del1, tp * (NM * 128 * TCOL) + chi * 128 * TCOL, [[512, 128], [1, 512]])
            k.dma("sync", dst, sbAtt[:, :], "dsync")
            return
        for gB in range(4):
            j = gB // 2
            mbase = (8 * gB) % 16
            off = tp * (NM * 128 * TCOL) + mbase * (128 * TCOL) + 32 * chi + 16 * j
            dst = _ap(d_del1, off, [[4096, 128], [512, 8], [1, 16]])
            src = _ap(sbAtt, 128 * gB, [[512, 128], [16, 8], [1, 16]])
            k.dma("gpsimd", dst, src, "dsync")

    def dump_att2(tp, chi):
        if "contig" in DBG_SKIP:
            dst = _ap(d_del2, tp * (NM * 128 * TCOL) + chi * 128 * TCOL, [[512, 128], [1, 512]])
            k.dma("scalar", dst, sbAtt[:, :], "dsync")
            return
        for wgB in range(4):
            for j in range(2):
                off = tp * (NM * 128 * TCOL) + wgB * (128 * TCOL) + 32 * chi + 16 * j
                dst = _ap(d_del2, off, [[512, 128], [4 * 128 * TCOL, 4], [1, 16]])
                src = _ap(sbAtt, 128 * wgB + 64 * j, [[512, 128], [16, 4], [1, 16]])
                k.dma("gpsimd", dst, src, "dsync")

    ast = {"qkv": 0, "evac": 0, "v1": 0, "v2": 0, "sca1": 0}

    def attn_s1(tile, chi):
        cidx = CHT * tile + chi
        k.wait("gpsimd", "ten", ast["qkv"])
        k.dma("gpsimd", sbD[:, :], d_xD[cidx], "lat")
        k.dma("gpsimd", sbP2[:, :], d_xP2[cidx], "lat")
        k.wait("tensor", "lat")
        k.wait("tensor", "vec", ast["evac"])
        qkp = [0, 0, 0, 0]
        for b4 in range(4):
            pband = [pbA0, pbA1][b4 % 2]
            if b4 >= 2:
                k.wait("tensor", "sca", qkp[b4 - 2])
            T(lambda e, b4=b4, pband=pband: e.matmul(
                pband[:, 0:192], sbD[:, 128 * b4:128 * b4 + 128], wbd1S[:], start=True, stop=True), inc=True)
            k.wait("scalar", "ten")
            C(lambda e, b4=b4, pband=pband: e.copy(sbQ1[:, 192 * b4:192 * b4 + 192], pband[:, 0:192]), inc=True)
            qkp[b4] = k.sem["sca"].n
        for b4 in range(4):
            pband = [pbA0, pbA1][b4 % 2]
            k.wait("tensor", "sca", qkp[(b4 + 2) % 4])
            T(lambda e, b4=b4, pband=pband: e.matmul(
                pband[:, 0:192], sbP2[:, 128 * b4:128 * b4 + 128], wbd2S[:], start=True, stop=True), inc=True)
            k.wait("scalar", "ten")
            C(lambda e, b4=b4, pband=pband: e.copy(sbQ2[:, 192 * b4:192 * b4 + 192], pband[:, 0:192]), inc=True)
            qkp[b4] = k.sem["sca"].n
        ast["qkv"] = k.sem["ten"].n
        ast["sca1"] = k.sem["sca"].n
        k.wait("vector", "sca")
        ast["v1"] = attn_branch_vec(sbQ1, 0, 6)

    def attn_s2(tile, chi, tp):
        k.wait("vector", "dsync")
        attn_branch_mat(wpLS, ast["v1"], ast["sca1"])
        k.wait("gpsimd", "vec")
        dump_att1(tp, chi)
        ast["v2"] = attn_branch_vec(sbQ2, 3, 7)

    def attn_s3(tile, chi, tp):
        k.wait("vector", "dsync")
        attn_branch_mat(wpHS, ast["v2"], ast["sca1"])
        k.wait("gpsimd", "vec")
        dump_att2(tp, chi)
        ast["evac"] = k.sem["vec"].n

    def attn_chunk(tile, chi, tp):
        attn_s1(tile, chi)
        attn_s2(tile, chi, tp)
        attn_s3(tile, chi, tp)

    def z_phase(tile, tp):
        for m in range(NM):
            k.wait("sync", "vec")
            k.dma("sync", sbDel1[:, :], _ap(d_del1, (tp * NM + m) * 128 * TCOL, [[512, 128], [1, 512]]), "ld")
            k.dma("sync", sbDel2[:, :], _ap(d_del2, (tp * NM + m) * 128 * TCOL, [[512, 128], [1, 512]]), "ld")
            k.dma("sync", sbXT[:, :], d_xTok[tile, m], "ld")
            k.dma("sync", sbXR[:, :], d_xRes[tile, m], "ld")
            k.wait("vector", "ld")
            V(lambda e: e.tensor_tensor(sbA[:], sbDel1[:], sbDel2[:], OP.add))
            k.spacer()
            V(lambda e: e.tensor_tensor(sbZt[:], sbA[:], biasT[:], OP.add), inc=True)
            k.spacer()
            V(lambda e: e.tensor_reduce(sbMu[:], sbZt[:].rearrange("p (a c) -> p a c", c=16), AX.X, OP.add))
            k.spacer()
            V(lambda e: e.tensor_scalar(sbRs[:], sbMu[:], 1.0 / 16.0, None, OP.mult))
            k.wait("scalar", "vec")
            C(lambda e: e.activation(sbSq[:], sbZt[:], AF.Square), inc=True)
            k.wait("vector", "sca")
            V(lambda e: e.tensor_reduce(sbVar[:], sbSq[:].rearrange("p (a c) -> p a c", c=16), AX.X, OP.add))
            k.spacer()
            V(lambda e: e.tensor_scalar(sbT2[:], sbVar[:], 1.0 / 16.0, 1e-5, OP.mult, OP.add))
            V(lambda e: e.tensor_tensor(sbMu[:], sbRs[:], sbRs[:], OP.mult))
            k.spacer()
            V(lambda e: e.tensor_tensor(sbVar[:], sbT2[:], sbMu[:], OP.subtract), inc=True)
            k.wait("scalar", "vec")
            C(lambda e: e.activation(sbT2[:], sbVar[:], AF.Sqrt), inc=True)
            k.wait("vector", "sca")
            V(lambda e: e.reciprocal(sbVar[:], sbT2[:]))
            mubc = _ap(sbRs, 0, [[32, 128], [1, 32], [0, 16]])
            rsbc = _ap(sbVar, 0, [[32, 128], [1, 32], [0, 16]])
            zv = sbZt[:].rearrange("p (a c) -> p a c", c=16)
            avz = sbA[:].rearrange("p (a c) -> p a c", c=16)
            V(lambda e, mubc=mubc, zv=zv, avz=avz: e.tensor_tensor(avz, zv, mubc, OP.subtract))
            k.spacer()
            V(lambda e, rsbc=rsbc, zv=zv, avz=avz: e.tensor_tensor(zv, avz, rsbc, OP.mult))
            k.spacer()
            V(lambda e: e.tensor_tensor(sbA[:], sbZt[:], gamT[:], OP.mult))
            k.spacer()
            V(lambda e: e.tensor_tensor(sbZt[:], sbA[:], betT[:], OP.add))
            k.spacer()
            V(lambda e: e.tensor_tensor(sbA[:], sbZt[:], sbXT[:], OP.add))
            k.spacer()
            V(lambda e: e.tensor_tensor(sbZt[:], sbA[:], sbXR[:], OP.add))
            k.spacer()
            split(sbZt[:, :], col(zh, m), col(zl, m))
            k.wait("scalar", "vec")
            C(lambda e: e.activation(sbSq[:], sbZt[:], AF.Square), inc=True)
            k.wait("vector", "sca")
            k.wait("vector", "ten")
            split(sbSq[:, :], sbBh[:, :], sbBl[:, :])
            k.wait("tensor", "vec")
            T(lambda e, m=m: e.matmul(pbG0[0:1, :], onescol[:], col(zh, m), start=(m == 0), stop=False))
            T(lambda e, m=m: e.matmul(pbG0[0:1, :], onescol[:], col(zl, m), start=False, stop=(m == NM - 1)))
            T(lambda e, m=m: e.matmul(pbG1[0:1, :], onescol[:], sbBh[:], start=(m == 0), stop=False))
            T(lambda e, m=m: e.matmul(pbG1[0:1, :], onescol[:], sbBl[:], start=False, stop=(m == NM - 1)), inc=True)
        finalize_ln(pbG0, pbG1)
        k.wait("vector", "ten")
        for m in range(NM):
            V(lambda e, m=m: e.tensor_tensor(sbA[:], col(zh, m), col(zl, m), OP.add))
            k.spacer()
            V(lambda e: e.tensor_tensor(sbZt[:], sbA[:], pbB0[:, :], OP.subtract))
            k.spacer()
            V(lambda e: e.tensor_tensor(sbA[:], sbZt[:], pbB1[:, :], OP.mult))
            k.spacer()
            split(sbA[:, :], col(zh, m), col(zl, m))

    def finalize_ln(psA, psB):
        k.wait("vector", "ten")
        V(lambda e, psA=psA: e.tensor_scalar(sbRow[0:1, :], psA[0:1, :], 1.0 / 2048.0, None, OP.mult))
        V(lambda e, psB=psB: e.tensor_scalar(sbFm[0:1, :], psB[0:1, :], 1.0 / 2048.0, 1e-5, OP.mult, OP.add))
        V(lambda e: e.tensor_tensor(sbY[0:1, :], sbRow[0:1, :], sbRow[0:1, :], OP.mult))
        k.spacer()
        V(lambda e: e.tensor_tensor(sbSq[0:1, :], sbFm[0:1, :], sbY[0:1, :], OP.subtract), inc=True)
        k.wait("scalar", "vec")
        C(lambda e: e.activation(sbFm[0:1, :], sbSq[0:1, :], AF.Sqrt), inc=True)
        k.wait("vector", "sca")
        V(lambda e: e.reciprocal(sbY[0:1, :], sbFm[0:1, :]), inc=True)
        k.wait("tensor", "vec")
        T(lambda e: e.matmul(pbB0[:, :], onesrow[:], sbRow[0:1, :], start=True, stop=True))
        T(lambda e: e.matmul(pbB1[:, :], onesrow[:], sbY[0:1, :], start=True, stop=True), inc=True)

    def router(tile):
        k.wait("tensor", "vec")
        for kc in range(NM):
            aoff = 0
            k.wait("sync", "ten")
            k.dma("sync", _ap(aABuf, aoff, [[2 * AACW, 128], [1, AACW]]),
                  _ap(d_aAllh, kc * AACW, [[AANW, 128], [1, AACW]]), "ldw")
            k.dma("sync", _ap(aABuf, aoff + AACW, [[2 * AACW, 128], [1, AACW]]),
                  _ap(d_aAlll, kc * AACW, [[AANW, 128], [1, AACW]]), "ldw")
            k.wait("tensor", "ldw")
            for mb, pband, mw in ((0, pbA0, 128), (1, pbA1, 96)):
                Ah = _ap(aABuf, aoff + 128 * mb, [[2 * AACW, 128], [1, mw]])
                Al = _ap(aABuf, aoff + AACW + 128 * mb, [[2 * AACW, 128], [1, mw]])
                for pi, (lhs, rhs) in enumerate(
                        ((Ah, col(zh, kc)), (Al, col(zh, kc)), (Ah, col(zl, kc)))):
                    T(lambda e, pband=pband, lhs=lhs, rhs=rhs, mw=mw,
                        st=(kc == 0 and pi == 0), sp=(kc == NM - 1 and pi == 2):
                        e.matmul(pband[0:mw, :], lhs, rhs, start=st, stop=sp),
                        inc=(mb == 1 and pi == 2))
        k.wait("scalar", "ten")
        C(lambda e: e.activation(sbFm[:, :], pbA1[64:72, :], AF.Sigmoid), inc=True)
        k.wait("tensor", "sca")
        for b4 in range(4):
            k.wait("tensor", "vec")
            T(lambda e, b4=b4: e.transpose(
                _ap(pbB0, 0, [[512, 128], [1, 8]]), sbFm[:, 128 * b4:128 * b4 + 128], ident[0:8, 0:8]), inc=True)
            k.wait("vector", "ten")
            V(lambda e, b4=b4: e.tensor_copy(
                _ap(sbSc, 8 * b4, [[32, 128], [1, 8]]), _ap(pbB0, 0, [[512, 128], [1, 8]])), inc=True)
        def B4(i):
            return dict(
                sc=_ap(sbSc, 8 * i, [[32, 128], [1, 8]]),
                bi=_ap(sbSc2, 8 * i, [[32, 128], [1, 8]]),
                eq=_ap(sbSc3, 8 * i, [[32, 128], [1, 8]]),
                msk=_ap(sbT1, 8 * i, [[256, 128], [1, 8]]),
                m1=_ap(sbSc4, i, [[8, 128], [1, 1]]),
                m1b=_ap(sbSc4, i, [[8, 128], [0, 8]]),
                m2=_ap(sbT2, i, [[32, 128], [1, 1]]),
                m2b=_ap(sbT2, i, [[32, 128], [0, 8]]),
                den=_ap(sbMu, i, [[32, 128], [1, 1]]),
                den2=_ap(sbMu, 8 + i, [[32, 128], [1, 1]]),
                rec=_ap(sbMu, 16 + i, [[32, 128], [1, 1]]),
            )
        BB4 = [B4(i) for i in range(4)]
        for d in BB4:
            V(lambda e, d=d: e.tensor_tensor(d["bi"], d["sc"], ebT[:], OP.add))
        for d in BB4:
            V(lambda e, d=d: e.tensor_reduce(d["m1"], d["bi"], AX.X, OP.max))
        for d in BB4:
            V(lambda e, d=d: e.scalar_tensor_tensor(d["eq"], d["bi"], 1.0, d["m1b"], OP.mult, OP.is_equal))
        for d in BB4:
            V(lambda e, d=d: e.scalar_tensor_tensor(d["msk"], d["eq"], -2e30, d["bi"], OP.mult, OP.add))
        for d in BB4:
            V(lambda e, d=d: e.tensor_reduce(d["m2"], d["msk"], AX.X, OP.max))
        for d in BB4:
            V(lambda e, d=d: e.scalar_tensor_tensor(d["eq"], d["bi"], 1.0, d["m2b"], OP.mult, OP.is_ge))
        for d in BB4:
            V(lambda e, d=d: e.tensor_tensor(d["msk"], d["sc"], d["eq"], OP.mult))
        for d in BB4:
            V(lambda e, d=d: e.tensor_reduce(d["den"], d["msk"], AX.X, OP.add))
        for d in BB4:
            V(lambda e, d=d: e.tensor_scalar(d["den2"], d["den"], 1e-8, 8.0, OP.add, OP.mult))
        for d in BB4:
            V(lambda e, d=d: e.reciprocal(d["rec"], d["den2"]))
        for d in BB4:
            V(lambda e, d=d: e.tensor_scalar(d["eq"], d["msk"], d["rec"], None, OP.mult), inc=True)
        k.spacer()
        for b4 in range(4):
            src = _ap(sbSc3, 8 * b4, [[32, 128], [1, 8], [0, 8]])
            V(lambda e, src=src: e.tensor_copy(
                sbWt[:].rearrange("p (e r) -> p e r", r=8), src), inc=True)
            k.wait("tensor", "vec")
            T(lambda e, b4=b4: e.transpose(_ap(pbB0, 0, [[512, 64], [1, 128]]), sbWt[:, :], ident[:]), inc=True)
            k.wait("vector", "ten")
            V(lambda e, b4=b4: e.tensor_copy(wexp[:, 128 * b4:128 * b4 + 128],
                                             _ap(pbB0, 0, [[512, 64], [1, 128]])), inc=True)
        V(lambda e: e.tensor_tensor(sbZt[0:64, :], pbA0[0:64, :], wexp[:, :], OP.mult))
        V(lambda e: e.tensor_scalar(sbZt[64:72, :], pbA1[0:8, :], 0.125, None, OP.mult), inc=True)
        k.spacer()
        split(sbZt[0:72, :], ug_h[0:72, :], ug_l[0:72, :])
        V(lambda e: e.tensor_tensor(sbZt[0:64, :], pbA0[64:128, :], wexp[:, :], OP.mult))
        V(lambda e: e.tensor_scalar(sbZt[64:72, :], pbA1[32:40, :], 0.125, None, OP.mult), inc=True)
        k.spacer()
        split(sbZt[0:72, :], uu_h[0:72, :], uu_l[0:72, :])

    def phase_b(tile, il=None):
        GW = KC_GU * 128
        tb = {"g": [0, 0], "u": [0, 0]}
        pinb = {"vec": {}, "sca": {}}

        def adn_mms(kg):
            aoff = (kg % 2) * 144
            Ah = _ap(aDBuf, aoff, [[4 * 72, 128], [1, 72]])
            Al = _ap(aDBuf, aoff + 72, [[4 * 72, 128], [1, 72]])
            k.wait("tensor", "vec", pinb["vec"][kg])
            k.wait("tensor", "lda")
            T(lambda e, Ah=Ah, kg=kg, st=(kg == 0): e.matmul(
                pbB0[0:72, :], Ah, col(hh, kg), start=st, stop=False))
            T(lambda e, Al=Al, kg=kg: e.matmul(pbB0[0:72, :], Al, col(hh, kg), start=False, stop=False))
            T(lambda e, Ah=Ah, kg=kg, sp=(kg == NKG - 1): e.matmul(
                pbB0[0:72, :], Ah, col(hl, kg), start=False, stop=sp), inc=True)

        for kg in range(NKG):
            if il is not None:
                il.tick()
            if kg > 0:
                adn_mms(kg - 1)
            pg = [pbG0, pbG1][kg % 2]
            pu = [pbU0, pbU1][kg % 2]
            for gi, (gu, pband, semn) in enumerate((("g", pg, "ldw"), ("u", pu, "ldu"))):
                half = 0 if gu == "g" else 4352
                k.wait("sync", "ten", tb[gu][0])
                k.dma("sync", _ap(wbuf, half, [[WBUF_W, 128], [1, GW]]),
                      _ap(d_wGUh, (kg * 128 * 2 + (0 if gu == "g" else 1)) * GW, [[2 * GW, 128], [1, GW]]), semn)
                k.dma("sync", _ap(wbuf, half + GW, [[WBUF_W, 128], [1, GW]]),
                      _ap(d_wGUl, (kg * 128 * 2 + (0 if gu == "g" else 1)) * GW, [[2 * GW, 128], [1, GW]]), semn)
                k.wait("tensor", semn)
                k.wait("tensor", "vec", pinb["vec"].get(kg - 2, 0))
                k.wait("tensor", "sca", pinb["sca"].get(kg - 2, 0))
                nmm = 0
                for kc in range(KC_GU):
                    Wh = _ap(wbuf, half + kc * 128, [[WBUF_W, 128], [1, 128]])
                    Wl = _ap(wbuf, half + GW + kc * 128, [[WBUF_W, 128], [1, 128]])
                    if kc < NM:
                        rh, rl = col(zh, kc), col(zl, kc)
                    else:
                        rh, rl = (ug_h[:, :], ug_l[:, :]) if gu == "g" else (uu_h[:, :], uu_l[:, :])
                    for lhs, rhs in ((Wh, rh), (Wl, rh), (Wh, rl)):
                        sp = (nmm == 3 * KC_GU - 1)
                        T(lambda e, pband=pband, lhs=lhs, rhs=rhs,
                            st=(nmm == 0), sp=sp:
                            e.matmul(pband[:, :], lhs, rhs, start=st, stop=sp),
                            inc=sp)
                        nmm += 1
                tb[gu][0] = k.sem["ten"].n
            k.wait("scalar", "ten")
            C(lambda e, pg=pg: e.activation(sbSil[:], pg[:, :], AF.Silu), inc=True)
            pinb["sca"][kg] = k.sem["sca"].n
            k.wait("vector", "sca")
            k.wait("vector", "ten")
            V(lambda e, pu=pu: e.tensor_tensor(sbZt[:], sbSil[:], pu[:, :], OP.mult))
            k.spacer()
            split(sbZt[:, :], col(hh, kg), col(hl, kg))
            pinb["vec"][kg] = k.sem["vec"].n
            aoff = (kg % 2) * 144
            k.wait("sync", "lda")
            k.dma("sync", _ap(aDBuf, aoff, [[4 * 72, 128], [1, 72]]),
                  _ap(d_aDnh, kg * 72, [[ADNWP, 128], [1, 72]]), "lda")
            k.dma("sync", _ap(aDBuf, aoff + 72, [[4 * 72, 128], [1, 72]]),
                  _ap(d_aDnl, kg * 72, [[ADNWP, 128], [1, 72]]), "lda")
        adn_mms(NKG - 1)
        k.wait("vector", "ten")
        V(lambda e: e.tensor_tensor(sbZt[0:64, :], pbB0[0:64, :], wexp[:, :], OP.mult))
        V(lambda e: e.tensor_scalar(sbZt[64:72, :], pbB0[64:72, :], 0.125, None, OP.mult), inc=True)
        k.spacer()
        split(sbZt[0:72, :], ud_h[0:72, :], ud_l[0:72, :])

    def phase_c(tile, tp, il=None):
        KH = 22
        DH = KH * 128
        tc_ = {"A": 0, "B": 0}
        pinc = {}
        pstat = {}

        def stats_mms(m):
            k.wait("tensor", "vec", pstat[m])
            T(lambda e, m=m: e.matmul(pbU0[0:1, :], onescol[:], col(zh, m), start=(m == 0), stop=False))
            T(lambda e, m=m: e.matmul(pbU0[0:1, :], onescol[:], col(zl, m), start=False, stop=(m == NM - 1)))
            T(lambda e, m=m: e.matmul(pbU1[0:1, :], onescol[:], sbBh[:], start=(m == 0), stop=False))
            T(lambda e, m=m: e.matmul(pbU1[0:1, :], onescol[:], sbBl[:], start=False, stop=(m == NM - 1)), inc=True)

        for m in range(NM):
            if il is not None:
                il.tick()
            pband = [pbG0, pbG1][m % 2]
            k.wait("sync", "ten", tc_["A"])
            k.dma("sync", _ap(wbuf, 0, [[WBUF_W, 128], [1, DH]]),
                  _ap(d_wDh, m * 128 * D_STRIDE, [[D_STRIDE, 128], [1, DH]]), "ldw")
            k.dma("sync", _ap(wbuf, DH, [[WBUF_W, 128], [1, DH]]),
                  _ap(d_wDl, m * 128 * D_STRIDE, [[D_STRIDE, 128], [1, DH]]), "ldw")
            k.wait("sync", "ten", tc_["B"])
            k.dma("sync", _ap(wbuf, 2 * DH, [[WBUF_W, 128], [1, D_STRIDE - DH]]),
                  _ap(d_wDh, m * 128 * D_STRIDE + DH, [[D_STRIDE, 128], [1, D_STRIDE - DH]]), "ldu")
            k.dma("sync", _ap(wbuf, 2 * DH + (D_STRIDE - DH), [[WBUF_W, 128], [1, D_STRIDE - DH]]),
                  _ap(d_wDl, m * 128 * D_STRIDE + DH, [[D_STRIDE, 128], [1, D_STRIDE - DH]]), "ldu")
            k.wait("tensor", "ldw")
            k.wait("tensor", "vec", pinc.get(m - 2, 0))
            nmm = 0
            for kk in range(KC_D):
                if kk == KH:
                    k.wait("tensor", "ldu")
                if kk < KH:
                    Wh = _ap(wbuf, kk * 128, [[WBUF_W, 128], [1, 128]])
                    Wl = _ap(wbuf, DH + kk * 128, [[WBUF_W, 128], [1, 128]])
                else:
                    Wh = _ap(wbuf, 2 * DH + (kk - KH) * 128, [[WBUF_W, 128], [1, 128]])
                    Wl = _ap(wbuf, 2 * DH + (D_STRIDE - DH) + (kk - KH) * 128, [[WBUF_W, 128], [1, 128]])
                if kk < NKG:
                    rh, rl = col(hh, kk), col(hl, kk)
                else:
                    rh, rl = ud_h[:, :], ud_l[:, :]
                for lhs, rhs in ((Wh, rh), (Wl, rh), (Wh, rl)):
                    sp = (nmm == 3 * KC_D - 1)
                    spA = (nmm == 3 * KH - 1)
                    T(lambda e, pband=pband, lhs=lhs, rhs=rhs,
                        st=(nmm == 0), sp=sp:
                        e.matmul(pband[:, :], lhs, rhs, start=st, stop=sp),
                        inc=(sp or spA))
                    if spA:
                        tc_["A"] = k.sem["ten"].n
                    nmm += 1
            tc_["B"] = k.sem["ten"].n
            if m > 0:
                stats_mms(m - 1)
            k.wait("vector", "ten")
            V(lambda e, pband=pband, m=m: e.tensor_tensor(sbZt[:], pband[:, :], col(zh, m), OP.add))
            k.spacer()
            V(lambda e, m=m: e.tensor_tensor(sbA[:], sbZt[:], col(zl, m), OP.add), inc=True)
            pinc[m] = k.sem["vec"].n
            k.spacer()
            split(sbA[:, :], col(zh, m), col(zl, m))
            k.wait("scalar", "vec")
            C(lambda e: e.activation(sbSq[:], sbA[:], AF.Square), inc=True)
            k.wait("vector", "sca")
            k.wait("vector", "ten")
            split(sbSq[:, :], sbBh[:, :], sbBl[:, :])
            pstat[m] = k.sem["vec"].n
        stats_mms(NM - 1)
        finalize_ln(pbU0, pbU1)
        k.wait("vector", "ten")
        for m in range(NM):
            k.wait("vector", "dsca")
            V(lambda e, m=m: e.tensor_tensor(sbA[:], col(zh, m), col(zl, m), OP.add))
            k.spacer()
            V(lambda e: e.tensor_tensor(sbZt[:], sbA[:], pbB0[:, :], OP.subtract))
            k.spacer()
            V(lambda e: e.tensor_tensor(sbY[:], sbZt[:], pbB1[:, :], OP.mult), inc=True)
            k.wait("scalar", "vec")
            k.dma("scalar", d_out[tile, m], sbY[:, :], "dsca")

    dv_iltiles = dev.get("iltiles", NTILE)
    dv_il = dev.get("il", (dv_tiles == NTILE or "iltiles" in dev)
                    and dv_phases == ("attn", "z", "router", "b", "c")
                    and dv_cht == CHT and not dv_expose)

    def sub_thunks(tile):
        th = []
        for chi in range(CHT):
            th.append(lambda chi=chi, tile=tile: attn_s1(tile, chi))
            th.append(lambda chi=chi, tile=tile: attn_s2(tile, chi, tile % 2))
            th.append(lambda chi=chi, tile=tile: attn_s3(tile, chi, tile % 2))
        return th

    il0 = IL(sub_thunks(0), [10 * (i + 1) for i in range(3 * CHT)]) if dv_il else None
    prepass(NKG * 128 * GU_STRIDE, d_wGU, d_wGUh, d_wGUl, il=il0)
    prepass(NM * 128 * D_STRIDE, d_wD, d_wDh, d_wDl, il=il0)
    prepass(128 * AANW, d_aAll, d_aAllh, d_aAlll, il=il0)
    prepass(128 * ADNWP, d_aDn, d_aDnh, d_aDnl, il=il0)
    if il0 is not None:
        il0.flush()
    k.barrier()

    if dv_il:
        for tile in range(dv_iltiles):
            tp = tile % 2
            z_phase(tile, tp)
            k.barrier()
            router(tile)
            k.barrier()
            if tile + 1 < dv_iltiles:
                ilbc = IL(sub_thunks(tile + 1), [1 + (i * 57) // 47 for i in range(48)])
            else:
                ilbc = None
            phase_b(tile, il=ilbc)
            k.barrier()
            phase_c(tile, tp, il=ilbc)
            if ilbc is not None:
                ilbc.flush()
            k.barrier()

    for tile in (() if dv_il else range(dv_tiles)):
        tp = tile % 2
        if "attn" in dv_phases:
            for chi in range(dv_cht):
                attn_chunk(tile, chi, tp)
            k.barrier()
            if d_dbgw is not None:
                for di, src in ((0, sbQ1[:, :]), (1, sbQ2[:, :])):
                    k.dma("scalar", _ap(d_dbgw, di * 128 * 768, [[768, 128], [1, 768]]), src, "dsca")
                for di, src, wdt in ((2, sbZr[:, :], 32), (3, sbPr[:, :], 256),
                                     (4, sbAV[:, :], 256), (5, sbL[:, :], 256)):
                    k.dma("scalar", _ap(d_dbgw, di * 128 * 768, [[768, 128], [1, wdt]]), src, "dsca")
                k.barrier()
        if "z" in dv_phases:
            z_phase(tile, tp)
            k.barrier()
            if d_dbgz is not None:
                for m in range(NM):
                    V(lambda e, m=m: e.tensor_tensor(sbY[:], col(zh, m), col(zl, m), OP.add), inc=True)
                    k.wait("scalar", "vec")
                    k.dma("scalar", _ap(d_dbgz, (tp * NM + m) * 128 * TCOL, [[512, 128], [1, 512]]), sbY[:, :], "dsca")
                    k.wait("vector", "dsca")
                k.barrier()
        if "router" in dv_phases:
            router(tile)
            k.barrier()
        if "b" in dv_phases:
            phase_b(tile)
            k.barrier()
            if d_dbgh is not None:
                for kg in range(4):
                    V(lambda e, kg=kg: e.tensor_tensor(sbY[:], col(hh, kg), col(hl, kg), OP.add), inc=True)
                    k.wait("scalar", "vec")
                    k.dma("scalar", _ap(d_dbgh, kg * 128 * TCOL, [[512, 128], [1, 512]]), sbY[:, :], "dsca")
                    k.wait("vector", "dsca")
                k.barrier()
        if "c" in dv_phases:
            phase_c(tile, tp)
            k.barrier()

    k.wait("sync", "dsca")
    k.wait("sync", "dsync")

    @blk.sync
    def _(eng):
        for fn in prog.ops["sync"]:
            fn(eng)

    @blk.tensor
    def _(eng):
        for fn in prog.ops["tensor"]:
            fn(eng)

    @blk.vector
    def _(eng):
        for fn in prog.ops["vector"]:
            fn(eng)

    @blk.scalar
    def _(eng):
        for fn in prog.ops["scalar"]:
            fn(eng)

    @blk.gpsimd
    def _(eng):
        for fn in prog.ops["gpsimd"]:
            fn(eng)

    ctx.close()
    return nc


def kernel(**inputs):
    x = np.asarray(inputs["L0_attn_x"], np.float32)
    wmaps = _stage_weights(inputs)
    in_maps = []
    for core in range(NCORES):
        m = dict(wmaps)
        m.update(_stage_x(x, core))
        in_maps.append(m)
    nc = build_nc()
    res = run_bass_kernel_spmd(nc, in_maps, core_ids=list(range(NCORES)))
    outs = [r["out"] for r in res.results]
    return _gather_out(outs)


# version 09
# speedup vs baseline: 1.4738x; 1.0247x over previous
import numpy as np
import ml_dtypes
from contextlib import ExitStack

import concourse.bass as bass
import concourse.mybir as mybir
from concourse.bass_utils import run_bass_kernel_spmd

F32 = mybir.dt.float32
BF16 = mybir.dt.bfloat16
OP = mybir.AluOpType
AF = mybir.ActivationFunctionType
AX = mybir.AxisListType

NCORES = 8
BB, HH, WW = 4, 8192, 2048
NCHUNK = 128
NTILE = 8
CHT = 16
NM = 16
TCOL = 512
DFF = 5504
NKG = 43
KC_GU = 17
KC_D = 44
GUON = 2176
DON = 5632
SCALE_ATT = float(8.0 ** -0.5)
DBG_HOOK = []
DBG_SKIP = set()
AANW = NM * 224
AACW = 224
ADNW = NKG * 72
ADNWP = 3104
GU_STRIDE = 2 * KC_GU * 128
D_STRIDE = KC_D * 128
WBUF_W = 2 * D_STRIDE


def _ap(t, offset, dims):
    return bass.AP(t.tensor if hasattr(t, "tensor") else t, offset, dims)



def _stage_weights(I):
    f = np.float32
    W_gate = np.asarray(I["L0_ffn_W_gate"], f)
    W_up = np.asarray(I["L0_ffn_W_up"], f)
    W_down = np.asarray(I["L0_ffn_W_down"], f)
    sB_gate = np.asarray(I["L0_ffn_sB_gate"], f)
    sB_up = np.asarray(I["L0_ffn_sB_up"], f)
    sB_down = np.asarray(I["L0_ffn_sB_down"], f)
    B_gate = np.asarray(I["L0_ffn_B_gate"], f).reshape(64, DFF)
    B_up = np.asarray(I["L0_ffn_B_up"], f).reshape(64, DFF)
    B_down = np.asarray(I["L0_ffn_B_down"], f).reshape(64, WW)
    A_gate = np.asarray(I["L0_ffn_A_gate"], f).transpose(1, 0, 2).reshape(WW, 64)
    A_up = np.asarray(I["L0_ffn_A_up"], f).transpose(1, 0, 2).reshape(WW, 64)
    A_down = np.asarray(I["L0_ffn_A_down"], f).transpose(1, 0, 2).reshape(DFF, 64)
    sA_gate = np.asarray(I["L0_ffn_sA_gate"], f)
    sA_up = np.asarray(I["L0_ffn_sA_up"], f)
    sA_down = np.asarray(I["L0_ffn_sA_down"], f)
    W_router = np.asarray(I["L0_ffn_W_router"], f)

    zg = np.zeros((GUON - 2120, DFF), f)
    Wg_aug = np.concatenate([W_gate, B_gate, sB_gate, zg], 0)
    Wu_aug = np.concatenate([W_up, B_up, sB_up, zg], 0)
    zd = np.zeros((DON - 5576, WW), f)
    Wd_aug = np.concatenate([W_down, B_down, sB_down, zd], 0)

    def slabify(Waug):
        return Waug.reshape(KC_GU, 128, NKG, 128).transpose(2, 1, 0, 3)
    wGU = np.stack([slabify(Wg_aug), slabify(Wu_aug)], 2)
    wGU = np.ascontiguousarray(wGU, f)
    wD = np.ascontiguousarray(
        Wd_aug.reshape(KC_D, 128, NM, 128).transpose(2, 1, 0, 3), f)

    A_full = np.zeros((WW, AACW), f)
    A_full[:, 0:64] = A_gate
    A_full[:, 64:128] = A_up
    A_full[:, 128:136] = sA_gate
    A_full[:, 160:168] = sA_up
    A_full[:, 192:200] = W_router
    aAll = np.ascontiguousarray(
        A_full.reshape(NM, 128, AACW).transpose(1, 0, 2).reshape(128, AANW), f)
    A_dn_full = np.concatenate([A_down, sA_down], 1)
    aDn = A_dn_full.reshape(NKG, 128, 72).transpose(1, 0, 2).reshape(128, ADNW)
    aDn = np.concatenate([aDn, np.zeros((128, ADNWP - ADNW), f)], 1)
    aDn = np.ascontiguousarray(aDn, f)

    qkv_w = np.asarray(I["L0_attn_qkv_w"], f)
    proj_w = np.asarray(I["L0_attn_proj_w"], f)
    col3_lo = np.r_[0:8, 16:24, 32:40]
    col3_hi = np.r_[8:16, 24:32, 40:48]
    wbd1 = np.zeros((128, 192), f)
    wbd2 = np.zeros((128, 192), f)
    for s in range(8):
        wbd1[16 * s:16 * s + 16, 24 * s:24 * s + 24] = qkv_w[:, col3_lo]
        wbd2[16 * s:16 * s + 16, 24 * s:24 * s + 24] = qkv_w[:, col3_hi]
    wpL = np.zeros((64, 128), f)
    wpH = np.zeros((64, 128), f)
    for s in range(8):
        wpL[8 * s:8 * s + 8, 16 * s:16 * s + 16] = proj_w[0:8, :]
        wpH[8 * s:8 * s + 8, 16 * s:16 * s + 16] = proj_w[8:16, :]

    cw0 = np.asarray(I["L0_attn_conv_w0"], f)
    cb0 = np.asarray(I["L0_attn_conv_b0"], f)
    cw1 = np.asarray(I["L0_attn_conv_w1"], f)
    cb1 = np.asarray(I["L0_attn_conv_b1"], f)
    taps = np.zeros((8, 64), f)
    for s in range(8):
        for c in range(8):
            for ti, tau in enumerate((-1, 0, 1)):
                taps[ti, 8 * s + c] = cw0[c, 0, 1, tau + 1]
                taps[3 + ti, 8 * s + c] = cw1[c, 0, tau + 1, 1]
            taps[6, 8 * s + c] = cb0[c]
            taps[7, 8 * s + c] = cb1[c]

    ln_w = np.asarray(I["L0_attn_ln_w"], f)
    ln_b = np.asarray(I["L0_attn_ln_b"], f)
    proj_b = np.asarray(I["L0_attn_proj_b"], f)
    rows = np.zeros((3, TCOL), f)
    for chi in range(16):
        for j in range(2):
            for c in range(16):
                col = 32 * chi + 16 * j + c
                rows[0, col] = ln_w[c]
                rows[1, col] = ln_b[c]
                rows[2, col] = proj_b[c]
    ebias = np.asarray(I["L0_ffn_expert_bias"], f).reshape(1, 8)
    return dict(wGU=wGU, wD=wD, aAll=aAll, aDn=aDn, wbd1=wbd1, wbd2=wbd2,
                wpL=wpL, wpH=wpH, taps=taps, rows=rows, ebias=ebias)


def _stage_x(x, core):
    f = np.float32
    u0 = core * NCHUNK
    b = u0 // 256
    k0 = u0 % 256
    xc = np.asarray(x[b, 32 * k0: 32 * k0 + 32 * NCHUNK, :], f)
    flat = xc.reshape(NCHUNK, 65536)
    xD = np.ascontiguousarray(
        flat.reshape(NCHUNK, 512, 8, 16).transpose(0, 2, 3, 1).reshape(NCHUNK, 128, 512))
    xP2 = np.ascontiguousarray(
        flat.reshape(NCHUNK, 8, 512, 16).transpose(0, 1, 3, 2).reshape(NCHUNK, 128, 512))
    v = flat.reshape(NTILE, CHT, 2, NM, 128, 16)
    xTok = np.ascontiguousarray(
        v.transpose(0, 3, 4, 1, 2, 5).reshape(NTILE, NM, 128, TCOL))
    t_, chi_, j_, c_ = np.meshgrid(np.arange(NTILE), np.arange(CHT),
                                   np.arange(2), np.arange(16), indexing="ij")
    hidx = 512 * c_ + 2 * (k0 + 16 * t_ + chi_) + j_
    xr = np.asarray(x[b], f)[hidx.reshape(-1), :]
    xr = xr.reshape(NTILE, CHT, 2, 16, NM, 128)
    xRes = np.ascontiguousarray(xr.transpose(0, 4, 5, 1, 2, 3).reshape(NTILE, NM, 128, TCOL))
    return dict(xD=xD, xP2=xP2, xTok=xTok, xRes=xRes)


def _gather_out(outs):
    final = np.empty((BB, HH, WW), np.float32)
    for core in range(NCORES):
        u0 = core * NCHUNK
        b = u0 // 256
        k0 = u0 % 256
        o = np.asarray(outs[core], np.float32).reshape(NTILE, NM, 128, CHT, 2, 16)
        o = o.transpose(0, 3, 4, 5, 1, 2).reshape(4096, 2048)
        t_, chi_, j_, c_ = np.meshgrid(np.arange(NTILE), np.arange(CHT),
                                       np.arange(2), np.arange(16), indexing="ij")
        hidx = (512 * c_ + 2 * (k0 + 16 * t_ + chi_) + j_).reshape(-1)
        final[b, hidx, :] = o
    return final



ENGS = ("sync", "tensor", "vector", "scalar", "gpsimd")


class Prog:
    def __init__(self):
        self.ops = {e: [] for e in ENGS}

    def on(self, eng, fn):
        self.ops[eng].append(fn)


class Sem:
    def __init__(self, ctx, nc, name):
        self.h = ctx.enter_context(nc.semaphore(name))
        self.n = 0


class K:
    def __init__(self, nc, ctx, prog):
        self.nc, self.ctx, self.p = nc, ctx, prog
        self.sem = {}
        for name in ("ld", "ld2", "lat", "ldw", "ldu", "lda", "ten", "vec", "sca",
                     "gp", "dsync", "dsca", "dsc2"):
            self.sem[name] = Sem(ctx, nc, "s_" + name)

    def dma(self, qeng, out, in_, sem_name):
        s = self.sem[sem_name]
        s.n += 16
        self.p.on(qeng, lambda e, o=out, i=in_, h=s.h: e.dma_start(out=o, in_=i).then_inc(h, 16))

    def wait(self, eng, sem_name, val=None):
        s = self.sem[sem_name]
        v = s.n if val is None else val
        if v > 0:
            self.p.on(eng, lambda e, h=s.h, vv=v: e.wait_ge(h, vv))

    def spacer(self):
        pass

    def _mk(self, eng, sem_name):
        dr = eng in ("vector", "scalar")
        def em(fn, inc=False):
            if inc:
                s = self.sem[sem_name]
                s.n += 1
                self.p.on(eng, lambda e, h=s.h, f=fn: f(e).then_inc(h, 1))
            else:
                self.p.on(eng, fn)
            if dr:
                self.p.on(eng, lambda e: e.drain())
        return em

    def barrier(self):
        for e in ENGS:
            for nm in ("ten", "vec", "sca", "gp", "ld", "ld2", "lat", "ldw",
                       "ldu", "lda", "dsync", "dsca", "dsc2"):
                self.wait(e, nm)


class IL:

    def __init__(self, thunks, positions):
        assert len(positions) == len(thunks)
        self.items = list(zip(positions, thunks))
        self.i = 0
        self.t = -1

    def tick(self):
        self.t += 1
        while self.i < len(self.items) and self.items[self.i][0] <= self.t:
            self.items[self.i][1]()
            self.i += 1

    def flush(self):
        while self.i < len(self.items):
            self.items[self.i][1]()
            self.i += 1



def build_nc(dev=None):
    dev = dev or {}
    dv_tiles = dev.get("tiles", NTILE)
    dv_phases = dev.get("phases", ("attn", "z", "router", "b", "c"))
    dv_expose = dev.get("expose", ())
    dv_cht = dev.get("cht", CHT)
    nc = bass.Bass("TRN2", target_bir_lowering=False)
    ctx = ExitStack()
    prog = Prog()

    d_xD = nc.dram_tensor("xD", [NCHUNK, 128, 512], F32, kind="ExternalInput").ap()
    d_xP2 = nc.dram_tensor("xP2", [NCHUNK, 128, 512], F32, kind="ExternalInput").ap()
    d_xTok = nc.dram_tensor("xTok", [NTILE, NM, 128, TCOL], F32, kind="ExternalInput").ap()
    d_xRes = nc.dram_tensor("xRes", [NTILE, NM, 128, TCOL], F32, kind="ExternalInput").ap()
    d_wGU = nc.dram_tensor("wGU", [NKG, 128, 2, KC_GU, 128], F32, kind="ExternalInput").ap()
    d_wD = nc.dram_tensor("wD", [NM, 128, KC_D, 128], F32, kind="ExternalInput").ap()
    d_aAll = nc.dram_tensor("aAll", [128, AANW], F32, kind="ExternalInput").ap()
    d_aDn = nc.dram_tensor("aDn", [128, ADNWP], F32, kind="ExternalInput").ap()
    d_wbd1 = nc.dram_tensor("wbd1", [128, 192], F32, kind="ExternalInput").ap()
    d_wbd2 = nc.dram_tensor("wbd2", [128, 192], F32, kind="ExternalInput").ap()
    d_wpL = nc.dram_tensor("wpL", [64, 128], F32, kind="ExternalInput").ap()
    d_wpH = nc.dram_tensor("wpH", [64, 128], F32, kind="ExternalInput").ap()
    d_taps = nc.dram_tensor("taps", [8, 64], F32, kind="ExternalInput").ap()
    d_rows = nc.dram_tensor("rows", [3, TCOL], F32, kind="ExternalInput").ap()
    d_ebias = nc.dram_tensor("ebias", [1, 8], F32, kind="ExternalInput").ap()
    d_out = nc.dram_tensor("out", [NTILE, NM, 128, TCOL], F32, kind="ExternalOutput").ap()

    d_wGUh = nc.dram_tensor("wGUs_hi", [NKG, 128, 2, KC_GU, 128], BF16).ap()
    d_wGUl = nc.dram_tensor("wGUs_lo", [NKG, 128, 2, KC_GU, 128], BF16).ap()
    d_wDh = nc.dram_tensor("wDs_hi", [NM, 128, KC_D, 128], BF16).ap()
    d_wDl = nc.dram_tensor("wDs_lo", [NM, 128, KC_D, 128], BF16).ap()
    d_aAllh = nc.dram_tensor("aAll_hi", [128, AANW], BF16).ap()
    d_aAlll = nc.dram_tensor("aAll_lo", [128, AANW], BF16).ap()
    d_aDnh = nc.dram_tensor("aDn_hi", [128, ADNWP], BF16).ap()
    d_aDnl = nc.dram_tensor("aDn_lo", [128, ADNWP], BF16).ap()
    delkind = dict(kind="ExternalOutput") if "del" in dv_expose else {}
    d_del1 = nc.dram_tensor("del1", [2, NM, 128, TCOL], F32, **delkind).ap()
    d_del2 = nc.dram_tensor("del2", [2, NM, 128, TCOL], F32, **delkind).ap()
    d_dbgz = None
    if "z" in dv_expose:
        d_dbgz = nc.dram_tensor("dbgz", [2, NM, 128, TCOL], F32, kind="ExternalOutput").ap()
    d_dbgh = None
    if "h" in dv_expose:
        d_dbgh = nc.dram_tensor("dbgh", [4, 128, TCOL], F32, kind="ExternalOutput").ap()
    d_dbgw = None
    if "attnwork" in dv_expose:
        d_dbgw = nc.dram_tensor("dbgw", [6, 128, 768], F32, kind="ExternalOutput").ap()

    e = ctx.enter_context
    sb = {}

    def S(name, shape, dtype=F32):
        sb[name] = e(nc.sbuf_tensor(name, list(shape), dtype))
        return sb[name]

    ident = S("ident", (128, 128))
    onesrow = S("onesrow", (1, 128))
    onescol = S("onescol", (128, 1), BF16)
    tapT = S("tapT", (128, 8 * 64))
    gamT = S("gamT", (128, TCOL))
    betT = S("betT", (128, TCOL))
    biasT = S("biasT", (128, TCOL))
    ebT = S("ebT", (128, 8))
    wbd1S = S("wbd1S", (128, 192))
    wbd2S = S("wbd2S", (128, 192))
    wpLS = S("wpLS", (64, 128))
    wpHS = S("wpHS", (64, 128))

    zh = S("zh", (128, NM * TCOL), BF16)
    zl = S("zl", (128, NM * TCOL), BF16)
    hh = S("hh", (128, NKG * TCOL), BF16)
    hl = S("hl", (128, NKG * TCOL), BF16)
    ug_h = S("ug_h", (128, TCOL), BF16)
    ug_l = S("ug_l", (128, TCOL), BF16)
    uu_h = S("uu_h", (128, TCOL), BF16)
    uu_l = S("uu_l", (128, TCOL), BF16)
    ud_h = S("ud_h", (128, TCOL), BF16)
    ud_l = S("ud_l", (128, TCOL), BF16)
    wexp = S("wexp", (64, TCOL))

    wbuf = S("wbuf", (128, WBUF_W), BF16)
    aABuf = S("aABuf", (128, 2 * AACW), BF16)
    aDBuf = S("aDBuf", (128, 4 * 72), BF16)

    sbD = S("sbD", (128, 512))
    sbP2 = S("sbP2", (128, 512))
    sbQ1 = S("sbQ1", (128, 768))
    sbQ2 = S("sbQ2", (128, 768))
    sbS = S("sbS", (128, 256))
    sbPr = S("sbPr", (128, 256))
    sbM = S("sbM", (128, 32))
    sbZr = S("sbZr", (128, 32))
    sbAV = S("sbAV", (128, 256))
    sbL = S("sbL", (128, 256))
    sbT1 = S("sbT1", (128, 256))
    sbOT = S("sbOT", (64, 512))
    sbAtt = S("sbAtt", (128, 512))

    sbDel1 = S("sbDel1", (128, TCOL))
    sbDel2 = S("sbDel2", (128, TCOL))
    sbXT = S("sbXT", (128, TCOL))
    sbXR = S("sbXR", (128, TCOL))
    sbA = S("sbA", (128, TCOL))
    sbSq = S("sbSq", (128, TCOL))
    sbMu = S("sbMu", (128, 32))
    sbVar = S("sbVar", (128, 32))
    sbRs = S("sbRs", (128, 32))
    sbT2 = S("sbT2", (128, 32))
    sbZt = S("sbZt", (128, TCOL))
    sbBh = S("sbBh", (128, TCOL), BF16)
    sbBl = S("sbBl", (128, TCOL), BF16)
    sbSil = S("sbSil", (128, TCOL))
    sbY = S("sbY", (128, TCOL))
    sbRow = S("sbRow", (4, TCOL))
    sbSc = S("sbSc", (128, 32))
    sbSc2 = S("sbSc2", (128, 32))
    sbSc3 = S("sbSc3", (128, 32))
    sbSc4 = S("sbSc4", (128, 8))
    sbWt = S("sbWt", (128, 64))
    sbFm = S("sbFm", (8, TCOL))

    sbBh2 = S("sbBh2", (128, TCOL), BF16)
    sbBl2 = S("sbBl2", (128, TCOL), BF16)

    ps = {}

    def P(name):
        ps[name] = e(nc.psum_tensor(name, [128, 512], F32))
        return ps[name]
    pbG0, pbG1 = P("pbG0"), P("pbG1")
    pbU0, pbU1 = P("pbU0"), P("pbU1")
    pbA0, pbA1 = P("pbA0"), P("pbA1")
    pbB0, pbB1 = P("pbB0"), P("pbB1")

    fence_v = S("fence_v", (1, 2))
    fence_s = S("fence_s", (1, 2))
    blk = e(nc.Block())
    k = K(nc, ctx, prog)
    k.fence_v = fence_v
    k.fence_s = fence_s
    T = k._mk("tensor", "ten")
    V = k._mk("vector", "vec")
    C = k._mk("scalar", "sca")
    G = k._mk("gpsimd", "gp")

    def col(t, m, w=TCOL):
        return t[:, m * w:(m + 1) * w]

    def split(src_f32, dst_h, dst_l):
        V(lambda e, s=src_f32, d=dst_h: e.tensor_copy(d, s))
        k.spacer()
        V(lambda e, s=src_f32, dh=dst_h, dl=dst_l: e.scalar_tensor_tensor(
            dl, s, 1.0, dh, OP.mult, OP.subtract), inc=True)

    k.dma("sync", wbd1S[:, :], d_wbd1[:, :], "ld")
    k.dma("sync", wbd2S[:, :], d_wbd2[:, :], "ld")
    k.dma("sync", wpLS[:, :], d_wpL[:, :], "ld")
    k.dma("sync", wpHS[:, :], d_wpH[:, :], "ld")

    V(lambda e: e.memset(ident[:], 1.0))
    V(lambda e: e.memset(onesrow[:], 1.0))
    V(lambda e: e.memset(onescol[:], 1.0))
    for nm_ in ("ug_h", "ug_l", "uu_h", "uu_l", "ud_h", "ud_l"):
        V(lambda e, t=sb[nm_]: e.memset(t[:], 0.0))
    V(lambda e: e.memset(sb["fence_v"][:], 0.0))
    V(lambda e: e.memset(sb["fence_s"][:], 0.0))
    V(lambda e: e.memset(sbT1[:], 0.0), inc=True)
    k.wait("gpsimd", "vec")
    G(lambda e: e.affine_select(ident[:], ident[:], [[-1, 128]], OP.is_equal,
                                0.0, base=0, channel_multiplier=1), inc=True)
    srow = sbFm
    def bcast_row(d_src_row, width, dst_tile, dst_w):
        k.wait("sync", "ten")
        k.dma("sync", srow[0:1, 0:width], d_src_row, "ld")
        k.wait("tensor", "ld")
        k.wait("tensor", "vec")
        k.wait("tensor", "gp")
        k.wait("tensor", "sca")
        T(lambda e, w=width: e.matmul(pbB0[:, 0:w], onesrow[:], srow.ap()[0:1, 0:w],
                                      start=True, stop=True), inc=True)
        k.wait("scalar", "ten")
        C(lambda e, d=dst_tile, w=dst_w: e.copy(d, pbB0[:, 0:w]), inc=True)

    bcast_row(d_rows[0:1, :], TCOL, gamT[:, :], TCOL)
    bcast_row(d_rows[1:2, :], TCOL, betT[:, :], TCOL)
    bcast_row(d_rows[2:3, :], TCOL, biasT[:, :], TCOL)
    bcast_row(d_ebias[0:1, :], 8, ebT[:, :], 8)
    for ti in range(8):
        bcast_row(d_taps[ti:ti + 1, :], 64, tapT[:, 64 * ti:64 * ti + 64], 64)
    k.barrier()

    pp_state = {"vc": [0, 0], "dc": [0, 0], "dc2": [0, 0], "lc": [0, 0]}

    def prepass(total, d_src, d_hi, d_lo, il=None):
        W = 512
        piece = 128 * W
        n = total // piece
        rem = total - n * piece
        assert rem % W == 0
        bufs = [(sbA, sbBh, sbBl), (sbSq, sbBh2, sbBl2)]
        ldsem = ("ld", "ld2")
        stsem = ("dsca", "dsc2")
        stsem2 = stsem
        st = pp_state
        for i in range(n + (1 if rem else 0)):
            if il is not None:
                il.tick()
            lo = i * piece
            rows = 128 if i < n else rem // W
            p = i % 2
            bA, bh, bl = bufs[p]
            k.wait("sync", "vec", st["vc"][p])
            k.wait("sync", stsem[p], st["dc"][p])
            k.wait("sync", stsem2[p], st["dc2"][p])
            k.dma("sync", bA[0:rows, :], _ap(d_src, lo, [[W, rows], [1, W]]), ldsem[p])
            st["lc"][p] = k.sem[ldsem[p]].n
            k.wait("vector", ldsem[p], st["lc"][p])
            V(lambda e, bA=bA, bh=bh, r=rows: e.tensor_copy(bh[0:r, :], bA[0:r, :]))
            k.spacer()
            V(lambda e, bA=bA, bh=bh, bl=bl, r=rows: e.scalar_tensor_tensor(
                bl[0:r, :], bA[0:r, :], 1.0, bh[0:r, :], OP.mult, OP.subtract), inc=True)
            st["vc"][p] = k.sem["vec"].n
            k.wait("scalar", "vec", st["vc"][p])
            k.dma("scalar", _ap(d_hi, lo, [[W, rows], [1, W]]), bh[0:rows, :], stsem[p])
            k.wait("scalar", "vec", st["vc"][p])
            k.dma("scalar", _ap(d_lo, lo, [[W, rows], [1, W]]), bl[0:rows, :], stsem2[p])
            st["dc"][p] = k.sem[stsem[p]].n
            st["dc2"][p] = k.sem[stsem2[p]].n

    def attn_branch_vec(Q, tapbase, cbrow):
        pbufs = [sbAV[:, :], sbL[:, :], sbT1[:, :]]
        def pview(i):
            t = pbufs[i % 3]
            return bass.AP(t.tensor, t.offset, [t.ap[0], [64, 4], [8, 8], [1, 8]])
        for i in range(8):
            in0 = _ap(Q, 24 * i, [[768, 128], [192, 4], [0, 8], [1, 8]])
            in1 = _ap(Q, 8, [[768, 128], [192, 4], [24, 8], [1, 8]])
            V(lambda e, in0=in0, in1=in1, dv=pview(i): e.tensor_tensor(dv, in0, in1, OP.mult))
            if i == 0:
                k.spacer()
            dst = _ap(sbS, 8 * i, [[256, 128], [64, 4], [1, 8]])
            rv = pbufs[i % 3]
            rv = bass.AP(rv.tensor, rv.offset, [rv.ap[0], [64, 4], [8, 8], [1, 8]])
            V(lambda e, dst=dst, rv=rv: e.tensor_reduce(dst, rv, AX.X, OP.add))
        k.spacer()
        V(lambda e: e.tensor_reduce(sbM[:], sbS[:].rearrange("p (bi j) -> p bi j", j=8), AX.X, OP.max))
        k.spacer()
        mbc = _ap(sbM, 0, [[32, 128], [1, 32], [0, 8]])
        V(lambda e, mbc=mbc: e.tensor_tensor(
            sbPr[:].rearrange("p (bi j) -> p bi j", j=8),
            sbS[:].rearrange("p (bi j) -> p bi j", j=8), mbc, OP.subtract), inc=True)
        k.wait("scalar", "vec")
        C(lambda e: e.activation(sbPr[:], sbPr[:], AF.Exp, scale=SCALE_ATT), inc=True)
        k.wait("vector", "sca")
        V(lambda e: e.tensor_reduce(sbZr[:], sbPr[:].rearrange("p (bi j) -> p bi j", j=8), AX.X, OP.add))
        k.spacer()
        V(lambda e: e.reciprocal(sbZr[:], sbZr[:]))
        abufs = [sbS[:, :], sbL[:, :], sbT1[:, :]]
        for i in range(8):
            in0 = _ap(sbPr, 8 * i, [[256, 128], [64, 4], [1, 8], [0, 8]])
            in1 = _ap(Q, 16, [[768, 128], [192, 4], [24, 8], [1, 8]])
            t = abufs[i % 3]
            dstp = bass.AP(t.tensor, t.offset, [t.ap[0], [64, 4], [1, 8], [8, 8]])
            V(lambda e, in0=in0, in1=in1, dstp=dstp: e.tensor_tensor(dstp, in0, in1, OP.mult))
            if i == 0:
                k.spacer()
            dst = _ap(sbAV, 8 * i, [[256, 128], [64, 4], [1, 8]])
            rv = bass.AP(t.tensor, t.offset, [t.ap[0], [64, 4], [8, 8], [1, 8]])
            V(lambda e, dst=dst, rv=rv: e.tensor_reduce(dst, rv, AX.X, OP.add))
        Vall = _ap(Q, 16, [[768, 128], [192, 4], [24, 8], [1, 8]])
        Lv = _ap(sbL, 0, [[256, 128], [64, 4], [1, 64]])
        cb = _ap(tapT, 64 * cbrow, [[512, 128], [0, 4], [1, 64]])
        V(lambda e, Vall=Vall: e.tensor_tensor(
            _ap(sbL, 0, [[256, 128], [64, 4], [8, 8], [1, 8]]), Vall,
            _ap(tapT, 64 * (tapbase + 1), [[512, 128], [0, 4], [8, 8], [1, 8]]), OP.mult))
        vs1 = _ap(Q, 16 + 24, [[768, 128], [192, 4], [24, 7], [1, 8]])
        tp1 = _ap(tapT, 64 * (tapbase + 2), [[512, 128], [0, 4], [8, 7], [1, 8]])
        tslA = _ap(sbS, 0, [[256, 128], [64, 4], [8, 7], [1, 8]])
        V(lambda e, vs1=vs1, tp1=tp1, tslA=tslA: e.tensor_tensor(tslA, vs1, tp1, OP.mult))
        vsm = _ap(Q, 16, [[768, 128], [192, 4], [24, 7], [1, 8]])
        tm1 = _ap(tapT, 64 * (tapbase + 0), [[512, 128], [0, 4], [8, 7], [1, 8]])
        tslB = _ap(sbT1, 0, [[256, 128], [64, 4], [8, 7], [1, 8]])
        V(lambda e, vsm=vsm, tm1=tm1, tslB=tslB: e.tensor_tensor(tslB, vsm, tm1, OP.mult))
        dstL1 = _ap(sbL, 0, [[256, 128], [64, 4], [8, 7], [1, 8]])
        V(lambda e, dstL1=dstL1, tslA=tslA: e.tensor_tensor(dstL1, dstL1, tslA, OP.add))
        k.spacer()
        dstL2 = _ap(sbL, 8, [[256, 128], [64, 4], [8, 7], [1, 8]])
        V(lambda e, dstL2=dstL2, tslB=tslB: e.tensor_tensor(dstL2, dstL2, tslB, OP.add))
        k.spacer()
        V(lambda e, Lv=Lv, cb=cb: e.tensor_tensor(Lv, Lv, cb, OP.add))
        zbc = _ap(sbZr, 0, [[32, 128], [8, 4], [1, 8], [0, 8]])
        avv = sbAV[:].rearrange("p (b i c) -> p b i c", b=4, c=8)
        k.spacer()
        t1v = sbT1[:].rearrange("p (b i c) -> p b i c", b=4, c=8)
        V(lambda e, zbc=zbc, avv=avv, t1v=t1v: e.tensor_tensor(t1v, avv, zbc, OP.mult))
        k.spacer()
        V(lambda e: e.tensor_tensor(sbAV[:], sbT1[:], sbL[:], OP.add), inc=True)
        return k.sem["vec"].n

    def attn_branch_mat(wp, vpin, spin):
        otp = [0, 0, 0, 0]
        for b4 in range(4):
            i, bank, (ps, pv) = rot_next()
            k.wait("tensor", "vec", max(pv, vpin))
            k.wait("tensor", "sca", max(ps, spin))
            T(lambda e, b4=b4, bank=bank: e.transpose(
                _ap(bank, 0, [[512, 64], [1, 128]]), sbAV[:, 64 * b4:64 * b4 + 64], ident[:]), inc=True)
            k.wait("scalar", "ten")
            C(lambda e, b4=b4, bank=bank: e.copy(
                sbOT[:, 128 * b4:128 * b4 + 128], _ap(bank, 0, [[512, 64], [1, 128]])), inc=True)
            otp[b4] = k.sem["sca"].n
            rot["pins"][i][0] = otp[b4]
        for b4 in range(4):
            i, bank, (ps, pv) = rot_next()
            k.wait("tensor", "sca", max(ps, otp[b4]))
            k.wait("tensor", "vec", pv)
            T(lambda e, b4=b4, bank=bank, wp=wp: e.matmul(
                bank[:, 0:128], sbOT[:, 128 * b4:128 * b4 + 128], wp[:], start=True, stop=True), inc=True)
            k.wait("vector", "ten")
            V(lambda e, b4=b4, bank=bank: e.tensor_copy(
                sbAtt[:, 128 * b4:128 * b4 + 128], bank[:, 0:128]), inc=True)
            rot["pins"][i][1] = k.sem["vec"].n
        return k.sem["vec"].n

    def dump_att1(tp, chi):
        if "contig" in DBG_SKIP:
            dst = _ap(d_del1, tp * (NM * 128 * TCOL) + chi * 128 * TCOL, [[512, 128], [1, 512]])
            k.dma("sync", dst, sbAtt[:, :], "dsync")
            return
        for gB in range(4):
            j = gB // 2
            mbase = (8 * gB) % 16
            off = tp * (NM * 128 * TCOL) + mbase * (128 * TCOL) + 32 * chi + 16 * j
            dst = _ap(d_del1, off, [[4096, 128], [512, 8], [1, 16]])
            src = _ap(sbAtt, 128 * gB, [[512, 128], [16, 8], [1, 16]])
            k.dma("gpsimd", dst, src, "dsync")

    def dump_att2(tp, chi):
        if "contig" in DBG_SKIP:
            dst = _ap(d_del2, tp * (NM * 128 * TCOL) + chi * 128 * TCOL, [[512, 128], [1, 512]])
            k.dma("scalar", dst, sbAtt[:, :], "dsync")
            return
        for wgB in range(4):
            for j in range(2):
                off = tp * (NM * 128 * TCOL) + wgB * (128 * TCOL) + 32 * chi + 16 * j
                dst = _ap(d_del2, off, [[512, 128], [4 * 128 * TCOL, 4], [1, 16]])
                src = _ap(sbAtt, 128 * wgB + 64 * j, [[512, 128], [16, 4], [1, 16]])
                k.dma("gpsimd", dst, src, "dsync")

    ast = {"qkv": 0, "evac": 0, "v1": 0, "v2": 0, "sca1": 0, "d1": 0, "pend2": None}
    rot = {"banks": [pbA0, pbA1, pbB1], "pins": [[0, 0], [0, 0], [0, 0]], "r": 0}

    def rot_next():
        i = rot["r"] % 3
        rot["r"] += 1
        return i, rot["banks"][i], rot["pins"][i]

    def attn_s1(tile, chi):
        cidx = CHT * tile + chi
        attn_flush_pend()
        k.wait("gpsimd", "ten", ast["qkv"])
        k.dma("gpsimd", sbD[:, :], d_xD[cidx], "lat")
        k.dma("gpsimd", sbP2[:, :], d_xP2[cidx], "lat")
        k.wait("tensor", "lat")
        for src, Qd, wbd in ((sbD, sbQ1, wbd1S), (sbP2, sbQ2, wbd2S)):
            for b4 in range(4):
                i, bank, (ps, pv) = rot_next()
                k.wait("tensor", "sca", ps)
                k.wait("tensor", "vec", pv)
                T(lambda e, b4=b4, bank=bank, src=src, wbd=wbd: e.matmul(
                    bank[:, 0:192], src[:, 128 * b4:128 * b4 + 128], wbd[:], start=True, stop=True), inc=True)
                k.wait("scalar", "ten")
                C(lambda e, b4=b4, bank=bank, Qd=Qd: e.copy(
                    Qd[:, 192 * b4:192 * b4 + 192], bank[:, 0:192]), inc=True)
                rot["pins"][i][0] = k.sem["sca"].n
        ast["qkv"] = k.sem["ten"].n
        ast["sca1"] = k.sem["sca"].n
        k.wait("vector", "sca")
        ast["v1"] = attn_branch_vec(sbQ1, 0, 6)

    def attn_s2(tile, chi, tp):
        k.wait("vector", "dsync")
        ast["d1"] = attn_branch_mat(wpLS, ast["v1"], ast["sca1"])
        ast["v2"] = attn_branch_vec(sbQ2, 3, 7)

    def attn_s3(tile, chi, tp):
        k.wait("gpsimd", "vec", ast["d1"])
        dump_att1(tp, chi)
        k.wait("vector", "dsync")
        ast["evac"] = attn_branch_mat(wpHS, ast["v2"], ast["sca1"])
        ast["pend2"] = (tp, chi)

    def attn_flush_pend():
        if ast["pend2"] is not None:
            tp, chi = ast["pend2"]
            k.wait("gpsimd", "vec", ast["evac"])
            dump_att2(tp, chi)
            ast["pend2"] = None

    def attn_chunk(tile, chi, tp):
        attn_s1(tile, chi)
        attn_s2(tile, chi, tp)
        attn_s3(tile, chi, tp)

    def z_phase(tile, tp):
        for m in range(NM):
            k.wait("sync", "vec")
            k.dma("sync", sbDel1[:, :], _ap(d_del1, (tp * NM + m) * 128 * TCOL, [[512, 128], [1, 512]]), "ld")
            k.dma("sync", sbDel2[:, :], _ap(d_del2, (tp * NM + m) * 128 * TCOL, [[512, 128], [1, 512]]), "ld")
            k.dma("sync", sbXT[:, :], d_xTok[tile, m], "ld")
            k.dma("sync", sbXR[:, :], d_xRes[tile, m], "ld")
            k.wait("vector", "ld")
            V(lambda e: e.tensor_tensor(sbA[:], sbDel1[:], sbDel2[:], OP.add))
            k.spacer()
            V(lambda e: e.tensor_tensor(sbZt[:], sbA[:], biasT[:], OP.add), inc=True)
            k.spacer()
            V(lambda e: e.tensor_reduce(sbMu[:], sbZt[:].rearrange("p (a c) -> p a c", c=16), AX.X, OP.add))
            k.spacer()
            V(lambda e: e.tensor_scalar(sbRs[:], sbMu[:], 1.0 / 16.0, None, OP.mult))
            k.wait("scalar", "vec")
            C(lambda e: e.activation(sbSq[:], sbZt[:], AF.Square), inc=True)
            k.wait("vector", "sca")
            V(lambda e: e.tensor_reduce(sbVar[:], sbSq[:].rearrange("p (a c) -> p a c", c=16), AX.X, OP.add))
            k.spacer()
            V(lambda e: e.tensor_scalar(sbT2[:], sbVar[:], 1.0 / 16.0, 1e-5, OP.mult, OP.add))
            V(lambda e: e.tensor_tensor(sbMu[:], sbRs[:], sbRs[:], OP.mult))
            k.spacer()
            V(lambda e: e.tensor_tensor(sbVar[:], sbT2[:], sbMu[:], OP.subtract), inc=True)
            k.wait("scalar", "vec")
            C(lambda e: e.activation(sbT2[:], sbVar[:], AF.Sqrt), inc=True)
            k.wait("vector", "sca")
            V(lambda e: e.reciprocal(sbVar[:], sbT2[:]))
            mubc = _ap(sbRs, 0, [[32, 128], [1, 32], [0, 16]])
            rsbc = _ap(sbVar, 0, [[32, 128], [1, 32], [0, 16]])
            zv = sbZt[:].rearrange("p (a c) -> p a c", c=16)
            avz = sbA[:].rearrange("p (a c) -> p a c", c=16)
            V(lambda e, mubc=mubc, zv=zv, avz=avz: e.tensor_tensor(avz, zv, mubc, OP.subtract))
            k.spacer()
            V(lambda e, rsbc=rsbc, zv=zv, avz=avz: e.tensor_tensor(zv, avz, rsbc, OP.mult))
            k.spacer()
            V(lambda e: e.tensor_tensor(sbA[:], sbZt[:], gamT[:], OP.mult))
            k.spacer()
            V(lambda e: e.tensor_tensor(sbZt[:], sbA[:], betT[:], OP.add))
            k.spacer()
            V(lambda e: e.tensor_tensor(sbA[:], sbZt[:], sbXT[:], OP.add))
            k.spacer()
            V(lambda e: e.tensor_tensor(sbZt[:], sbA[:], sbXR[:], OP.add))
            k.spacer()
            split(sbZt[:, :], col(zh, m), col(zl, m))
            k.wait("scalar", "vec")
            C(lambda e: e.activation(sbSq[:], sbZt[:], AF.Square), inc=True)
            k.wait("vector", "sca")
            k.wait("vector", "ten")
            split(sbSq[:, :], sbBh[:, :], sbBl[:, :])
            k.wait("tensor", "vec")
            T(lambda e, m=m: e.matmul(pbG0[0:1, :], onescol[:], col(zh, m), start=(m == 0), stop=False))
            T(lambda e, m=m: e.matmul(pbG0[0:1, :], onescol[:], col(zl, m), start=False, stop=(m == NM - 1)))
            T(lambda e, m=m: e.matmul(pbG1[0:1, :], onescol[:], sbBh[:], start=(m == 0), stop=False))
            T(lambda e, m=m: e.matmul(pbG1[0:1, :], onescol[:], sbBl[:], start=False, stop=(m == NM - 1)), inc=True)
        finalize_ln(pbG0, pbG1)
        k.wait("vector", "ten")
        for m in range(NM):
            V(lambda e, m=m: e.tensor_tensor(sbA[:], col(zh, m), col(zl, m), OP.add))
            k.spacer()
            V(lambda e: e.tensor_tensor(sbZt[:], sbA[:], pbB0[:, :], OP.subtract))
            k.spacer()
            V(lambda e: e.tensor_tensor(sbA[:], sbZt[:], pbB1[:, :], OP.mult))
            k.spacer()
            split(sbA[:, :], col(zh, m), col(zl, m))

    def finalize_ln(psA, psB):
        k.wait("vector", "ten")
        V(lambda e, psA=psA: e.tensor_scalar(sbRow[0:1, :], psA[0:1, :], 1.0 / 2048.0, None, OP.mult))
        V(lambda e, psB=psB: e.tensor_scalar(sbFm[0:1, :], psB[0:1, :], 1.0 / 2048.0, 1e-5, OP.mult, OP.add))
        V(lambda e: e.tensor_tensor(sbY[0:1, :], sbRow[0:1, :], sbRow[0:1, :], OP.mult))
        k.spacer()
        V(lambda e: e.tensor_tensor(sbSq[0:1, :], sbFm[0:1, :], sbY[0:1, :], OP.subtract), inc=True)
        k.wait("scalar", "vec")
        C(lambda e: e.activation(sbFm[0:1, :], sbSq[0:1, :], AF.Sqrt), inc=True)
        k.wait("vector", "sca")
        V(lambda e: e.reciprocal(sbY[0:1, :], sbFm[0:1, :]), inc=True)
        k.wait("tensor", "vec")
        T(lambda e: e.matmul(pbB0[:, :], onesrow[:], sbRow[0:1, :], start=True, stop=True))
        T(lambda e: e.matmul(pbB1[:, :], onesrow[:], sbY[0:1, :], start=True, stop=True), inc=True)

    def router(tile):
        k.wait("tensor", "vec")
        for kc in range(NM):
            aoff = 0
            k.wait("sync", "ten")
            k.dma("sync", _ap(aABuf, aoff, [[2 * AACW, 128], [1, AACW]]),
                  _ap(d_aAllh, kc * AACW, [[AANW, 128], [1, AACW]]), "ldw")
            k.dma("sync", _ap(aABuf, aoff + AACW, [[2 * AACW, 128], [1, AACW]]),
                  _ap(d_aAlll, kc * AACW, [[AANW, 128], [1, AACW]]), "ldw")
            k.wait("tensor", "ldw")
            for mb, pband, mw in ((0, pbA0, 128), (1, pbA1, 96)):
                Ah = _ap(aABuf, aoff + 128 * mb, [[2 * AACW, 128], [1, mw]])
                Al = _ap(aABuf, aoff + AACW + 128 * mb, [[2 * AACW, 128], [1, mw]])
                for pi, (lhs, rhs) in enumerate(
                        ((Ah, col(zh, kc)), (Al, col(zh, kc)), (Ah, col(zl, kc)))):
                    T(lambda e, pband=pband, lhs=lhs, rhs=rhs, mw=mw,
                        st=(kc == 0 and pi == 0), sp=(kc == NM - 1 and pi == 2):
                        e.matmul(pband[0:mw, :], lhs, rhs, start=st, stop=sp),
                        inc=(mb == 1 and pi == 2))
        k.wait("scalar", "ten")
        C(lambda e: e.activation(sbFm[:, :], pbA1[64:72, :], AF.Sigmoid), inc=True)
        k.wait("tensor", "sca")
        for b4 in range(4):
            k.wait("tensor", "vec")
            T(lambda e, b4=b4: e.transpose(
                _ap(pbB0, 0, [[512, 128], [1, 8]]), sbFm[:, 128 * b4:128 * b4 + 128], ident[0:8, 0:8]), inc=True)
            k.wait("vector", "ten")
            V(lambda e, b4=b4: e.tensor_copy(
                _ap(sbSc, 8 * b4, [[32, 128], [1, 8]]), _ap(pbB0, 0, [[512, 128], [1, 8]])), inc=True)
        def B4(i):
            return dict(
                sc=_ap(sbSc, 8 * i, [[32, 128], [1, 8]]),
                bi=_ap(sbSc2, 8 * i, [[32, 128], [1, 8]]),
                eq=_ap(sbSc3, 8 * i, [[32, 128], [1, 8]]),
                msk=_ap(sbT1, 8 * i, [[256, 128], [1, 8]]),
                m1=_ap(sbSc4, i, [[8, 128], [1, 1]]),
                m1b=_ap(sbSc4, i, [[8, 128], [0, 8]]),
                m2=_ap(sbT2, i, [[32, 128], [1, 1]]),
                m2b=_ap(sbT2, i, [[32, 128], [0, 8]]),
                den=_ap(sbMu, i, [[32, 128], [1, 1]]),
                den2=_ap(sbMu, 8 + i, [[32, 128], [1, 1]]),
                rec=_ap(sbMu, 16 + i, [[32, 128], [1, 1]]),
            )
        BB4 = [B4(i) for i in range(4)]
        for d in BB4:
            V(lambda e, d=d: e.tensor_tensor(d["bi"], d["sc"], ebT[:], OP.add))
        for d in BB4:
            V(lambda e, d=d: e.tensor_reduce(d["m1"], d["bi"], AX.X, OP.max))
        for d in BB4:
            V(lambda e, d=d: e.scalar_tensor_tensor(d["eq"], d["bi"], 1.0, d["m1b"], OP.mult, OP.is_equal))
        for d in BB4:
            V(lambda e, d=d: e.scalar_tensor_tensor(d["msk"], d["eq"], -2e30, d["bi"], OP.mult, OP.add))
        for d in BB4:
            V(lambda e, d=d: e.tensor_reduce(d["m2"], d["msk"], AX.X, OP.max))
        for d in BB4:
            V(lambda e, d=d: e.scalar_tensor_tensor(d["eq"], d["bi"], 1.0, d["m2b"], OP.mult, OP.is_ge))
        for d in BB4:
            V(lambda e, d=d: e.tensor_tensor(d["msk"], d["sc"], d["eq"], OP.mult))
        for d in BB4:
            V(lambda e, d=d: e.tensor_reduce(d["den"], d["msk"], AX.X, OP.add))
        for d in BB4:
            V(lambda e, d=d: e.tensor_scalar(d["den2"], d["den"], 1e-8, 8.0, OP.add, OP.mult))
        for d in BB4:
            V(lambda e, d=d: e.reciprocal(d["rec"], d["den2"]))
        for d in BB4:
            V(lambda e, d=d: e.tensor_scalar(d["eq"], d["msk"], d["rec"], None, OP.mult), inc=True)
        k.spacer()
        for b4 in range(4):
            src = _ap(sbSc3, 8 * b4, [[32, 128], [1, 8], [0, 8]])
            V(lambda e, src=src: e.tensor_copy(
                sbWt[:].rearrange("p (e r) -> p e r", r=8), src), inc=True)
            k.wait("tensor", "vec")
            T(lambda e, b4=b4: e.transpose(_ap(pbB0, 0, [[512, 64], [1, 128]]), sbWt[:, :], ident[:]), inc=True)
            k.wait("vector", "ten")
            V(lambda e, b4=b4: e.tensor_copy(wexp[:, 128 * b4:128 * b4 + 128],
                                             _ap(pbB0, 0, [[512, 64], [1, 128]])), inc=True)
        V(lambda e: e.tensor_tensor(sbZt[0:64, :], pbA0[0:64, :], wexp[:, :], OP.mult))
        V(lambda e: e.tensor_scalar(sbZt[64:72, :], pbA1[0:8, :], 0.125, None, OP.mult), inc=True)
        k.spacer()
        split(sbZt[0:72, :], ug_h[0:72, :], ug_l[0:72, :])
        V(lambda e: e.tensor_tensor(sbZt[0:64, :], pbA0[64:128, :], wexp[:, :], OP.mult))
        V(lambda e: e.tensor_scalar(sbZt[64:72, :], pbA1[32:40, :], 0.125, None, OP.mult), inc=True)
        k.spacer()
        split(sbZt[0:72, :], uu_h[0:72, :], uu_l[0:72, :])

    def phase_b(tile, il=None):
        GW = KC_GU * 128
        tb = {"g": [0, 0], "u": [0, 0]}
        pinb = {"vec": {}, "sca": {}}

        def adn_mms(kg):
            aoff = (kg % 2) * 144
            Ah = _ap(aDBuf, aoff, [[4 * 72, 128], [1, 72]])
            Al = _ap(aDBuf, aoff + 72, [[4 * 72, 128], [1, 72]])
            k.wait("tensor", "vec", pinb["vec"][kg])
            k.wait("tensor", "lda")
            T(lambda e, Ah=Ah, kg=kg, st=(kg == 0): e.matmul(
                pbB0[0:72, :], Ah, col(hh, kg), start=st, stop=False))
            T(lambda e, Al=Al, kg=kg: e.matmul(pbB0[0:72, :], Al, col(hh, kg), start=False, stop=False))
            T(lambda e, Ah=Ah, kg=kg, sp=(kg == NKG - 1): e.matmul(
                pbB0[0:72, :], Ah, col(hl, kg), start=False, stop=sp), inc=True)
            pa[kg] = k.sem["ten"].n

        pa = {}
        for kg in range(NKG):
            if il is not None:
                il.tick()
            if kg > 1:
                adn_mms(kg - 2)
            pg = [pbG0, pbG1][kg % 2]
            pu = [pbU0, pbU1][kg % 2]
            for gi, (gu, pband, semn) in enumerate((("g", pg, "ldw"), ("u", pu, "ldu"))):
                half = 0 if gu == "g" else 4352
                k.wait("sync", "ten", tb[gu][0])
                k.dma("sync", _ap(wbuf, half, [[WBUF_W, 128], [1, GW]]),
                      _ap(d_wGUh, (kg * 128 * 2 + (0 if gu == "g" else 1)) * GW, [[2 * GW, 128], [1, GW]]), semn)
                k.dma("sync", _ap(wbuf, half + GW, [[WBUF_W, 128], [1, GW]]),
                      _ap(d_wGUl, (kg * 128 * 2 + (0 if gu == "g" else 1)) * GW, [[2 * GW, 128], [1, GW]]), semn)
                k.wait("tensor", semn)
                k.wait("tensor", "vec", pinb["vec"].get(kg - 2, 0))
                k.wait("tensor", "sca", pinb["sca"].get(kg - 2, 0))
                nmm = 0
                for kc in range(KC_GU):
                    Wh = _ap(wbuf, half + kc * 128, [[WBUF_W, 128], [1, 128]])
                    Wl = _ap(wbuf, half + GW + kc * 128, [[WBUF_W, 128], [1, 128]])
                    if kc < NM:
                        rh, rl = col(zh, kc), col(zl, kc)
                    else:
                        rh, rl = (ug_h[:, :], ug_l[:, :]) if gu == "g" else (uu_h[:, :], uu_l[:, :])
                    for lhs, rhs in ((Wh, rh), (Wl, rh), (Wh, rl)):
                        sp = (nmm == 3 * KC_GU - 1)
                        T(lambda e, pband=pband, lhs=lhs, rhs=rhs,
                            st=(nmm == 0), sp=sp:
                            e.matmul(pband[:, :], lhs, rhs, start=st, stop=sp),
                            inc=sp)
                        nmm += 1
                tb[gu][0] = k.sem["ten"].n
            k.wait("scalar", "ten")
            k.wait("scalar", "vec", pinb["vec"].get(kg - 1, 0))
            C(lambda e, pg=pg: e.activation(sbSil[:], pg[:, :], AF.Silu), inc=True)
            pinb["sca"][kg] = k.sem["sca"].n
            k.wait("vector", "sca")
            k.wait("vector", "ten")
            V(lambda e, pu=pu: e.tensor_tensor(sbZt[:], sbSil[:], pu[:, :], OP.mult))
            k.spacer()
            split(sbZt[:, :], col(hh, kg), col(hl, kg))
            pinb["vec"][kg] = k.sem["vec"].n
            aoff = (kg % 2) * 144
            k.wait("sync", "lda")
            k.wait("sync", "ten", pa.get(kg - 2, 0))
            k.dma("sync", _ap(aDBuf, aoff, [[4 * 72, 128], [1, 72]]),
                  _ap(d_aDnh, kg * 72, [[ADNWP, 128], [1, 72]]), "lda")
            k.dma("sync", _ap(aDBuf, aoff + 72, [[4 * 72, 128], [1, 72]]),
                  _ap(d_aDnl, kg * 72, [[ADNWP, 128], [1, 72]]), "lda")
        adn_mms(NKG - 2)
        adn_mms(NKG - 1)
        k.wait("vector", "ten")
        V(lambda e: e.tensor_tensor(sbZt[0:64, :], pbB0[0:64, :], wexp[:, :], OP.mult))
        V(lambda e: e.tensor_scalar(sbZt[64:72, :], pbB0[64:72, :], 0.125, None, OP.mult), inc=True)
        k.spacer()
        split(sbZt[0:72, :], ud_h[0:72, :], ud_l[0:72, :])

    def phase_c(tile, tp, il=None):
        KH = 22
        DH = KH * 128
        tc_ = {"A": 0, "B": 0}
        pinc = {}
        pstat = {}

        def stats_mms(m):
            k.wait("tensor", "vec", pstat[m])
            T(lambda e, m=m: e.matmul(pbU0[0:1, :], onescol[:], col(zh, m), start=(m == 0), stop=False))
            T(lambda e, m=m: e.matmul(pbU0[0:1, :], onescol[:], col(zl, m), start=False, stop=(m == NM - 1)))
            T(lambda e, m=m: e.matmul(pbU1[0:1, :], onescol[:], sbBh[:], start=(m == 0), stop=False))
            T(lambda e, m=m: e.matmul(pbU1[0:1, :], onescol[:], sbBl[:], start=False, stop=(m == NM - 1)), inc=True)

        for m in range(NM):
            if il is not None:
                il.tick()
            pband = [pbG0, pbG1][m % 2]
            k.wait("sync", "ten", tc_["A"])
            k.dma("sync", _ap(wbuf, 0, [[WBUF_W, 128], [1, DH]]),
                  _ap(d_wDh, m * 128 * D_STRIDE, [[D_STRIDE, 128], [1, DH]]), "ldw")
            k.dma("sync", _ap(wbuf, DH, [[WBUF_W, 128], [1, DH]]),
                  _ap(d_wDl, m * 128 * D_STRIDE, [[D_STRIDE, 128], [1, DH]]), "ldw")
            k.wait("sync", "ten", tc_["B"])
            k.dma("sync", _ap(wbuf, 2 * DH, [[WBUF_W, 128], [1, D_STRIDE - DH]]),
                  _ap(d_wDh, m * 128 * D_STRIDE + DH, [[D_STRIDE, 128], [1, D_STRIDE - DH]]), "ldu")
            k.dma("sync", _ap(wbuf, 2 * DH + (D_STRIDE - DH), [[WBUF_W, 128], [1, D_STRIDE - DH]]),
                  _ap(d_wDl, m * 128 * D_STRIDE + DH, [[D_STRIDE, 128], [1, D_STRIDE - DH]]), "ldu")
            k.wait("tensor", "ldw")
            k.wait("tensor", "vec", pinc.get(m - 2, 0))
            nmm = 0
            for kk in range(KC_D):
                if kk == KH:
                    k.wait("tensor", "ldu")
                if kk < KH:
                    Wh = _ap(wbuf, kk * 128, [[WBUF_W, 128], [1, 128]])
                    Wl = _ap(wbuf, DH + kk * 128, [[WBUF_W, 128], [1, 128]])
                else:
                    Wh = _ap(wbuf, 2 * DH + (kk - KH) * 128, [[WBUF_W, 128], [1, 128]])
                    Wl = _ap(wbuf, 2 * DH + (D_STRIDE - DH) + (kk - KH) * 128, [[WBUF_W, 128], [1, 128]])
                if kk < NKG:
                    rh, rl = col(hh, kk), col(hl, kk)
                else:
                    rh, rl = ud_h[:, :], ud_l[:, :]
                for lhs, rhs in ((Wh, rh), (Wl, rh), (Wh, rl)):
                    sp = (nmm == 3 * KC_D - 1)
                    spA = (nmm == 3 * KH - 1)
                    T(lambda e, pband=pband, lhs=lhs, rhs=rhs,
                        st=(nmm == 0), sp=sp:
                        e.matmul(pband[:, :], lhs, rhs, start=st, stop=sp),
                        inc=(sp or spA))
                    if spA:
                        tc_["A"] = k.sem["ten"].n
                    nmm += 1
            tc_["B"] = k.sem["ten"].n
            if m > 0:
                stats_mms(m - 1)
            k.wait("vector", "ten")
            V(lambda e, pband=pband, m=m: e.tensor_tensor(sbZt[:], pband[:, :], col(zh, m), OP.add))
            k.spacer()
            V(lambda e, m=m: e.tensor_tensor(sbA[:], sbZt[:], col(zl, m), OP.add), inc=True)
            pinc[m] = k.sem["vec"].n
            k.spacer()
            split(sbA[:, :], col(zh, m), col(zl, m))
            k.wait("scalar", "vec")
            C(lambda e: e.activation(sbSq[:], sbA[:], AF.Square), inc=True)
            k.wait("vector", "sca")
            k.wait("vector", "ten")
            split(sbSq[:, :], sbBh[:, :], sbBl[:, :])
            pstat[m] = k.sem["vec"].n
        stats_mms(NM - 1)
        finalize_ln(pbU0, pbU1)
        k.wait("vector", "ten")
        for m in range(NM):
            k.wait("vector", "dsca")
            V(lambda e, m=m: e.tensor_tensor(sbA[:], col(zh, m), col(zl, m), OP.add))
            k.spacer()
            V(lambda e: e.tensor_tensor(sbZt[:], sbA[:], pbB0[:, :], OP.subtract))
            k.spacer()
            V(lambda e: e.tensor_tensor(sbY[:], sbZt[:], pbB1[:, :], OP.mult), inc=True)
            k.wait("scalar", "vec")
            k.dma("scalar", d_out[tile, m], sbY[:, :], "dsca")

    dv_iltiles = dev.get("iltiles", NTILE)
    dv_il = dev.get("il", (dv_tiles == NTILE or "iltiles" in dev)
                    and dv_phases == ("attn", "z", "router", "b", "c")
                    and dv_cht == CHT and not dv_expose)

    def sub_thunks(tile):
        th = []
        for chi in range(CHT):
            th.append(lambda chi=chi, tile=tile: attn_s1(tile, chi))
            th.append(lambda chi=chi, tile=tile: attn_s2(tile, chi, tile % 2))
            th.append(lambda chi=chi, tile=tile: attn_s3(tile, chi, tile % 2))
        return th

    il0 = IL(sub_thunks(0), [10 * (i + 1) for i in range(3 * CHT)]) if dv_il else None
    prepass(NKG * 128 * GU_STRIDE, d_wGU, d_wGUh, d_wGUl, il=il0)
    prepass(NM * 128 * D_STRIDE, d_wD, d_wDh, d_wDl, il=il0)
    prepass(128 * AANW, d_aAll, d_aAllh, d_aAlll, il=il0)
    prepass(128 * ADNWP, d_aDn, d_aDnh, d_aDnl, il=il0)
    if il0 is not None:
        il0.flush()
        attn_flush_pend()
    k.barrier()

    if dv_il:
        for tile in range(dv_iltiles):
            tp = tile % 2
            z_phase(tile, tp)
            k.barrier()
            router(tile)
            k.barrier()
            if tile + 1 < dv_iltiles:
                ilbc = IL(sub_thunks(tile + 1), [1 + (i * 57) // 47 for i in range(48)])
            else:
                ilbc = None
            phase_b(tile, il=ilbc)
            k.barrier()
            phase_c(tile, tp, il=ilbc)
            if ilbc is not None:
                ilbc.flush()
                attn_flush_pend()
            k.barrier()

    for tile in (() if dv_il else range(dv_tiles)):
        tp = tile % 2
        if "attn" in dv_phases:
            for chi in range(dv_cht):
                attn_chunk(tile, chi, tp)
            attn_flush_pend()
            k.barrier()
            if d_dbgw is not None:
                for di, src in ((0, sbQ1[:, :]), (1, sbQ2[:, :])):
                    k.dma("scalar", _ap(d_dbgw, di * 128 * 768, [[768, 128], [1, 768]]), src, "dsca")
                for di, src, wdt in ((2, sbZr[:, :], 32), (3, sbPr[:, :], 256),
                                     (4, sbAV[:, :], 256), (5, sbL[:, :], 256)):
                    k.dma("scalar", _ap(d_dbgw, di * 128 * 768, [[768, 128], [1, wdt]]), src, "dsca")
                k.barrier()
        if "z" in dv_phases:
            z_phase(tile, tp)
            k.barrier()
            if d_dbgz is not None:
                for m in range(NM):
                    V(lambda e, m=m: e.tensor_tensor(sbY[:], col(zh, m), col(zl, m), OP.add), inc=True)
                    k.wait("scalar", "vec")
                    k.dma("scalar", _ap(d_dbgz, (tp * NM + m) * 128 * TCOL, [[512, 128], [1, 512]]), sbY[:, :], "dsca")
                    k.wait("vector", "dsca")
                k.barrier()
        if "router" in dv_phases:
            router(tile)
            k.barrier()
        if "b" in dv_phases:
            phase_b(tile)
            k.barrier()
            if d_dbgh is not None:
                for kg in range(4):
                    V(lambda e, kg=kg: e.tensor_tensor(sbY[:], col(hh, kg), col(hl, kg), OP.add), inc=True)
                    k.wait("scalar", "vec")
                    k.dma("scalar", _ap(d_dbgh, kg * 128 * TCOL, [[512, 128], [1, 512]]), sbY[:, :], "dsca")
                    k.wait("vector", "dsca")
                k.barrier()
        if "c" in dv_phases:
            phase_c(tile, tp)
            k.barrier()

    k.wait("sync", "dsca")
    k.wait("sync", "dsync")

    @blk.sync
    def _(eng):
        for fn in prog.ops["sync"]:
            fn(eng)

    @blk.tensor
    def _(eng):
        for fn in prog.ops["tensor"]:
            fn(eng)

    @blk.vector
    def _(eng):
        for fn in prog.ops["vector"]:
            fn(eng)

    @blk.scalar
    def _(eng):
        for fn in prog.ops["scalar"]:
            fn(eng)

    @blk.gpsimd
    def _(eng):
        for fn in prog.ops["gpsimd"]:
            fn(eng)

    ctx.close()
    return nc


def kernel(**inputs):
    x = np.asarray(inputs["L0_attn_x"], np.float32)
    wmaps = _stage_weights(inputs)
    in_maps = []
    for core in range(NCORES):
        m = dict(wmaps)
        m.update(_stage_x(x, core))
        in_maps.append(m)
    nc = build_nc()
    res = run_bass_kernel_spmd(nc, in_maps, core_ids=list(range(NCORES)))
    outs = [r["out"] for r in res.results]
    return _gather_out(outs)


# version 10
# speedup vs baseline: 1.5207x; 1.0318x over previous
import numpy as np
import ml_dtypes
from contextlib import ExitStack

import concourse.bass as bass
import concourse.mybir as mybir
from concourse.bass_utils import run_bass_kernel_spmd

F32 = mybir.dt.float32
BF16 = mybir.dt.bfloat16
OP = mybir.AluOpType
AF = mybir.ActivationFunctionType
AX = mybir.AxisListType

NCORES = 8
BB, HH, WW = 4, 8192, 2048
NCHUNK = 128
NTILE = 8
CHT = 16
NM = 16
TCOL = 512
DFF = 5504
NKG = 43
KC_GU = 17
KC_D = 44
GUON = 2176
DON = 5632
SCALE_ATT = float(8.0 ** -0.5)
DBG_HOOK = []
DBG_SKIP = set()
AANW = NM * 224
AACW = 224
ADNW = NKG * 72
ADNWP = 3104
GU_STRIDE = 2 * KC_GU * 128
D_STRIDE = KC_D * 128
WBUF_W = 2 * D_STRIDE


def _ap(t, offset, dims):
    return bass.AP(t.tensor if hasattr(t, "tensor") else t, offset, dims)



def _stage_weights(I):
    f = np.float32
    W_gate = np.asarray(I["L0_ffn_W_gate"], f)
    W_up = np.asarray(I["L0_ffn_W_up"], f)
    W_down = np.asarray(I["L0_ffn_W_down"], f)
    sB_gate = np.asarray(I["L0_ffn_sB_gate"], f)
    sB_up = np.asarray(I["L0_ffn_sB_up"], f)
    sB_down = np.asarray(I["L0_ffn_sB_down"], f)
    B_gate = np.asarray(I["L0_ffn_B_gate"], f).reshape(64, DFF)
    B_up = np.asarray(I["L0_ffn_B_up"], f).reshape(64, DFF)
    B_down = np.asarray(I["L0_ffn_B_down"], f).reshape(64, WW)
    A_gate = np.asarray(I["L0_ffn_A_gate"], f).transpose(1, 0, 2).reshape(WW, 64)
    A_up = np.asarray(I["L0_ffn_A_up"], f).transpose(1, 0, 2).reshape(WW, 64)
    A_down = np.asarray(I["L0_ffn_A_down"], f).transpose(1, 0, 2).reshape(DFF, 64)
    sA_gate = np.asarray(I["L0_ffn_sA_gate"], f)
    sA_up = np.asarray(I["L0_ffn_sA_up"], f)
    sA_down = np.asarray(I["L0_ffn_sA_down"], f)
    W_router = np.asarray(I["L0_ffn_W_router"], f)

    zg = np.zeros((GUON - 2120, DFF), f)
    Wg_aug = np.concatenate([W_gate, B_gate, sB_gate, zg], 0)
    Wu_aug = np.concatenate([W_up, B_up, sB_up, zg], 0)
    zd = np.zeros((DON - 5576, WW), f)
    Wd_aug = np.concatenate([W_down, B_down, sB_down, zd], 0)

    def slabify(Waug):
        return Waug.reshape(KC_GU, 128, NKG, 128).transpose(2, 1, 0, 3)
    wGU = np.stack([slabify(Wg_aug), slabify(Wu_aug)], 2)
    wGU = np.ascontiguousarray(wGU, f)
    wD = np.ascontiguousarray(
        Wd_aug.reshape(KC_D, 128, NM, 128).transpose(2, 1, 0, 3), f)

    A_full = np.zeros((WW, AACW), f)
    A_full[:, 0:64] = A_gate
    A_full[:, 64:128] = A_up
    A_full[:, 128:136] = sA_gate
    A_full[:, 160:168] = sA_up
    A_full[:, 192:200] = W_router
    aAll = np.ascontiguousarray(
        A_full.reshape(NM, 128, AACW).transpose(1, 0, 2).reshape(128, AANW), f)
    A_dn_full = np.concatenate([A_down, sA_down], 1)
    aDn = A_dn_full.reshape(NKG, 128, 72).transpose(1, 0, 2).reshape(128, ADNW)
    aDn = np.concatenate([aDn, np.zeros((128, ADNWP - ADNW), f)], 1)
    aDn = np.ascontiguousarray(aDn, f)

    qkv_w = np.asarray(I["L0_attn_qkv_w"], f)
    proj_w = np.asarray(I["L0_attn_proj_w"], f)
    col3_lo = np.r_[0:8, 16:24, 32:40]
    col3_hi = np.r_[8:16, 24:32, 40:48]
    wbd1 = np.zeros((128, 192), f)
    wbd2 = np.zeros((128, 192), f)
    for s in range(8):
        wbd1[16 * s:16 * s + 16, 24 * s:24 * s + 24] = qkv_w[:, col3_lo]
        wbd2[16 * s:16 * s + 16, 24 * s:24 * s + 24] = qkv_w[:, col3_hi]
    wpL = np.zeros((64, 128), f)
    wpH = np.zeros((64, 128), f)
    for s in range(8):
        wpL[8 * s:8 * s + 8, 16 * s:16 * s + 16] = proj_w[0:8, :]
        wpH[8 * s:8 * s + 8, 16 * s:16 * s + 16] = proj_w[8:16, :]

    cw0 = np.asarray(I["L0_attn_conv_w0"], f)
    cb0 = np.asarray(I["L0_attn_conv_b0"], f)
    cw1 = np.asarray(I["L0_attn_conv_w1"], f)
    cb1 = np.asarray(I["L0_attn_conv_b1"], f)
    taps = np.zeros((8, 64), f)
    for s in range(8):
        for c in range(8):
            for ti, tau in enumerate((-1, 0, 1)):
                taps[ti, 8 * s + c] = cw0[c, 0, 1, tau + 1]
                taps[3 + ti, 8 * s + c] = cw1[c, 0, tau + 1, 1]
            taps[6, 8 * s + c] = cb0[c]
            taps[7, 8 * s + c] = cb1[c]

    ln_w = np.asarray(I["L0_attn_ln_w"], f)
    ln_b = np.asarray(I["L0_attn_ln_b"], f)
    proj_b = np.asarray(I["L0_attn_proj_b"], f)
    rows = np.zeros((3, TCOL), f)
    for chi in range(16):
        for j in range(2):
            for c in range(16):
                col = 32 * chi + 16 * j + c
                rows[0, col] = ln_w[c]
                rows[1, col] = ln_b[c]
                rows[2, col] = proj_b[c]
    ebias = np.asarray(I["L0_ffn_expert_bias"], f).reshape(1, 8)
    return dict(wGU=wGU, wD=wD, aAll=aAll, aDn=aDn, wbd1=wbd1, wbd2=wbd2,
                wpL=wpL, wpH=wpH, taps=taps, rows=rows, ebias=ebias)


def _stage_x(x, core):
    f = np.float32
    u0 = core * NCHUNK
    b = u0 // 256
    k0 = u0 % 256
    xc = np.asarray(x[b, 32 * k0: 32 * k0 + 32 * NCHUNK, :], f)
    flat = xc.reshape(NCHUNK, 65536)
    xD = np.ascontiguousarray(
        flat.reshape(NCHUNK, 512, 8, 16).transpose(0, 2, 3, 1).reshape(NCHUNK, 128, 512))
    xP2 = np.ascontiguousarray(
        flat.reshape(NCHUNK, 8, 512, 16).transpose(0, 1, 3, 2).reshape(NCHUNK, 128, 512))
    v = flat.reshape(NTILE, CHT, 2, NM, 128, 16)
    xTok = np.ascontiguousarray(
        v.transpose(0, 3, 4, 1, 2, 5).reshape(NTILE, NM, 128, TCOL))
    t_, chi_, j_, c_ = np.meshgrid(np.arange(NTILE), np.arange(CHT),
                                   np.arange(2), np.arange(16), indexing="ij")
    hidx = 512 * c_ + 2 * (k0 + 16 * t_ + chi_) + j_
    xr = np.asarray(x[b], f)[hidx.reshape(-1), :]
    xr = xr.reshape(NTILE, CHT, 2, 16, NM, 128)
    xRes = np.ascontiguousarray(xr.transpose(0, 4, 5, 1, 2, 3).reshape(NTILE, NM, 128, TCOL))
    return dict(xD=xD, xP2=xP2, xTok=xTok, xRes=xRes)


def _gather_out(outs):
    final = np.empty((BB, HH, WW), np.float32)
    for core in range(NCORES):
        u0 = core * NCHUNK
        b = u0 // 256
        k0 = u0 % 256
        o = np.asarray(outs[core], np.float32).reshape(NTILE, NM, 128, CHT, 2, 16)
        o = o.transpose(0, 3, 4, 5, 1, 2).reshape(4096, 2048)
        t_, chi_, j_, c_ = np.meshgrid(np.arange(NTILE), np.arange(CHT),
                                       np.arange(2), np.arange(16), indexing="ij")
        hidx = (512 * c_ + 2 * (k0 + 16 * t_ + chi_) + j_).reshape(-1)
        final[b, hidx, :] = o
    return final



ENGS = ("sync", "tensor", "vector", "scalar", "gpsimd")


class Prog:
    def __init__(self):
        self.ops = {e: [] for e in ENGS}

    def on(self, eng, fn):
        self.ops[eng].append(fn)


class Sem:
    def __init__(self, ctx, nc, name):
        self.h = ctx.enter_context(nc.semaphore(name))
        self.n = 0


class K:
    def __init__(self, nc, ctx, prog):
        self.nc, self.ctx, self.p = nc, ctx, prog
        self.sem = {}
        for name in ("ld", "ld2", "lat", "ldw", "ldu", "lda", "ten", "vec", "sca",
                     "gp", "dsync", "dsca", "dsc2"):
            self.sem[name] = Sem(ctx, nc, "s_" + name)

    def dma(self, qeng, out, in_, sem_name):
        s = self.sem[sem_name]
        s.n += 16
        self.p.on(qeng, lambda e, o=out, i=in_, h=s.h: e.dma_start(out=o, in_=i).then_inc(h, 16))

    def wait(self, eng, sem_name, val=None):
        s = self.sem[sem_name]
        v = s.n if val is None else val
        if v > 0:
            self.p.on(eng, lambda e, h=s.h, vv=v: e.wait_ge(h, vv))

    def spacer(self):
        pass

    def _mk(self, eng, sem_name):
        dr = eng in ("vector", "scalar")
        def em(fn, inc=False):
            if inc:
                s = self.sem[sem_name]
                s.n += 1
                self.p.on(eng, lambda e, h=s.h, f=fn: f(e).then_inc(h, 1))
            else:
                self.p.on(eng, fn)
            if dr:
                self.p.on(eng, lambda e: e.drain())
        return em

    def barrier(self):
        for e in ENGS:
            for nm in ("ten", "vec", "sca", "gp", "ld", "ld2", "lat", "ldw",
                       "ldu", "lda", "dsync", "dsca", "dsc2"):
                self.wait(e, nm)


class IL:

    def __init__(self, thunks, positions):
        assert len(positions) == len(thunks)
        self.items = list(zip(positions, thunks))
        self.i = 0
        self.t = -1

    def tick(self):
        self.t += 1
        while self.i < len(self.items) and self.items[self.i][0] <= self.t:
            self.items[self.i][1]()
            self.i += 1

    def flush(self):
        while self.i < len(self.items):
            self.items[self.i][1]()
            self.i += 1



def build_nc(dev=None):
    dev = dev or {}
    dv_tiles = dev.get("tiles", NTILE)
    dv_phases = dev.get("phases", ("attn", "z", "router", "b", "c"))
    dv_expose = dev.get("expose", ())
    dv_cht = dev.get("cht", CHT)
    nc = bass.Bass("TRN2", target_bir_lowering=False)
    ctx = ExitStack()
    prog = Prog()

    d_xD = nc.dram_tensor("xD", [NCHUNK, 128, 512], F32, kind="ExternalInput").ap()
    d_xP2 = nc.dram_tensor("xP2", [NCHUNK, 128, 512], F32, kind="ExternalInput").ap()
    d_xTok = nc.dram_tensor("xTok", [NTILE, NM, 128, TCOL], F32, kind="ExternalInput").ap()
    d_xRes = nc.dram_tensor("xRes", [NTILE, NM, 128, TCOL], F32, kind="ExternalInput").ap()
    d_wGU = nc.dram_tensor("wGU", [NKG, 128, 2, KC_GU, 128], F32, kind="ExternalInput").ap()
    d_wD = nc.dram_tensor("wD", [NM, 128, KC_D, 128], F32, kind="ExternalInput").ap()
    d_aAll = nc.dram_tensor("aAll", [128, AANW], F32, kind="ExternalInput").ap()
    d_aDn = nc.dram_tensor("aDn", [128, ADNWP], F32, kind="ExternalInput").ap()
    d_wbd1 = nc.dram_tensor("wbd1", [128, 192], F32, kind="ExternalInput").ap()
    d_wbd2 = nc.dram_tensor("wbd2", [128, 192], F32, kind="ExternalInput").ap()
    d_wpL = nc.dram_tensor("wpL", [64, 128], F32, kind="ExternalInput").ap()
    d_wpH = nc.dram_tensor("wpH", [64, 128], F32, kind="ExternalInput").ap()
    d_taps = nc.dram_tensor("taps", [8, 64], F32, kind="ExternalInput").ap()
    d_rows = nc.dram_tensor("rows", [3, TCOL], F32, kind="ExternalInput").ap()
    d_ebias = nc.dram_tensor("ebias", [1, 8], F32, kind="ExternalInput").ap()
    d_out = nc.dram_tensor("out", [NTILE, NM, 128, TCOL], F32, kind="ExternalOutput").ap()

    d_wGUh = nc.dram_tensor("wGUs_hi", [NKG, 128, 2, KC_GU, 128], BF16).ap()
    d_wGUl = nc.dram_tensor("wGUs_lo", [NKG, 128, 2, KC_GU, 128], BF16).ap()
    d_wDh = nc.dram_tensor("wDs_hi", [NM, 128, KC_D, 128], BF16).ap()
    d_wDl = nc.dram_tensor("wDs_lo", [NM, 128, KC_D, 128], BF16).ap()
    d_aAllh = nc.dram_tensor("aAll_hi", [128, AANW], BF16).ap()
    d_aAlll = nc.dram_tensor("aAll_lo", [128, AANW], BF16).ap()
    d_aDnh = nc.dram_tensor("aDn_hi", [128, ADNWP], BF16).ap()
    d_aDnl = nc.dram_tensor("aDn_lo", [128, ADNWP], BF16).ap()
    delkind = dict(kind="ExternalOutput") if "del" in dv_expose else {}
    d_del1 = nc.dram_tensor("del1", [2, NM, 128, TCOL], F32, **delkind).ap()
    d_del2 = nc.dram_tensor("del2", [2, NM, 128, TCOL], F32, **delkind).ap()
    d_dbgz = None
    if "z" in dv_expose:
        d_dbgz = nc.dram_tensor("dbgz", [2, NM, 128, TCOL], F32, kind="ExternalOutput").ap()
    d_dbgh = None
    if "h" in dv_expose:
        d_dbgh = nc.dram_tensor("dbgh", [4, 128, TCOL], F32, kind="ExternalOutput").ap()
    d_dbgw = None
    if "attnwork" in dv_expose:
        d_dbgw = nc.dram_tensor("dbgw", [6, 128, 768], F32, kind="ExternalOutput").ap()

    e = ctx.enter_context
    sb = {}

    def S(name, shape, dtype=F32):
        sb[name] = e(nc.sbuf_tensor(name, list(shape), dtype))
        return sb[name]

    ident = S("ident", (128, 128))
    onesrow = S("onesrow", (1, 128))
    onescol = S("onescol", (128, 1), BF16)
    tapT = S("tapT", (128, 8 * 64))
    gamT = S("gamT", (128, TCOL))
    betT = S("betT", (128, TCOL))
    biasT = S("biasT", (128, TCOL))
    ebT = S("ebT", (128, 8))
    wbd1S = S("wbd1S", (128, 192))
    wbd2S = S("wbd2S", (128, 192))
    wpLS = S("wpLS", (64, 128))
    wpHS = S("wpHS", (64, 128))

    zh = S("zh", (128, NM * TCOL), BF16)
    zl = S("zl", (128, NM * TCOL), BF16)
    hh = S("hh", (128, NKG * TCOL), BF16)
    hl = S("hl", (128, NKG * TCOL), BF16)
    ug_h = S("ug_h", (128, TCOL), BF16)
    ug_l = S("ug_l", (128, TCOL), BF16)
    uu_h = S("uu_h", (128, TCOL), BF16)
    uu_l = S("uu_l", (128, TCOL), BF16)
    ud_h = S("ud_h", (128, TCOL), BF16)
    ud_l = S("ud_l", (128, TCOL), BF16)
    wexp = S("wexp", (64, TCOL))

    wbuf = S("wbuf", (128, WBUF_W), BF16)
    aABuf = S("aABuf", (128, 2 * AACW), BF16)
    aDBuf = S("aDBuf", (128, 4 * 72), BF16)

    sbD = S("sbD", (128, 512))
    sbP2 = S("sbP2", (128, 512))
    sbQ1 = S("sbQ1", (128, 768))
    sbQ2 = S("sbQ2", (128, 768))
    sbS = S("sbS", (128, 256))
    sbPr = S("sbPr", (128, 256))
    sbM = S("sbM", (128, 32))
    sbZr = S("sbZr", (128, 32))
    sbAV = S("sbAV", (128, 256))
    sbL = S("sbL", (128, 256))
    sbT1 = S("sbT1", (128, 256))
    sbOT = S("sbOT", (64, 512))
    sbAtt = S("sbAtt", (128, 512))

    sbDel1 = S("sbDel1", (128, TCOL))
    sbDel2 = S("sbDel2", (128, TCOL))
    sbXT = S("sbXT", (128, TCOL))
    sbXR = S("sbXR", (128, TCOL))
    sbA = S("sbA", (128, TCOL))
    sbSq = S("sbSq", (128, TCOL))
    sbMu = S("sbMu", (128, 32))
    sbVar = S("sbVar", (128, 32))
    sbRs = S("sbRs", (128, 32))
    sbT2 = S("sbT2", (128, 32))
    sbZt = S("sbZt", (128, TCOL))
    sbBh = S("sbBh", (128, TCOL), BF16)
    sbBl = S("sbBl", (128, TCOL), BF16)
    sbSil = S("sbSil", (128, TCOL))
    sbY = S("sbY", (128, TCOL))
    sbRow = S("sbRow", (4, TCOL))
    sbSc = S("sbSc", (128, 32))
    sbSc2 = S("sbSc2", (128, 32))
    sbSc3 = S("sbSc3", (128, 32))
    sbSc4 = S("sbSc4", (128, 8))
    sbWt = S("sbWt", (128, 64))
    sbFm = S("sbFm", (8, TCOL))

    sbEps = S("sbEps", (128, 1))
    sbBh2 = S("sbBh2", (128, TCOL), BF16)
    sbBl2 = S("sbBl2", (128, TCOL), BF16)

    ps = {}

    def P(name):
        ps[name] = e(nc.psum_tensor(name, [128, 512], F32))
        return ps[name]
    pbG0, pbG1 = P("pbG0"), P("pbG1")
    pbU0, pbU1 = P("pbU0"), P("pbU1")
    pbA0, pbA1 = P("pbA0"), P("pbA1")
    pbB0, pbB1 = P("pbB0"), P("pbB1")

    fence_v = S("fence_v", (1, 2))
    fence_s = S("fence_s", (1, 2))
    blk = e(nc.Block())
    k = K(nc, ctx, prog)
    k.fence_v = fence_v
    k.fence_s = fence_s
    T = k._mk("tensor", "ten")
    V = k._mk("vector", "vec")
    C = k._mk("scalar", "sca")
    G = k._mk("gpsimd", "gp")

    def col(t, m, w=TCOL):
        return t[:, m * w:(m + 1) * w]

    def split(src_f32, dst_h, dst_l):
        V(lambda e, s=src_f32, d=dst_h: e.tensor_copy(d, s))
        k.spacer()
        V(lambda e, s=src_f32, dh=dst_h, dl=dst_l: e.scalar_tensor_tensor(
            dl, s, 1.0, dh, OP.mult, OP.subtract), inc=True)

    k.dma("sync", wbd1S[:, :], d_wbd1[:, :], "ld")
    k.dma("sync", wbd2S[:, :], d_wbd2[:, :], "ld")
    k.dma("sync", wpLS[:, :], d_wpL[:, :], "ld")
    k.dma("sync", wpHS[:, :], d_wpH[:, :], "ld")

    V(lambda e: e.memset(ident[:], 1.0))
    V(lambda e: e.memset(onesrow[:], 1.0))
    V(lambda e: e.memset(onescol[:], 1.0))
    for nm_ in ("ug_h", "ug_l", "uu_h", "uu_l", "ud_h", "ud_l"):
        V(lambda e, t=sb[nm_]: e.memset(t[:], 0.0))
    V(lambda e: e.memset(sb["fence_v"][:], 0.0))
    V(lambda e: e.memset(sb["fence_s"][:], 0.0))
    V(lambda e: e.memset(sbEps[:], 1e-5))
    V(lambda e: e.memset(sbT1[:], 0.0), inc=True)
    k.wait("gpsimd", "vec")
    G(lambda e: e.affine_select(ident[:], ident[:], [[-1, 128]], OP.is_equal,
                                0.0, base=0, channel_multiplier=1), inc=True)
    srow = sbFm
    def bcast_row(d_src_row, width, dst_tile, dst_w):
        k.wait("sync", "ten")
        k.dma("sync", srow[0:1, 0:width], d_src_row, "ld")
        k.wait("tensor", "ld")
        k.wait("tensor", "vec")
        k.wait("tensor", "gp")
        k.wait("tensor", "sca")
        T(lambda e, w=width: e.matmul(pbB0[:, 0:w], onesrow[:], srow.ap()[0:1, 0:w],
                                      start=True, stop=True), inc=True)
        k.wait("scalar", "ten")
        C(lambda e, d=dst_tile, w=dst_w: e.copy(d, pbB0[:, 0:w]), inc=True)

    bcast_row(d_rows[0:1, :], TCOL, gamT[:, :], TCOL)
    bcast_row(d_rows[1:2, :], TCOL, betT[:, :], TCOL)
    bcast_row(d_rows[2:3, :], TCOL, biasT[:, :], TCOL)
    bcast_row(d_ebias[0:1, :], 8, ebT[:, :], 8)
    for ti in range(8):
        bcast_row(d_taps[ti:ti + 1, :], 64, tapT[:, 64 * ti:64 * ti + 64], 64)
    k.barrier()

    pp_state = {"vc": [0, 0], "dc": [0, 0], "dc2": [0, 0], "lc": [0, 0]}

    def prepass(total, d_src, d_hi, d_lo, il=None):
        W = 512
        piece = 128 * W
        n = total // piece
        rem = total - n * piece
        assert rem % W == 0
        bufs = [(sbA, sbBh, sbBl), (sbSq, sbBh2, sbBl2)]
        ldsem = ("ld", "ld2")
        stsem = ("dsca", "dsc2")
        stsem2 = stsem
        st = pp_state
        for i in range(n + (1 if rem else 0)):
            if il is not None:
                il.tick()
            lo = i * piece
            rows = 128 if i < n else rem // W
            p = i % 2
            bA, bh, bl = bufs[p]
            k.wait("sync", "vec", st["vc"][p])
            k.wait("sync", stsem[p], st["dc"][p])
            k.wait("sync", stsem2[p], st["dc2"][p])
            k.dma("sync", bA[0:rows, :], _ap(d_src, lo, [[W, rows], [1, W]]), ldsem[p])
            st["lc"][p] = k.sem[ldsem[p]].n
            k.wait("vector", ldsem[p], st["lc"][p])
            V(lambda e, bA=bA, bh=bh, r=rows: e.tensor_copy(bh[0:r, :], bA[0:r, :]))
            k.spacer()
            V(lambda e, bA=bA, bh=bh, bl=bl, r=rows: e.scalar_tensor_tensor(
                bl[0:r, :], bA[0:r, :], 1.0, bh[0:r, :], OP.mult, OP.subtract), inc=True)
            st["vc"][p] = k.sem["vec"].n
            k.wait("scalar", "vec", st["vc"][p])
            k.dma("scalar", _ap(d_hi, lo, [[W, rows], [1, W]]), bh[0:rows, :], stsem[p])
            k.wait("scalar", "vec", st["vc"][p])
            k.dma("scalar", _ap(d_lo, lo, [[W, rows], [1, W]]), bl[0:rows, :], stsem2[p])
            st["dc"][p] = k.sem[stsem[p]].n
            st["dc2"][p] = k.sem[stsem2[p]].n

    def attn_branch_vec(Q, tapbase, cbrow):
        pbufs = [sbAV[:, :], sbL[:, :], sbT1[:, :]]
        def pview(i):
            t = pbufs[i % 3]
            return bass.AP(t.tensor, t.offset, [t.ap[0], [64, 4], [8, 8], [1, 8]])
        for i in range(8):
            in0 = _ap(Q, 24 * i, [[768, 128], [192, 4], [0, 8], [1, 8]])
            in1 = _ap(Q, 8, [[768, 128], [192, 4], [24, 8], [1, 8]])
            V(lambda e, in0=in0, in1=in1, dv=pview(i): e.tensor_tensor(dv, in0, in1, OP.mult))
            if i == 0:
                k.spacer()
            dst = _ap(sbS, 8 * i, [[256, 128], [64, 4], [1, 8]])
            rv = pbufs[i % 3]
            rv = bass.AP(rv.tensor, rv.offset, [rv.ap[0], [64, 4], [8, 8], [1, 8]])
            V(lambda e, dst=dst, rv=rv: e.tensor_reduce(dst, rv, AX.X, OP.add))
        k.spacer()
        V(lambda e: e.tensor_reduce(sbM[:], sbS[:].rearrange("p (bi j) -> p bi j", j=8), AX.X, OP.max))
        k.spacer()
        mbc = _ap(sbM, 0, [[32, 128], [1, 32], [0, 8]])
        V(lambda e, mbc=mbc: e.tensor_tensor(
            sbPr[:].rearrange("p (bi j) -> p bi j", j=8),
            sbS[:].rearrange("p (bi j) -> p bi j", j=8), mbc, OP.subtract), inc=True)
        k.wait("scalar", "vec")
        C(lambda e: e.activation(sbPr[:], sbPr[:], AF.Exp, scale=SCALE_ATT), inc=True)
        k.wait("vector", "sca")
        V(lambda e: e.tensor_reduce(sbZr[:], sbPr[:].rearrange("p (bi j) -> p bi j", j=8), AX.X, OP.add))
        k.spacer()
        V(lambda e: e.reciprocal(sbZr[:], sbZr[:]))
        abufs = [sbS[:, :], sbL[:, :], sbT1[:, :]]
        for i in range(8):
            in0 = _ap(sbPr, 8 * i, [[256, 128], [64, 4], [1, 8], [0, 8]])
            in1 = _ap(Q, 16, [[768, 128], [192, 4], [24, 8], [1, 8]])
            t = abufs[i % 3]
            dstp = bass.AP(t.tensor, t.offset, [t.ap[0], [64, 4], [1, 8], [8, 8]])
            V(lambda e, in0=in0, in1=in1, dstp=dstp: e.tensor_tensor(dstp, in0, in1, OP.mult))
            if i == 0:
                k.spacer()
            dst = _ap(sbAV, 8 * i, [[256, 128], [64, 4], [1, 8]])
            rv = bass.AP(t.tensor, t.offset, [t.ap[0], [64, 4], [8, 8], [1, 8]])
            V(lambda e, dst=dst, rv=rv: e.tensor_reduce(dst, rv, AX.X, OP.add))
        Vall = _ap(Q, 16, [[768, 128], [192, 4], [24, 8], [1, 8]])
        Lv = _ap(sbL, 0, [[256, 128], [64, 4], [1, 64]])
        cb = _ap(tapT, 64 * cbrow, [[512, 128], [0, 4], [1, 64]])
        V(lambda e, Vall=Vall: e.tensor_tensor(
            _ap(sbL, 0, [[256, 128], [64, 4], [8, 8], [1, 8]]), Vall,
            _ap(tapT, 64 * (tapbase + 1), [[512, 128], [0, 4], [8, 8], [1, 8]]), OP.mult))
        vs1 = _ap(Q, 16 + 24, [[768, 128], [192, 4], [24, 7], [1, 8]])
        tp1 = _ap(tapT, 64 * (tapbase + 2), [[512, 128], [0, 4], [8, 7], [1, 8]])
        tslA = _ap(sbS, 0, [[256, 128], [64, 4], [8, 7], [1, 8]])
        V(lambda e, vs1=vs1, tp1=tp1, tslA=tslA: e.tensor_tensor(tslA, vs1, tp1, OP.mult))
        vsm = _ap(Q, 16, [[768, 128], [192, 4], [24, 7], [1, 8]])
        tm1 = _ap(tapT, 64 * (tapbase + 0), [[512, 128], [0, 4], [8, 7], [1, 8]])
        tslB = _ap(sbT1, 0, [[256, 128], [64, 4], [8, 7], [1, 8]])
        V(lambda e, vsm=vsm, tm1=tm1, tslB=tslB: e.tensor_tensor(tslB, vsm, tm1, OP.mult))
        dstL1 = _ap(sbL, 0, [[256, 128], [64, 4], [8, 7], [1, 8]])
        V(lambda e, dstL1=dstL1, tslA=tslA: e.tensor_tensor(dstL1, dstL1, tslA, OP.add))
        k.spacer()
        dstL2 = _ap(sbL, 8, [[256, 128], [64, 4], [8, 7], [1, 8]])
        V(lambda e, dstL2=dstL2, tslB=tslB: e.tensor_tensor(dstL2, dstL2, tslB, OP.add))
        k.spacer()
        V(lambda e, Lv=Lv, cb=cb: e.tensor_tensor(Lv, Lv, cb, OP.add))
        zbc = _ap(sbZr, 0, [[32, 128], [8, 4], [1, 8], [0, 8]])
        avv = sbAV[:].rearrange("p (b i c) -> p b i c", b=4, c=8)
        k.spacer()
        t1v = sbT1[:].rearrange("p (b i c) -> p b i c", b=4, c=8)
        V(lambda e, zbc=zbc, avv=avv, t1v=t1v: e.tensor_tensor(t1v, avv, zbc, OP.mult))
        k.spacer()
        V(lambda e: e.tensor_tensor(sbAV[:], sbT1[:], sbL[:], OP.add), inc=True)
        return k.sem["vec"].n

    def attn_branch_mat(wp, vpin, spin):
        otp = {}

        def t_op(b4):
            i, bank, (ps, pv) = rot_next()
            k.wait("tensor", "vec", max(pv, vpin))
            k.wait("tensor", "sca", max(ps, spin))
            T(lambda e, b4=b4, bank=bank: e.transpose(
                _ap(bank, 0, [[512, 64], [1, 128]]), sbAV[:, 64 * b4:64 * b4 + 64], ident[:]), inc=True)
            k.wait("scalar", "ten")
            C(lambda e, b4=b4, bank=bank: e.copy(
                sbOT[:, 128 * b4:128 * b4 + 128], _ap(bank, 0, [[512, 64], [1, 128]])), inc=True)
            otp[b4] = k.sem["sca"].n
            rot["pins"][i][0] = otp[b4]

        def p_op(b4):
            i, bank, (ps, pv) = rot_next()
            k.wait("tensor", "sca", max(ps, otp[b4]))
            k.wait("tensor", "vec", pv)
            T(lambda e, b4=b4, bank=bank, wp=wp: e.matmul(
                bank[:, 0:128], sbOT[:, 128 * b4:128 * b4 + 128], wp[:], start=True, stop=True), inc=True)
            k.wait("vector", "ten")
            V(lambda e, b4=b4, bank=bank: e.tensor_copy(
                sbAtt[:, 128 * b4:128 * b4 + 128], bank[:, 0:128]), inc=True)
            rot["pins"][i][1] = k.sem["vec"].n

        for kind, b4 in (("t", 0), ("t", 1), ("t", 2), ("p", 0),
                         ("t", 3), ("p", 1), ("p", 2), ("p", 3)):
            (t_op if kind == "t" else p_op)(b4)
        return k.sem["vec"].n

    def dump_att1(tp, chi):
        if "contig" in DBG_SKIP:
            dst = _ap(d_del1, tp * (NM * 128 * TCOL) + chi * 128 * TCOL, [[512, 128], [1, 512]])
            k.dma("sync", dst, sbAtt[:, :], "dsync")
            return
        for gB in range(4):
            j = gB // 2
            mbase = (8 * gB) % 16
            off = tp * (NM * 128 * TCOL) + mbase * (128 * TCOL) + 32 * chi + 16 * j
            dst = _ap(d_del1, off, [[4096, 128], [512, 8], [1, 16]])
            src = _ap(sbAtt, 128 * gB, [[512, 128], [16, 8], [1, 16]])
            k.dma("gpsimd", dst, src, "dsync")

    def dump_att2(tp, chi):
        if "contig" in DBG_SKIP:
            dst = _ap(d_del2, tp * (NM * 128 * TCOL) + chi * 128 * TCOL, [[512, 128], [1, 512]])
            k.dma("scalar", dst, sbAtt[:, :], "dsync")
            return
        for wgB in range(4):
            for j in range(2):
                off = tp * (NM * 128 * TCOL) + wgB * (128 * TCOL) + 32 * chi + 16 * j
                dst = _ap(d_del2, off, [[512, 128], [4 * 128 * TCOL, 4], [1, 16]])
                src = _ap(sbAtt, 128 * wgB + 64 * j, [[512, 128], [16, 4], [1, 16]])
                k.dma("gpsimd", dst, src, "dsync")

    ast = {"qkv": 0, "evac": 0, "v1": 0, "v2": 0, "sca1": 0, "d1": 0, "pend2": None}
    rot = {"banks": [pbA0, pbA1, pbB1], "pins": [[0, 0], [0, 0], [0, 0]], "r": 0}

    def rot_next():
        i = rot["r"] % 3
        rot["r"] += 1
        return i, rot["banks"][i], rot["pins"][i]

    def attn_s1(tile, chi):
        cidx = CHT * tile + chi
        attn_flush_pend()
        k.wait("gpsimd", "ten", ast["qkv"])
        k.dma("gpsimd", sbD[:, :], d_xD[cidx], "lat")
        k.dma("gpsimd", sbP2[:, :], d_xP2[cidx], "lat")
        k.wait("tensor", "lat")
        for src, Qd, wbd in ((sbD, sbQ1, wbd1S), (sbP2, sbQ2, wbd2S)):
            for b4 in range(4):
                i, bank, (ps, pv) = rot_next()
                k.wait("tensor", "sca", ps)
                k.wait("tensor", "vec", pv)
                T(lambda e, b4=b4, bank=bank, src=src, wbd=wbd: e.matmul(
                    bank[:, 0:192], src[:, 128 * b4:128 * b4 + 128], wbd[:], start=True, stop=True), inc=True)
                k.wait("scalar", "ten")
                C(lambda e, b4=b4, bank=bank, Qd=Qd: e.copy(
                    Qd[:, 192 * b4:192 * b4 + 192], bank[:, 0:192]), inc=True)
                rot["pins"][i][0] = k.sem["sca"].n
        ast["qkv"] = k.sem["ten"].n
        ast["sca1"] = k.sem["sca"].n
        k.wait("vector", "sca")
        ast["v1"] = attn_branch_vec(sbQ1, 0, 6)

    def attn_s2(tile, chi, tp):
        k.wait("vector", "dsync")
        ast["d1"] = attn_branch_mat(wpLS, ast["v1"], ast["sca1"])
        ast["v2"] = attn_branch_vec(sbQ2, 3, 7)

    def attn_s3(tile, chi, tp):
        k.wait("gpsimd", "vec", ast["d1"])
        dump_att1(tp, chi)
        k.wait("vector", "dsync")
        ast["evac"] = attn_branch_mat(wpHS, ast["v2"], ast["sca1"])
        ast["pend2"] = (tp, chi)

    def attn_flush_pend():
        if ast["pend2"] is not None:
            tp, chi = ast["pend2"]
            k.wait("gpsimd", "vec", ast["evac"])
            dump_att2(tp, chi)
            ast["pend2"] = None

    def attn_chunk(tile, chi, tp):
        attn_s1(tile, chi)
        attn_s2(tile, chi, tp)
        attn_s3(tile, chi, tp)

    def z_phase(tile, tp):
        ZE = dict(d1=sbDel1[:, :], d2=sbDel2[:, :], xT=sbXT[:, :], xR=sbXR[:, :],
                  a=sbA[:, :], zt=sbZt[:, :],
                  zt3=sbZt[:].rearrange("p (a c) -> p a c", c=16),
                  sq=sbSq[:, :], sq3=sbSq[:].rearrange("p (a c) -> p a c", c=16),
                  mu=sbMu[:, :], var=sbVar[:, :], t2=sbT2[:, :],
                  mubc=_ap(sbMu, 0, [[32, 128], [1, 32], [0, 16]]),
                  rsbc=_ap(sbT2, 0, [[32, 128], [1, 32], [0, 16]]),
                  a3=sbA[:].rearrange("p (a c) -> p a c", c=16),
                  bh=sbBh[:, :], bl=sbBl[:, :])
        ZO = dict(d1=sbD[:, :], d2=sbP2[:, :], xT=sbSil[:, :], xR=sbY[:, :],
                  a=sbAtt[:, :], zt=_ap(sbQ1, 0, [[768, 128], [1, 512]]),
                  zt3=_ap(sbQ1, 0, [[768, 128], [16, 32], [1, 16]]),
                  sq=_ap(sbQ2, 0, [[768, 128], [1, 512]]),
                  sq3=_ap(sbQ2, 0, [[768, 128], [16, 32], [1, 16]]),
                  mu=sbM[:, :], var=sbSc[:, :], t2=sbSc2[:, :],
                  mubc=_ap(sbM, 0, [[32, 128], [1, 32], [0, 16]]),
                  rsbc=_ap(sbSc2, 0, [[32, 128], [1, 32], [0, 16]]),
                  a3=sbAtt[:].rearrange("p (a c) -> p a c", c=16),
                  bh=sbBh2[:, :], bl=sbBl2[:, :])
        dsem = ("ld", "ldw")
        xsem = ("ld2", "lda")
        dpin = {}
        xpin = {}
        pins = {}

        def load_del(p, gate):
            if p >= NM // 2:
                return
            k.wait("sync", "vec", gate)
            for mm in (2 * p, 2 * p + 1):
                Z = ZE if mm % 2 == 0 else ZO
                k.dma("sync", Z["d1"], _ap(d_del1, (tp * NM + mm) * 128 * TCOL, [[512, 128], [1, 512]]), dsem[p % 2])
                k.dma("sync", Z["d2"], _ap(d_del2, (tp * NM + mm) * 128 * TCOL, [[512, 128], [1, 512]]), dsem[p % 2])
            dpin[p] = k.sem[dsem[p % 2]].n

        def load_xtr(p, gate):
            if p >= NM // 2:
                return
            k.wait("sync", "vec", gate)
            for mm in (2 * p, 2 * p + 1):
                Z = ZE if mm % 2 == 0 else ZO
                k.dma("sync", Z["xT"], d_xTok[tile, mm], xsem[p % 2])
                k.dma("sync", Z["xR"], d_xRes[tile, mm], xsem[p % 2])
            xpin[p] = k.sem[xsem[p % 2]].n

        def z_steps(m):
            Z = ZE if m % 2 == 0 else ZO
            p = m // 2

            def s0():
                k.wait("vector", dsem[p % 2], dpin[p])
                V(lambda e: e.tensor_tensor(Z["a"], Z["d1"], Z["d2"], OP.add))

            def s1():
                V(lambda e: e.tensor_tensor(Z["zt"], Z["a"], biasT[:], OP.add), inc=True)
                pins[(m, "v2")] = k.sem["vec"].n

            def s2():
                V(lambda e: e.tensor_reduce(Z["mu"], Z["zt3"], AX.X, OP.add))

            def s3():
                k.wait("scalar", "vec", pins[(m, "v2")])
                C(lambda e: e.activation(Z["sq"], Z["zt"], AF.Square), inc=True)
                pins[(m, "c1")] = k.sem["sca"].n

            def s4():
                k.wait("vector", "sca", pins[(m, "c1")])
                V(lambda e: e.tensor_reduce(Z["var"], Z["sq3"], AX.X, OP.add))

            def s5():
                V(lambda e: e.tensor_scalar(Z["mu"], Z["mu"], -1.0 / 16.0, None, OP.mult))

            def s6():
                V(lambda e: e.tensor_tensor(Z["a3"], Z["zt3"], Z["mubc"], OP.add))

            def s7():
                V(lambda e: e.tensor_tensor(Z["t2"], Z["mu"], Z["mu"], OP.mult))

            def s8():
                V(lambda e: e.scalar_tensor_tensor(
                    Z["var"], Z["var"], 1.0 / 16.0, Z["t2"], OP.mult, OP.subtract), inc=True)
                pins[(m, "v6")] = k.sem["vec"].n

            def s9():
                k.wait("scalar", "vec", pins[(m, "v6")])
                C(lambda e: e.activation(Z["t2"], Z["var"], AF.Sqrt, bias=sbEps[:, :]), inc=True)
                pins[(m, "c2")] = k.sem["sca"].n

            def s10():
                k.wait("vector", "sca", pins[(m, "c2")])
                V(lambda e: e.reciprocal(Z["t2"], Z["t2"]))
                V(lambda e: e.tensor_tensor(Z["zt3"], Z["a3"], Z["rsbc"], OP.mult))

            def s11():
                V(lambda e: e.tensor_tensor(Z["a"], Z["zt"], gamT[:], OP.mult))

            def s12():
                V(lambda e: e.tensor_tensor(Z["zt"], Z["a"], betT[:], OP.add))

            def s13():
                k.wait("vector", xsem[p % 2], xpin[p])
                V(lambda e: e.tensor_tensor(Z["a"], Z["zt"], Z["xT"], OP.add))

            def s14():
                V(lambda e: e.tensor_tensor(Z["zt"], Z["a"], Z["xR"], OP.add))

            def s15():
                split(Z["zt"], col(zh, m), col(zl, m))
                pins[(m, "sp")] = k.sem["vec"].n

            def s16():
                k.wait("scalar", "vec", pins[(m, "sp")])
                C(lambda e: e.activation(Z["sq"], Z["zt"], AF.Square), inc=True)
                pins[(m, "c3")] = k.sem["sca"].n

            def s17():
                k.wait("vector", "sca", pins[(m, "c3")])
                k.wait("vector", "ten")
                split(Z["sq"], Z["bh"], Z["bl"])
                pins[(m, "sp2")] = k.sem["vec"].n

            def s18():
                k.wait("tensor", "vec", pins[(m, "sp2")])
                T(lambda e, m=m: e.matmul(pbG0[0:1, :], onescol[:], col(zh, m), start=(m == 0), stop=False))
                T(lambda e, m=m: e.matmul(pbG0[0:1, :], onescol[:], col(zl, m), start=False, stop=(m == NM - 1)))
                T(lambda e, m=m, Z=Z: e.matmul(pbG1[0:1, :], onescol[:], Z["bh"], start=(m == 0), stop=False))
                T(lambda e, m=m, Z=Z: e.matmul(pbG1[0:1, :], onescol[:], Z["bl"], start=False, stop=(m == NM - 1)), inc=True)

            return [s0, s1, s2, s3, s4, s5, s6, s7, s8, s9,
                    s10, s11, s12, s13, s14, s15, s16, s17, s18]

        load_del(0, 0)
        load_xtr(0, 0)
        for p in range(NM // 2):
            mA, mB = 2 * p, 2 * p + 1
            sa, sb = z_steps(mA), z_steps(mB)
            for i in range(len(sa)):
                sa[i]()
                sb[i]()
                if i == 1:
                    load_del(p + 1, pins[(mB, "v2")])
                elif i == 15:
                    load_xtr(p + 1, pins[(mB, "sp")])
        finalize_ln(pbG0, pbG1)
        k.wait("vector", "ten")
        for m in range(NM):
            V(lambda e, m=m: e.tensor_tensor(sbA[:], col(zh, m), col(zl, m), OP.add))
            k.spacer()
            V(lambda e: e.tensor_tensor(sbZt[:], sbA[:], pbB0[:, :], OP.subtract))
            k.spacer()
            V(lambda e: e.tensor_tensor(sbA[:], sbZt[:], pbB1[:, :], OP.mult))
            k.spacer()
            split(sbA[:, :], col(zh, m), col(zl, m))

    def finalize_ln(psA, psB):
        k.wait("vector", "ten")
        V(lambda e, psA=psA: e.tensor_scalar(sbRow[0:1, :], psA[0:1, :], 1.0 / 2048.0, None, OP.mult))
        V(lambda e, psB=psB: e.tensor_scalar(sbFm[0:1, :], psB[0:1, :], 1.0 / 2048.0, 1e-5, OP.mult, OP.add))
        V(lambda e: e.tensor_tensor(sbY[0:1, :], sbRow[0:1, :], sbRow[0:1, :], OP.mult))
        k.spacer()
        V(lambda e: e.tensor_tensor(sbSq[0:1, :], sbFm[0:1, :], sbY[0:1, :], OP.subtract), inc=True)
        k.wait("scalar", "vec")
        C(lambda e: e.activation(sbFm[0:1, :], sbSq[0:1, :], AF.Sqrt), inc=True)
        k.wait("vector", "sca")
        V(lambda e: e.reciprocal(sbY[0:1, :], sbFm[0:1, :]), inc=True)
        k.wait("tensor", "vec")
        T(lambda e: e.matmul(pbB0[:, :], onesrow[:], sbRow[0:1, :], start=True, stop=True))
        T(lambda e: e.matmul(pbB1[:, :], onesrow[:], sbY[0:1, :], start=True, stop=True), inc=True)

    def router(tile):
        k.wait("tensor", "vec")
        for kc in range(NM):
            aoff = 0
            k.wait("sync", "ten")
            k.dma("sync", _ap(aABuf, aoff, [[2 * AACW, 128], [1, AACW]]),
                  _ap(d_aAllh, kc * AACW, [[AANW, 128], [1, AACW]]), "ldw")
            k.dma("sync", _ap(aABuf, aoff + AACW, [[2 * AACW, 128], [1, AACW]]),
                  _ap(d_aAlll, kc * AACW, [[AANW, 128], [1, AACW]]), "ldw")
            k.wait("tensor", "ldw")
            for mb, pband, mw in ((0, pbA0, 128), (1, pbA1, 96)):
                Ah = _ap(aABuf, aoff + 128 * mb, [[2 * AACW, 128], [1, mw]])
                Al = _ap(aABuf, aoff + AACW + 128 * mb, [[2 * AACW, 128], [1, mw]])
                for pi, (lhs, rhs) in enumerate(
                        ((Ah, col(zh, kc)), (Al, col(zh, kc)), (Ah, col(zl, kc)))):
                    T(lambda e, pband=pband, lhs=lhs, rhs=rhs, mw=mw,
                        st=(kc == 0 and pi == 0), sp=(kc == NM - 1 and pi == 2):
                        e.matmul(pband[0:mw, :], lhs, rhs, start=st, stop=sp),
                        inc=(mb == 1 and pi == 2))
        k.wait("scalar", "ten")
        C(lambda e: e.activation(sbFm[:, :], pbA1[64:72, :], AF.Sigmoid), inc=True)
        k.wait("tensor", "sca")
        for b4 in range(4):
            k.wait("tensor", "vec")
            T(lambda e, b4=b4: e.transpose(
                _ap(pbB0, 0, [[512, 128], [1, 8]]), sbFm[:, 128 * b4:128 * b4 + 128], ident[0:8, 0:8]), inc=True)
            k.wait("vector", "ten")
            V(lambda e, b4=b4: e.tensor_copy(
                _ap(sbSc, 8 * b4, [[32, 128], [1, 8]]), _ap(pbB0, 0, [[512, 128], [1, 8]])), inc=True)
        def B4(i):
            return dict(
                sc=_ap(sbSc, 8 * i, [[32, 128], [1, 8]]),
                bi=_ap(sbSc2, 8 * i, [[32, 128], [1, 8]]),
                eq=_ap(sbSc3, 8 * i, [[32, 128], [1, 8]]),
                msk=_ap(sbT1, 8 * i, [[256, 128], [1, 8]]),
                m1=_ap(sbSc4, i, [[8, 128], [1, 1]]),
                m1b=_ap(sbSc4, i, [[8, 128], [0, 8]]),
                m2=_ap(sbT2, i, [[32, 128], [1, 1]]),
                m2b=_ap(sbT2, i, [[32, 128], [0, 8]]),
                den=_ap(sbMu, i, [[32, 128], [1, 1]]),
                den2=_ap(sbMu, 8 + i, [[32, 128], [1, 1]]),
                rec=_ap(sbMu, 16 + i, [[32, 128], [1, 1]]),
            )
        BB4 = [B4(i) for i in range(4)]
        for d in BB4:
            V(lambda e, d=d: e.tensor_tensor(d["bi"], d["sc"], ebT[:], OP.add))
        for d in BB4:
            V(lambda e, d=d: e.tensor_reduce(d["m1"], d["bi"], AX.X, OP.max))
        for d in BB4:
            V(lambda e, d=d: e.scalar_tensor_tensor(d["eq"], d["bi"], 1.0, d["m1b"], OP.mult, OP.is_equal))
        for d in BB4:
            V(lambda e, d=d: e.scalar_tensor_tensor(d["msk"], d["eq"], -2e30, d["bi"], OP.mult, OP.add))
        for d in BB4:
            V(lambda e, d=d: e.tensor_reduce(d["m2"], d["msk"], AX.X, OP.max))
        for d in BB4:
            V(lambda e, d=d: e.scalar_tensor_tensor(d["eq"], d["bi"], 1.0, d["m2b"], OP.mult, OP.is_ge))
        for d in BB4:
            V(lambda e, d=d: e.tensor_tensor(d["msk"], d["sc"], d["eq"], OP.mult))
        for d in BB4:
            V(lambda e, d=d: e.tensor_reduce(d["den"], d["msk"], AX.X, OP.add))
        for d in BB4:
            V(lambda e, d=d: e.tensor_scalar(d["den2"], d["den"], 1e-8, 8.0, OP.add, OP.mult))
        for d in BB4:
            V(lambda e, d=d: e.reciprocal(d["rec"], d["den2"]))
        for d in BB4:
            V(lambda e, d=d: e.tensor_scalar(d["eq"], d["msk"], d["rec"], None, OP.mult), inc=True)
        k.spacer()
        for b4 in range(4):
            src = _ap(sbSc3, 8 * b4, [[32, 128], [1, 8], [0, 8]])
            V(lambda e, src=src: e.tensor_copy(
                sbWt[:].rearrange("p (e r) -> p e r", r=8), src), inc=True)
            k.wait("tensor", "vec")
            T(lambda e, b4=b4: e.transpose(_ap(pbB0, 0, [[512, 64], [1, 128]]), sbWt[:, :], ident[:]), inc=True)
            k.wait("vector", "ten")
            V(lambda e, b4=b4: e.tensor_copy(wexp[:, 128 * b4:128 * b4 + 128],
                                             _ap(pbB0, 0, [[512, 64], [1, 128]])), inc=True)
        V(lambda e: e.tensor_tensor(sbZt[0:64, :], pbA0[0:64, :], wexp[:, :], OP.mult))
        V(lambda e: e.tensor_scalar(sbZt[64:72, :], pbA1[0:8, :], 0.125, None, OP.mult), inc=True)
        k.spacer()
        split(sbZt[0:72, :], ug_h[0:72, :], ug_l[0:72, :])
        V(lambda e: e.tensor_tensor(sbZt[0:64, :], pbA0[64:128, :], wexp[:, :], OP.mult))
        V(lambda e: e.tensor_scalar(sbZt[64:72, :], pbA1[32:40, :], 0.125, None, OP.mult), inc=True)
        k.spacer()
        split(sbZt[0:72, :], uu_h[0:72, :], uu_l[0:72, :])

    def phase_b(tile, il=None):
        GW = KC_GU * 128
        KA = 9
        GA = KA * 128
        tbq = {"g": [0, 0], "u": [0, 0]}
        pinb = {"vec": {}, "sca": {}}

        def adn_mms(kg):
            aoff = (kg % 2) * 144
            Ah = _ap(aDBuf, aoff, [[4 * 72, 128], [1, 72]])
            Al = _ap(aDBuf, aoff + 72, [[4 * 72, 128], [1, 72]])
            k.wait("tensor", "vec", pinb["vec"][kg])
            k.wait("tensor", "lda")
            T(lambda e, Ah=Ah, kg=kg, st=(kg == 0): e.matmul(
                pbB0[0:72, :], Ah, col(hh, kg), start=st, stop=False))
            T(lambda e, Al=Al, kg=kg: e.matmul(pbB0[0:72, :], Al, col(hh, kg), start=False, stop=False))
            T(lambda e, Ah=Ah, kg=kg, sp=(kg == NKG - 1): e.matmul(
                pbB0[0:72, :], Ah, col(hl, kg), start=False, stop=sp), inc=True)
            pa[kg] = k.sem["ten"].n

        pa = {}
        for kg in range(NKG):
            if il is not None:
                il.tick()
            if kg > 1:
                adn_mms(kg - 2)
            pg = [pbG0, pbG1][kg % 2]
            pu = [pbU0, pbU1][kg % 2]
            for gi, (gu, pband, semA, semB) in enumerate((("g", pg, "ldw", "ld"), ("u", pu, "ldu", "ld2"))):
                half = 0 if gu == "g" else 4352
                base = (kg * 128 * 2 + (0 if gu == "g" else 1)) * GW
                k.wait("sync", "ten", tbq[gu][0])
                k.dma("sync", _ap(wbuf, half, [[WBUF_W, 128], [1, GA]]),
                      _ap(d_wGUh, base, [[2 * GW, 128], [1, GA]]), semA)
                k.dma("sync", _ap(wbuf, half + GW, [[WBUF_W, 128], [1, GA]]),
                      _ap(d_wGUl, base, [[2 * GW, 128], [1, GA]]), semA)
                k.wait("sync", "ten", tbq[gu][1])
                k.dma("sync", _ap(wbuf, half + GA, [[WBUF_W, 128], [1, GW - GA]]),
                      _ap(d_wGUh, base + GA, [[2 * GW, 128], [1, GW - GA]]), semB)
                k.dma("sync", _ap(wbuf, half + GW + GA, [[WBUF_W, 128], [1, GW - GA]]),
                      _ap(d_wGUl, base + GA, [[2 * GW, 128], [1, GW - GA]]), semB)
                k.wait("tensor", semA)
                k.wait("tensor", "vec", pinb["vec"].get(kg - 2, 0))
                k.wait("tensor", "sca", pinb["sca"].get(kg - 2, 0))
                nmm = 0
                for kc in range(KC_GU):
                    if kc == KA:
                        k.wait("tensor", semB)
                    Wh = _ap(wbuf, half + kc * 128, [[WBUF_W, 128], [1, 128]])
                    Wl = _ap(wbuf, half + GW + kc * 128, [[WBUF_W, 128], [1, 128]])
                    if kc < NM:
                        rh, rl = col(zh, kc), col(zl, kc)
                    else:
                        rh, rl = (ug_h[:, :], ug_l[:, :]) if gu == "g" else (uu_h[:, :], uu_l[:, :])
                    for lhs, rhs in ((Wh, rh), (Wl, rh), (Wh, rl)):
                        sp = (nmm == 3 * KC_GU - 1)
                        spA = (nmm == 3 * KA - 1)
                        T(lambda e, pband=pband, lhs=lhs, rhs=rhs,
                            st=(nmm == 0), sp=sp:
                            e.matmul(pband[:, :], lhs, rhs, start=st, stop=sp),
                            inc=(sp or spA))
                        if spA:
                            tbq[gu][0] = k.sem["ten"].n
                        nmm += 1
                tbq[gu][1] = k.sem["ten"].n
            k.wait("scalar", "ten")
            k.wait("scalar", "vec", pinb["vec"].get(kg - 1, 0))
            C(lambda e, pg=pg: e.activation(sbSil[:], pg[:, :], AF.Silu), inc=True)
            pinb["sca"][kg] = k.sem["sca"].n
            k.wait("vector", "sca")
            k.wait("vector", "ten")
            V(lambda e, pu=pu: e.tensor_tensor(sbZt[:], sbSil[:], pu[:, :], OP.mult))
            k.spacer()
            split(sbZt[:, :], col(hh, kg), col(hl, kg))
            pinb["vec"][kg] = k.sem["vec"].n
            aoff = (kg % 2) * 144
            k.wait("sync", "lda")
            k.wait("sync", "ten", pa.get(kg - 2, 0))
            k.dma("sync", _ap(aDBuf, aoff, [[4 * 72, 128], [1, 72]]),
                  _ap(d_aDnh, kg * 72, [[ADNWP, 128], [1, 72]]), "lda")
            k.dma("sync", _ap(aDBuf, aoff + 72, [[4 * 72, 128], [1, 72]]),
                  _ap(d_aDnl, kg * 72, [[ADNWP, 128], [1, 72]]), "lda")
        adn_mms(NKG - 2)
        adn_mms(NKG - 1)
        k.wait("vector", "ten")
        V(lambda e: e.tensor_tensor(sbZt[0:64, :], pbB0[0:64, :], wexp[:, :], OP.mult))
        V(lambda e: e.tensor_scalar(sbZt[64:72, :], pbB0[64:72, :], 0.125, None, OP.mult), inc=True)
        k.spacer()
        split(sbZt[0:72, :], ud_h[0:72, :], ud_l[0:72, :])

    def phase_c(tile, tp, il=None):
        KQ = 11
        DQ = KQ * 128 * 2
        qsems = ("ldw", "ld", "ldu", "ld2")
        tcq = [0, 0, 0, 0]
        pinc = {}
        pstat = {}

        def stats_mms(m):
            k.wait("tensor", "vec", pstat[m])
            T(lambda e, m=m: e.matmul(pbU0[0:1, :], onescol[:], col(zh, m), start=(m == 0), stop=False))
            T(lambda e, m=m: e.matmul(pbU0[0:1, :], onescol[:], col(zl, m), start=False, stop=(m == NM - 1)))
            T(lambda e, m=m: e.matmul(pbU1[0:1, :], onescol[:], sbBh[:], start=(m == 0), stop=False))
            T(lambda e, m=m: e.matmul(pbU1[0:1, :], onescol[:], sbBl[:], start=False, stop=(m == NM - 1)), inc=True)

        for m in range(NM):
            if il is not None:
                il.tick()
            pband = [pbG0, pbG1][m % 2]
            for q in range(4):
                k.wait("sync", "ten", tcq[q])
                k.dma("sync", _ap(wbuf, q * DQ, [[WBUF_W, 128], [1, KQ * 128]]),
                      _ap(d_wDh, m * 128 * D_STRIDE + q * KQ * 128, [[D_STRIDE, 128], [1, KQ * 128]]), qsems[q])
                k.dma("sync", _ap(wbuf, q * DQ + KQ * 128, [[WBUF_W, 128], [1, KQ * 128]]),
                      _ap(d_wDl, m * 128 * D_STRIDE + q * KQ * 128, [[D_STRIDE, 128], [1, KQ * 128]]), qsems[q])
            k.wait("tensor", qsems[0])
            k.wait("tensor", "vec", pinc.get(m - 2, 0))
            nmm = 0
            for kk in range(KC_D):
                q = kk // KQ
                if kk % KQ == 0 and q > 0:
                    k.wait("tensor", qsems[q])
                Wh = _ap(wbuf, q * DQ + (kk - q * KQ) * 128, [[WBUF_W, 128], [1, 128]])
                Wl = _ap(wbuf, q * DQ + KQ * 128 + (kk - q * KQ) * 128, [[WBUF_W, 128], [1, 128]])
                if kk < NKG:
                    rh, rl = col(hh, kk), col(hl, kk)
                else:
                    rh, rl = ud_h[:, :], ud_l[:, :]
                for lhs, rhs in ((Wh, rh), (Wl, rh), (Wh, rl)):
                    sp = (nmm == 3 * KC_D - 1)
                    spQ = (nmm == 3 * KQ * (q + 1) - 1)
                    T(lambda e, pband=pband, lhs=lhs, rhs=rhs,
                        st=(nmm == 0), sp=sp:
                        e.matmul(pband[:, :], lhs, rhs, start=st, stop=sp),
                        inc=(sp or spQ))
                    if spQ:
                        tcq[q] = k.sem["ten"].n
                    nmm += 1
            if m > 0:
                stats_mms(m - 1)
            k.wait("vector", "ten")
            V(lambda e, pband=pband, m=m: e.tensor_tensor(sbZt[:], pband[:, :], col(zh, m), OP.add))
            k.spacer()
            V(lambda e, m=m: e.tensor_tensor(sbA[:], sbZt[:], col(zl, m), OP.add), inc=True)
            pinc[m] = k.sem["vec"].n
            k.spacer()
            split(sbA[:, :], col(zh, m), col(zl, m))
            k.wait("scalar", "vec")
            C(lambda e: e.activation(sbSq[:], sbA[:], AF.Square), inc=True)
            k.wait("vector", "sca")
            k.wait("vector", "ten")
            split(sbSq[:, :], sbBh[:, :], sbBl[:, :])
            pstat[m] = k.sem["vec"].n
        stats_mms(NM - 1)
        finalize_ln(pbU0, pbU1)
        k.wait("vector", "ten")
        for m in range(NM):
            k.wait("vector", "dsca")
            V(lambda e, m=m: e.tensor_tensor(sbA[:], col(zh, m), col(zl, m), OP.add))
            k.spacer()
            V(lambda e: e.tensor_tensor(sbZt[:], sbA[:], pbB0[:, :], OP.subtract))
            k.spacer()
            V(lambda e: e.tensor_tensor(sbY[:], sbZt[:], pbB1[:, :], OP.mult), inc=True)
            k.wait("scalar", "vec")
            k.dma("scalar", d_out[tile, m], sbY[:, :], "dsca")

    dv_iltiles = dev.get("iltiles", NTILE)
    dv_il = dev.get("il", (dv_tiles == NTILE or "iltiles" in dev)
                    and dv_phases == ("attn", "z", "router", "b", "c")
                    and dv_cht == CHT and not dv_expose)

    def sub_thunks(tile):
        th = []
        for chi in range(CHT):
            th.append(lambda chi=chi, tile=tile: attn_s1(tile, chi))
            th.append(lambda chi=chi, tile=tile: attn_s2(tile, chi, tile % 2))
            th.append(lambda chi=chi, tile=tile: attn_s3(tile, chi, tile % 2))
        return th

    il0 = IL(sub_thunks(0), [10 * (i + 1) for i in range(3 * CHT)]) if dv_il else None
    prepass(NKG * 128 * GU_STRIDE, d_wGU, d_wGUh, d_wGUl, il=il0)
    prepass(NM * 128 * D_STRIDE, d_wD, d_wDh, d_wDl, il=il0)
    prepass(128 * AANW, d_aAll, d_aAllh, d_aAlll, il=il0)
    prepass(128 * ADNWP, d_aDn, d_aDnh, d_aDnl, il=il0)
    if il0 is not None:
        il0.flush()
        attn_flush_pend()
    k.barrier()

    if dv_il:
        for tile in range(dv_iltiles):
            tp = tile % 2
            z_phase(tile, tp)
            k.barrier()
            router(tile)
            k.barrier()
            if tile + 1 < dv_iltiles:
                ilbc = IL(sub_thunks(tile + 1), [1 + (i * 57) // 47 for i in range(48)])
            else:
                ilbc = None
            phase_b(tile, il=ilbc)
            k.barrier()
            phase_c(tile, tp, il=ilbc)
            if ilbc is not None:
                ilbc.flush()
                attn_flush_pend()
            k.barrier()

    for tile in (() if dv_il else range(dv_tiles)):
        tp = tile % 2
        if "attn" in dv_phases:
            for chi in range(dv_cht):
                attn_chunk(tile, chi, tp)
            attn_flush_pend()
            k.barrier()
            if d_dbgw is not None:
                for di, src in ((0, sbQ1[:, :]), (1, sbQ2[:, :])):
                    k.dma("scalar", _ap(d_dbgw, di * 128 * 768, [[768, 128], [1, 768]]), src, "dsca")
                for di, src, wdt in ((2, sbZr[:, :], 32), (3, sbPr[:, :], 256),
                                     (4, sbAV[:, :], 256), (5, sbL[:, :], 256)):
                    k.dma("scalar", _ap(d_dbgw, di * 128 * 768, [[768, 128], [1, wdt]]), src, "dsca")
                k.barrier()
        if "z" in dv_phases:
            z_phase(tile, tp)
            k.barrier()
            if d_dbgz is not None:
                for m in range(NM):
                    V(lambda e, m=m: e.tensor_tensor(sbY[:], col(zh, m), col(zl, m), OP.add), inc=True)
                    k.wait("scalar", "vec")
                    k.dma("scalar", _ap(d_dbgz, (tp * NM + m) * 128 * TCOL, [[512, 128], [1, 512]]), sbY[:, :], "dsca")
                    k.wait("vector", "dsca")
                k.barrier()
        if "router" in dv_phases:
            router(tile)
            k.barrier()
        if "b" in dv_phases:
            phase_b(tile)
            k.barrier()
            if d_dbgh is not None:
                for kg in range(4):
                    V(lambda e, kg=kg: e.tensor_tensor(sbY[:], col(hh, kg), col(hl, kg), OP.add), inc=True)
                    k.wait("scalar", "vec")
                    k.dma("scalar", _ap(d_dbgh, kg * 128 * TCOL, [[512, 128], [1, 512]]), sbY[:, :], "dsca")
                    k.wait("vector", "dsca")
                k.barrier()
        if "c" in dv_phases:
            phase_c(tile, tp)
            k.barrier()

    k.wait("sync", "dsca")
    k.wait("sync", "dsync")

    @blk.sync
    def _(eng):
        for fn in prog.ops["sync"]:
            fn(eng)

    @blk.tensor
    def _(eng):
        for fn in prog.ops["tensor"]:
            fn(eng)

    @blk.vector
    def _(eng):
        for fn in prog.ops["vector"]:
            fn(eng)

    @blk.scalar
    def _(eng):
        for fn in prog.ops["scalar"]:
            fn(eng)

    @blk.gpsimd
    def _(eng):
        for fn in prog.ops["gpsimd"]:
            fn(eng)

    ctx.close()
    return nc


def kernel(**inputs):
    x = np.asarray(inputs["L0_attn_x"], np.float32)
    wmaps = _stage_weights(inputs)
    in_maps = []
    for core in range(NCORES):
        m = dict(wmaps)
        m.update(_stage_x(x, core))
        in_maps.append(m)
    nc = build_nc()
    res = run_bass_kernel_spmd(nc, in_maps, core_ids=list(range(NCORES)))
    outs = [r["out"] for r in res.results]
    return _gather_out(outs)
